# version 23
# baseline (speedup 1.0000x reference)
import contextlib
import numpy as np
import ml_dtypes
import concourse.bass as bass
import concourse.mybir as mybir
from concourse.bass import IndirectOffsetOnAxis
from concourse.bass_utils import run_bass_kernel_spmd
from concourse.alu_op_type import AluOpType as OP

F32 = mybir.dt.float32
BF16 = mybir.dt.bfloat16
I32 = mybir.dt.int32
AF = mybir.ActivationFunctionType
AXX = mybir.AxisListType.X

S = 4096
D = 1024
NT = S // 128
NG = S // 512
ENGINES = ("sync", "gpsimd", "vector", "scalar", "tensor")


class _Sem:
    def __init__(self, handle):
        self.h = handle
        self.n = 0


class Tok:
    __slots__ = ("sem", "val")

    def __init__(self, sem, val):
        self.sem = sem
        self.val = val


class Ctx:

    def __init__(self, nc, es):
        self.nc = nc
        self.es = es
        self.streams = {e: [] for e in ENGINES}
        self.sems = {e: _Sem(es.enter_context(nc.semaphore(f"sem_{e}")))
                     for e in ENGINES}
        for extra in ("st", "acc", "gpmem", "w2s", "w1s", "trq", "tr2"):
            self.sems[extra] = _Sem(es.enter_context(nc.semaphore("sem_" + extra)))
        self._dma_sems = {id(self.sems[k]) for k in
                          ("sync", "gpsimd", "st", "acc", "w2s", "w1s",
                           "trq", "tr2")}
        self.psum = [es.enter_context(nc.psum_tensor(f"pb{i}", [128, 512], F32))
                     for i in range(8)]
        self.ptok = [None] * 8
        self._pnext = 0
        self._names = set()

    def sbuf(self, name, shape, dtype):
        assert name not in self._names, name
        self._names.add(name)
        return self.es.enter_context(self.nc.sbuf_tensor(name, shape, dtype))

    def op(self, eng, fn, waits=(), amt=1, sem_name=None):
        if sem_name is None and eng == "gpsimd" and amt == 1:
            sem_name = "gpmem"
        sem = self.sems[sem_name or eng]
        frozen = []
        best = {}
        for wt in waits:
            if wt is None:
                continue
            k = id(wt.sem)
            v = wt.sem.n if k in self._dma_sems else wt.val
            if best.get(k, -1) < v:
                best[k] = v
        for k, v in best.items():
            pass
        seen = set()
        for wt in waits:
            if wt is None:
                continue
            k = id(wt.sem)
            if k in seen:
                continue
            seen.add(k)
            frozen.append((wt.sem.h, best[k]))
        sem.n += amt
        tok = Tok(sem, sem.n)
        semh = sem.h

        def run(e, frozen=tuple(frozen), fn=fn, semh=semh, amt=amt):
            for h, v in frozen:
                e.wait_ge(h, v)
            ins = fn(e)
            ins.then_inc(semh, amt)

        self.streams[eng].append(run)
        return tok

    def dma(self, q, out, in_, waits=(), sem_name=None):
        return self.op(q, lambda e: e.dma_start(out=out, in_=in_), waits,
                       amt=16, sem_name=sem_name)

    def dma_st(self, q, out, in_, waits=()):
        return self.dma(q, out, in_, waits, sem_name="st")

    def dma_tr(self, q, out, in_, waits=()):
        return self.op("scalar", lambda e: e.dma_start_transpose(out, in_),
                       waits, amt=16, sem_name="trq")

    def bank(self):
        i = self._pnext
        self._pnext = (self._pnext + 1) % 8
        return self.psum[i], i

    def mm_chain(self, pb, pairs, waits=(), tile_position=None):
        def fn(e, pb=pb, pairs=tuple(pairs), tp=tile_position):
            ins = None
            n = len(pairs)
            for i, (l, r) in enumerate(pairs):
                kw = {}
                if tp is not None:
                    kw["tile_position"] = tp
                ins = e.matmul(pb, l, r, start=(i == 0), stop=(i == n - 1), **kw)
            return ins
        return self.op("tensor", fn, waits)

    def finalize(self, block):
        for e in ENGINES:
            fns = self.streams[e]
            if not fns:
                continue

            def body(eng, fns=fns):
                for f in fns:
                    f(eng)
            getattr(block, e)(body)


def host_consts():
    ii = np.arange(128)
    c = {}
    c["ident128"] = np.eye(128, dtype=np.float32)
    c["stril"] = (ii[:, None] > ii[None, :]).astype(np.float32)
    c["triu0"] = (ii[:, None] <= ii[None, :]).astype(np.float32)
    c["sutril"] = (ii[:, None] < ii[None, :]).astype(np.float32)
    c["tril0"] = (ii[:, None] >= ii[None, :]).astype(np.float32)
    dstr = ii[:, None] - 1 - ii[None, :]
    c["Dstrict"] = np.where(dstr >= 0, dstr, 0).astype(np.float32)
    c["DstrictT"] = c["Dstrict"].T.copy()
    dinc = ii[:, None] - ii[None, :]
    c["Dincl"] = np.where(dinc >= 0, dinc, 0).astype(np.float32)
    c["DinclT"] = c["Dincl"].T.copy()
    c["iota"] = ii.astype(np.float32).reshape(128, 1)
    c["e127"] = np.eye(128, dtype=np.float32)[:, 127:128].copy()
    return c


def decl_consts(nc):
    return {k: nc.dram_tensor("c_" + k, list(v.shape), F32,
                              kind="ExternalInput").ap()
            for k, v in host_consts().items()}


class Common:
    def __init__(self, ctx, caps):
        self.ctx = ctx
        self.caps = caps
        self.ones_row = ctx.sbuf("ones_row", [1, 512], F32)
        self.ones_col = ctx.sbuf("ones_col", [128, 1], F32)
        self.eps_ln = ctx.sbuf("eps_ln", [128, 1], F32)
        t0 = ctx.op("gpsimd", lambda e: e.memset(self.ones_row[:], 1.0))
        t1 = ctx.op("gpsimd", lambda e: e.memset(self.ones_col[:], 1.0),
                    waits=[t0])
        self.init_tok = ctx.op("gpsimd", lambda e: e.memset(self.eps_ln[:], 1e-5),
                               waits=[t1])
        self._cbuf = {}

    def ident16(self):
        if not hasattr(self, "_id16"):
            ctx = self.ctx
            buf, tok = self.const("ident128")
            b16 = ctx.sbuf("cb_ident16", [128, 128], BF16)
            t = ctx.op("vector", lambda e: e.tensor_copy(b16[:], buf[:]),
                       waits=[tok])
            self._id16 = (b16, t)
        return self._id16

    def const(self, name):
        if name not in self._cbuf:
            ctx = self.ctx
            v = host_consts()[name]
            buf = ctx.sbuf("cb_" + name, list(v.shape), F32)
            tok = ctx.dma("sync", buf[:], self.caps[name][:])
            self._cbuf[name] = (buf, tok)
        return self._cbuf[name]


def emit_ln(ctx, com, x_ap, xnT4_ap, tag, mem=None, tok_ap=None):
    if mem is None:
        mem = ctx.sbuf
    id16, id16_tok = com.ident16()
    if not hasattr(com, "ln_bufs"):
        xt = mem("ln_x", [128, 2 * 1024], F32)
        sq = mem("ln_sq", [128, 2 * 1024], F32)
        sm = mem("ln_sm", [128, 2 * 8], F32)
        xnb = mem("ln_out", [128, 2 * 1024], BF16)
        xts = mem("ln_xts", [128, 2 * 2048], BF16)
        com.ln_bufs = (xt, sq, sm, xnb, xts)
    xt, sq, sm, xnb, xts = com.ln_bufs
    fin = [None, None]
    tfin = [None, None]
    prev_cps = [None]
    out_toks = []
    for g in range(NT):
        p = g % 2
        xv = xt[:, p * 1024:(p + 1) * 1024]
        sqv = sq[:, p * 1024:(p + 1) * 1024]
        ov = xnb[:, p * 1024:(p + 1) * 1024]
        sslot = (g // 2) % 2
        tv = xts[:, sslot * 2048:(sslot + 1) * 2048]
        smv = sm[:, p * 8:(p + 1) * 8]
        ld = ctx.dma("sync", xv, x_ap[g * 128:(g + 1) * 128, :],
                     waits=list(fin[p] or []))
        t1 = ctx.op("vector", lambda e, xv=xv, smv=smv:
                    e.reduce_sum(smv[:, 0:1], xv, axis=AXX), waits=[ld])
        t2 = ctx.op("vector", lambda e, smv=smv: e.tensor_scalar(
            smv[:, 1:2], smv[:, 0:1], -1.0 / 1024.0, None, OP.mult), waits=[t1])
        t3 = ctx.op("scalar", lambda e, xv=xv, sqv=sqv:
                    e.activation(sqv, xv, AF.Square), waits=[ld])
        t4 = ctx.op("vector", lambda e, sqv=sqv, smv=smv:
                    e.reduce_sum(smv[:, 2:3], sqv, axis=AXX), waits=[t3])
        t5 = ctx.op("vector", lambda e, smv=smv: e.tensor_mul(
            smv[:, 3:4], smv[:, 1:2], smv[:, 1:2]), waits=[t2])
        t6 = ctx.op("vector", lambda e, smv=smv: e.scalar_tensor_tensor(
            smv[:, 4:5], smv[:, 2:3], 1.0 / 1024.0, smv[:, 3:4],
            OP.mult, OP.subtract), waits=[t4, t5])
        t7 = ctx.op("scalar", lambda e, smv=smv: e.activation(
            smv[:, 5:6], smv[:, 4:5], AF.Sqrt, bias=com.eps_ln[:]),
            waits=[t6, com.init_tok])
        t8 = ctx.op("vector", lambda e, smv=smv:
                    e.reciprocal(smv[:, 6:7], smv[:, 5:6]), waits=[t7])
        t9 = ctx.op("vector", lambda e, xv=xv, ov=ov, smv=smv: e.tensor_scalar(
            ov, xv, smv[:, 1:2], smv[:, 6:7], OP.add, OP.mult),
            waits=[t8] + list(fin[p] or []))
        cps = []
        for k in range(8):
            pb, pi = ctx.bank()
            ptr = pb[:, 0:64].bitcast(BF16)
            tr = ctx.op("tensor", lambda e, ptr=ptr, ov=ov, k=k: e.transpose(
                ptr, ov[:, k * 128:(k + 1) * 128], id16[:]),
                waits=[t9, id16_tok, ctx.ptok[pi]])
            cp = ctx.op("vector", lambda e, ptr=ptr, tv=tv, k=k, g=g:
                        e.tensor_copy(
                tv[:, k * 256 + (g % 2) * 128: k * 256 + (g % 2) * 128 + 128],
                ptr),
                waits=[tr, tfin[sslot]])
            ctx.ptok[pi] = cp
            cps.append(cp)
        rd = [cps[-1]]
        if tok_ap is not None:
            rd.append(ctx.dma_st("sync", tok_ap[g * 128:(g + 1) * 128, :], ov,
                                 waits=[t9]))
        if g % 2 == 1:
            stt = ctx.dma_st("sync", xnT4_ap[g // 2, :, :], tv,
                             waits=cps + [prev_cps[0]])
            tfin[sslot] = stt
            out_toks.append(stt)
        prev_cps[0] = cps[-1]
        fin[p] = rd
    return out_toks[-1]


def emit_lnf_tile(ctx, com, key, carve_mem, xrv, tile, xnT4_ap, waits):
    id16, id16_tok = com.ident16()
    st_ = getattr(com, "lnf_" + key, None)
    if st_ is None:
        st_ = {
            "sm": carve_mem(f"lnf_{key}_sm", [128, 16], F32),
            "xnb": carve_mem(f"lnf_{key}_xnb", [128, 2 * 1024], BF16),
            "xts": carve_mem(f"lnf_{key}_xts", [128, 2 * 2048], BF16),
            "fin": [None, None],
            "tfin": [None, None],
            "pcps": [None],
        }
        setattr(com, "lnf_" + key, st_)
    p = tile % 2
    sslot = (tile // 2) % 2
    sm = st_["sm"][:, p * 8:(p + 1) * 8]
    ov = st_["xnb"][:, p * 1024:(p + 1) * 1024]
    tv = st_["xts"][:, sslot * 2048:(sslot + 1) * 2048]
    w0 = list(waits)
    t1 = ctx.op("vector", lambda e: e.reduce_sum(sm[:, 0:1], xrv, axis=AXX),
                waits=w0)
    t2 = ctx.op("vector", lambda e: e.tensor_scalar(
        sm[:, 1:2], sm[:, 0:1], -1.0 / 1024.0, None, OP.mult), waits=[t1])
    pb, pi = ctx.bank()
    t3a = ctx.op("scalar", lambda e, pb=pb: e.activation(
        pb[:], xrv[:, 0:512], AF.Square), waits=w0 + [ctx.ptok[pi]])
    pb2, pi2 = ctx.bank()
    t3b = ctx.op("scalar", lambda e, pb2=pb2: e.activation(
        pb2[:], xrv[:, 512:1024], AF.Square), waits=w0 + [ctx.ptok[pi2]])
    t4a = ctx.op("vector", lambda e, pb=pb: e.reduce_sum(
        sm[:, 2:3], pb[:], axis=AXX), waits=[t3a])
    ctx.ptok[pi] = t4a
    t4b = ctx.op("vector", lambda e, pb2=pb2: e.reduce_sum(
        sm[:, 3:4], pb2[:], axis=AXX), waits=[t3b])
    ctx.ptok[pi2] = t4b
    t4c = ctx.op("vector", lambda e: e.tensor_add(
        sm[:, 2:3], sm[:, 2:3], sm[:, 3:4]), waits=[t4a, t4b])
    t5 = ctx.op("vector", lambda e: e.tensor_mul(
        sm[:, 4:5], sm[:, 1:2], sm[:, 1:2]), waits=[t2])
    t6 = ctx.op("vector", lambda e: e.scalar_tensor_tensor(
        sm[:, 5:6], sm[:, 2:3], 1.0 / 1024.0, sm[:, 4:5],
        OP.mult, OP.subtract), waits=[t4c, t5])
    t7 = ctx.op("scalar", lambda e: e.activation(
        sm[:, 6:7], sm[:, 5:6], AF.Sqrt, bias=com.eps_ln[:]),
        waits=[t6, com.init_tok])
    t8 = ctx.op("vector", lambda e: e.reciprocal(sm[:, 7:8], sm[:, 6:7]),
                waits=[t7])
    t9 = ctx.op("vector", lambda e: e.tensor_scalar(
        ov, xrv, sm[:, 1:2], sm[:, 7:8], OP.add, OP.mult),
        waits=[t8] + ([st_["fin"][p]] if st_["fin"][p] else []))
    cps = []
    for k in range(8):
        pb3, pi3 = ctx.bank()
        ptr = pb3[:, 0:64].bitcast(BF16)
        tr = ctx.op("tensor", lambda e, ptr=ptr, k=k: e.transpose(
            ptr, ov[:, k * 128:(k + 1) * 128], id16[:]),
            waits=[t9, id16_tok, ctx.ptok[pi3]])
        cp = ctx.op("vector", lambda e, ptr=ptr, k=k: e.tensor_copy(
            tv[:, k * 256 + (tile % 2) * 128: k * 256 + (tile % 2) * 128 + 128],
            ptr),
            waits=[tr, st_["tfin"][sslot]])
        ctx.ptok[pi3] = cp
        cps.append(cp)
    if tile % 2 == 1:
        stt = ctx.dma_st("sync", xnT4_ap[tile // 2, :, :], tv,
                         waits=cps + [st_["pcps"][0]])
        st_["tfin"][sslot] = stt
    st_["pcps"][0] = cps[-1]
    st_["fin"][p] = cps[-1]
    return cps[-1]


def load_xnT(ctx, xnT, xnT4_ap, g, waits=(), tg=512):
    if tg == 256:
        return [ctx.dma("sync", xnT[:], xnT4_ap[g, :, :], waits=list(waits))]
    toks = []
    for h in range(2):
        toks.append(ctx.dma("sync", xnT[:, h * 2048:(h + 1) * 2048],
                            xnT4_ap[g * 2 + h, :, :], waits=list(waits)))
    return toks


def emit_l0_attn(ctx, com, x_ap, xn16_ap, w, sc, xout_ap, mem=None,
                 lnf=None):
    if mem is None:
        mem = ctx.sbuf
    nc = ctx.nc
    q4, v4d, of4 = sc["q4"], sc["v4d"], sc["of4"]
    SCALE = 64.0 ** -0.5

    wbig = mem("a_wbig", [128, 8 * 4096], BF16)
    xnT = mem("a_xnT", [128, 8 * 512], BF16)
    ksum = mem("a_ksum", [128, NT * 64], F32)
    ev = mem("a_ev", [128, 4 * 512], F32)
    ss = mem("a_ss", [128, 4 * 16], F32)
    ob = mem("a_ob", [128, 4 * 512], BF16)
    eps_qn = mem("a_epsqn", [128, 1], F32)
    ctx.op("gpsimd", lambda e: e.memset(eps_qn[:], 1e-6))

    wtok = [None] * 8
    wcons = [None]

    def wload(name):
        for k in range(8):
            wtok[k] = ctx.dma("gpsimd", wbig[:, k * 4096:(k + 1) * 4096],
                              w[name][k * 128:(k + 1) * 128, :],
                              waits=[wcons[0]])

    xslot_toks = [None, None]
    xslot_cons = [None, None]
    xiter = [0]

    def xview(slot):
        return xnT[:]

    def xprefetch(t):
        if t >= 3 * NG:
            return
        xslot_toks[t % 2] = load_xnT(ctx, xnT[:], xn16_ap, t % NG,
                                     waits=[xslot_cons[0], xslot_cons[1]])
    xprefetch(0)

    def proj_pass(evac_cb):
        for g in range(NG):
            t_ = xiter[0]
            slot = t_ % 2
            xtoks = xslot_toks[slot]
            xv = xview(slot)
            for ti in range(4):
                tile = g * 4 + ti
                xof = (ti // 2) * 2048 + (ti % 2) * 128
                for fs in range(8):
                    pb, pi = ctx.bank()
                    pairs = [(xv[:, xof + k * 256: xof + k * 256 + 128],
                              wbig[:, k * 4096 + fs * 512: k * 4096 + (fs + 1) * 512])
                             for k in range(8)]
                    mt = ctx.mm_chain(pb[:], pairs,
                                      waits=xtoks + wtok + [ctx.ptok[pi]])
                    evac_cb(tile, fs, pb, pi, mt)
            xslot_cons[slot] = Tok(ctx.sems["tensor"], ctx.sems["tensor"].n)
            xiter[0] = t_ + 1
            xprefetch(t_ + 1)

    wload("Wk")
    kz = ctx.op("gpsimd", lambda e: e.memset(ksum[:], 0.0))
    ktmp = mem("a_ktmp", [128, 64], F32)
    ev_cons = {}

    def evac_k(tile, fs, pb, pi, mt):
        b = (tile * 8 + fs) % 4
        evv = ev[:, b * 512:(b + 1) * 512]
        r1 = ctx.op("scalar", lambda e, evv=evv, pb=pb:
                    e.activation(evv, pb[:], AF.Relu),
                    waits=[mt, ev_cons.get(b)])
        ctx.ptok[pi] = r1
        kv = ksum[:, tile * 64:(tile + 1) * 64]
        r2 = ctx.op("vector", lambda e, evv=evv: e.reduce_sum(
            ktmp[:], evv.rearrange("p (j d) -> p d j", j=8), axis=AXX),
            waits=[r1])
        r3 = ctx.op("vector", lambda e, kv=kv:
                    e.tensor_add(kv, kv, ktmp[:]), waits=[r2, kz])
        ev_cons[b] = r2

    proj_pass(evac_k)
    wcons[0] = Tok(ctx.sems["tensor"], ctx.sems["tensor"].n)

    wload("Wv")

    ob_cons = {}
    vstg = mem("a_vstg", [128, 2 * 1024], BF16)
    vstg_cons = {}

    def evac_v(tile, fs, pb, pi, mt):
        b = (tile * 8 + fs) % 4
        b2 = (tile * 8 + fs) % 2
        obv = ob[:, b * 512:(b + 1) * 512]
        c1 = ctx.op("vector", lambda e, obv=obv, pb=pb:
                    e.tensor_copy(obv, pb[:]), waits=[mt, ob_cons.get(b)])
        ctx.ptok[pi] = c1
        sv = vstg[:, b2 * 1024:(b2 + 1) * 1024]             .rearrange("p (j c d) -> p j c d", j=8, c=2)
        g1 = ctx.op("scalar", lambda e, sv=sv, obv=obv: e.copy(
            sv[:, :, 0:1, :].squeeze(2), obv.rearrange("p (j d) -> p j d", j=8)),
            waits=[c1, vstg_cons.get(b2)])
        g2 = ctx.op("scalar", lambda e, sv=sv, obv=obv: e.copy(
            sv[:, :, 1:2, :].squeeze(2), obv.rearrange("p (j d) -> p j d", j=8)),
            waits=[c1])
        ob_cons[b] = g2
        t0 = tile * 128
        d1 = ctx.dma_st("sync", v4d[t0:t0 + 128, fs * 1024:(fs + 1) * 1024],
                        vstg[:, b2 * 1024:(b2 + 1) * 1024], waits=[g2])
        vstg_cons[b2] = d1

    proj_pass(evac_v)
    wcons[0] = Tok(ctx.sems["tensor"], ctx.sems["tensor"].n)

    wload("Wq")

    evq_cons = {}
    obq_cons = {}

    def evac_q(tile, fs, pb, pi, mt):
        b = (tile * 8 + fs) % 4
        bo_ = b
        evv = ev[:, b * 512:(b + 1) * 512]
        obv = ob[:, bo_ * 512:(bo_ + 1) * 512]
        ssv = ss[:, b * 16:(b + 1) * 16]
        r1 = ctx.op("scalar", lambda e, evv=evv, pb=pb: e.activation(
            evv, pb[:], AF.Relu, scale=SCALE), waits=[mt, evq_cons.get(b)])
        ctx.ptok[pi] = r1
        kv = ksum[:, tile * 64:(tile + 1) * 64]
        r2 = ctx.op("vector", lambda e, evv=evv, kv=kv: e.tensor_tensor(
            evv.rearrange("p (i d) -> p i d", i=8),
            evv.rearrange("p (i d) -> p i d", i=8),
            kv[:, None, :].broadcast_to([128, 8, 64]), OP.mult), waits=[r1])
        r3 = ctx.op("vector", lambda e, evv=evv, ssv=ssv: e.reduce_sum(
            ssv[:, 0:8], evv.rearrange("p (i d) -> p i d", i=8), axis=AXX),
            waits=[r2])
        r4 = ctx.op("vector", lambda e, ssv=ssv: e.tensor_scalar(
            ssv[:, 8:16], ssv[:, 0:8], eps_qn[:], None, OP.add), waits=[r3])
        r5 = ctx.op("vector", lambda e, ssv=ssv:
                    e.reciprocal(ssv[:, 0:8], ssv[:, 8:16]), waits=[r4])
        r6 = ctx.op("vector", lambda e, evv=evv, obv=obv, ssv=ssv: e.tensor_tensor(
            obv.rearrange("p (i d) -> p i d", i=8),
            evv.rearrange("p (i d) -> p i d", i=8),
            ssv[:, 0:8, None].broadcast_to([128, 8, 64]), OP.mult),
            waits=[r5, obq_cons.get(bo_)])
        t0 = tile * 128
        dq = ctx.dma_st("sync", q4[t0:t0 + 128, fs * 512:(fs + 1) * 512], obv,
                        waits=[r6])
        evq_cons[b] = r6
        obq_cons[bo_] = dq

    proj_pass(evac_q)
    wcons[0] = Tok(ctx.sems["tensor"], ctx.sems["tensor"].n)

    pbsnap = [Tok(ctx.sems["st"], ctx.sems["st"].n)]
    qh = mem("a_qh", [128, 2 * 4096], BF16)
    vd = mem("a_vd", [128, 2 * 8192], BF16)
    qh_cons = [None, None]
    vd_cons = [None, None]
    ob_cons2 = [None] * 4
    q4v = q4.rearrange("t (i2 c) -> (t i2) c", c=128)
    v4v = v4d.rearrange("t (j c) -> (t j) c", c=128)
    for tile in range(NT):
        p = tile % 2
        qhv = qh[:, p * 4096:(p + 1) * 4096]
        vdv = vd[:, p * 8192:(p + 1) * 8192]
        t0 = tile * 128
        extra = pbsnap if tile < 2 else []
        qt = ctx.dma_tr("sync", qhv, q4v[t0 * 32:(t0 + 128) * 32, :],
                        waits=[qh_cons[p]] + extra)
        vt = ctx.dma_tr("sync", vdv, v4v[t0 * 64:(t0 + 128) * 64, :],
                        waits=[vd_cons[p]] + extra)
        for fill in range(8):
            pb, pi = ctx.bank()

            def fills(e, pb=pb, qhv=qhv, vdv=vdv, fill=fill):
                ins = None
                for tp in range(2):
                    for par in range(2):
                        rb = par * 64
                        ob_ = tp * 64 + par * 32
                        for slot in range(8):
                            tl = fill * 16 + slot * 2 + tp
                            lhsT = qhv[rb:rb + 64, tl * 32:(tl + 1) * 32]
                            rhs = vdv[rb:rb + 64, tl * 64:(tl + 1) * 64]
                            ins = e.matmul(
                                pb[ob_:ob_ + 32, slot * 64:(slot + 1) * 64],
                                lhsT, rhs, start=(slot == 0), stop=(slot == 7),
                                tile_position=(rb, ob_),
                                skip_group_check=True)
                return ins

            mt = ctx.op("tensor", fills, waits=[qt, vt, ctx.ptok[pi]])
            b = fill % 4
            obv = ob[:, b * 512:(b + 1) * 512]
            c1 = ctx.op("vector", lambda e, obv=obv, pb=pb:
                        e.tensor_copy(obv, pb[:]),
                        waits=[mt, ob_cons2[b]] + (pbsnap if tile == 0 else []))
            ctx.ptok[pi] = c1
            base2 = tile * 64 + fill * 8
            of2 = of4.rearrange("(t2 two) f -> t2 two f", two=2)
            for tp in range(2):
                dst = of2[base2:base2 + 8, tp:tp + 1, :].squeeze(1) \
                    .rearrange("t (f j) -> f t j", j=64)
                d = ctx.dma_st("sync", dst,
                               obv[tp * 64:(tp + 1) * 64, :]
                               .rearrange("f (s j) -> f s j", j=64),
                               waits=[c1])
                ob_cons2[b] = d
        qh_cons[p] = Tok(ctx.sems["tensor"], ctx.sems["tensor"].n)
        vd_cons[p] = qh_cons[p]

    wop_tok = []
    for cch in range(32):
        wop_tok.append(ctx.dma("gpsimd", wbig[:, cch * 1024:(cch + 1) * 1024],
                               w["WoP"][cch * 128:(cch + 1) * 128, :],
                               waits=[wcons[0]]))
    bo_sb = mem("a_bo", [1, 1024], F32)
    bo_tok = ctx.dma("sync", bo_sb[:], w["bo"][:])
    ofT = mem("a_ofT", [128, 2 * 32 * 512], BF16)
    xres = mem("a_xres", [128, 2 * 1024], F32)
    passB_end = Tok(ctx.sems["tensor"], ctx.sems["tensor"].n)
    passA_vec = Tok(ctx.sems["vector"], ctx.sems["vector"].n)
    oslot_toks = [None, None]
    oslot_cons = [passB_end, passB_end]

    def oview(slot):
        return ofT[:, slot * 16384:(slot + 1) * 16384]

    def oprefetch(g):
        if g >= NG:
            return
        toks = []
        ov = oview(g % 2)
        for cch in range(32):
            toks.append(ctx.dma_tr(
                "sync", ov[:, cch * 512:(cch + 1) * 512],
                of4[g * 512:(g + 1) * 512, cch * 128:(cch + 1) * 128],
                waits=[oslot_cons[g % 2], pbsnap[0]]))
        oslot_toks[g % 2] = toks
    oprefetch(0)
    xres_cons = [passA_vec, passA_vec]
    for g in range(NG):
        slot = g % 2
        ttoks = oslot_toks[slot]
        ov = oview(slot)
        oprefetch(g + 1)
        for ti in range(4):
            tile = g * 4 + ti
            t0 = tile * 128
            p = ti % 2
            xrv = xres[:, p * 1024:(p + 1) * 1024]
            xl = ctx.dma("sync", xrv, x_ap[t0:t0 + 128, :],
                         waits=[xres_cons[p]])
            for ms in range(2):
                pb, pi = ctx.bank()
                pairs = [(ov[:, cch * 512 + ti * 128: cch * 512 + (ti + 1) * 128],
                          wbig[:, cch * 1024 + ms * 512: cch * 1024 + (ms + 1) * 512])
                         for cch in range(32)]
                pairs.append((com.ones_row[:, 0:128],
                              bo_sb[:, ms * 512:(ms + 1) * 512]))
                mt = ctx.mm_chain(pb[:], pairs,
                                  waits=ttoks + wop_tok +
                                  [ctx.ptok[pi], bo_tok, com.init_tok])
                a1 = ctx.op("vector", lambda e, pb=pb, xrv=xrv, ms=ms:
                            e.tensor_add(xrv[:, ms * 512:(ms + 1) * 512],
                                         xrv[:, ms * 512:(ms + 1) * 512], pb[:]),
                            waits=[mt, xl])
                ctx.ptok[pi] = a1
                if ms == 1:
                    so = ctx.dma_st("sync", xout_ap[t0:t0 + 128, :], xrv,
                                    waits=[a1])
                    if lnf is not None:
                        lt = emit_lnf_tile(ctx, com, "A", lnf[0], xrv, tile,
                                           lnf[1], [a1])
                        xres_cons[p] = lt
                    else:
                        xres_cons[p] = so
        oslot_cons[slot] = Tok(ctx.sems["tensor"], ctx.sems["tensor"].n)
    return Tok(ctx.sems["sync"], ctx.sems["sync"].n)


def emit_first_match(ctx, name_tag, gl, h, scratch2, waits):
    mx = scratch2[:, 0:1]
    cc = scratch2[:, 1:2]
    tp_ = scratch2[:, 2:3]
    t1 = ctx.op("vector", lambda e: e.reduce_max(mx, gl, axis=AXX), waits=waits)
    t2 = ctx.op("vector", lambda e: e.tensor_scalar(
        h, gl, mx, None, OP.is_ge), waits=[t1])
    t3 = ctx.op("gpsimd", lambda e: e.memset(cc, 1.0), waits=[t2])
    last = t3
    for ee in range(8):
        a = ctx.op("vector", lambda e, ee=ee: e.tensor_mul(
            h[:, ee:ee + 1], h[:, ee:ee + 1], cc), waits=[last])
        if ee < 7:
            b = ctx.op("vector", lambda e, ee=ee: e.tensor_scalar(
                tp_, h[:, ee:ee + 1], -1.0, 1.0, OP.mult, OP.add), waits=[a])
            last = ctx.op("vector", lambda e: e.tensor_mul(cc, cc, tp_),
                          waits=[b])
        else:
            last = a
    return last


def emit_l0_ffn(ctx, com, x_ap, xn16_ap, w, xout_ap, mem=None):
    if mem is None:
        mem = ctx.sbuf
    nc = ctx.nc
    f_wi = mem("f_wi", [128, 8 * 4096], BF16)
    f_wo = mem("f_wo", [128, 32 * 1024], BF16)
    a_sb = mem("f_asb", [128, 8 * 32], BF16)
    b_sb = mem("f_bsb", [32, 4096], BF16)
    gwt_sb = mem("f_gwt", [128, 8 * 8], BF16)
    gb_sb = mem("f_gb", [1, 8], F32)
    wib_sb = mem("f_wib", [128, 32], F32)
    wob_sb = mem("f_wob", [1, 1024], F32)
    TG = 256
    xnT = mem("f_xnT", [128, 2 * 8 * TG], BF16)
    hbuf = mem("f_hbuf", [128, 32 * TG], BF16)
    rtT = mem("f_rtT", [32, TG], BF16)
    gl = mem("f_gl", [128, 8], F32)
    hmask = mem("f_h", [128, 8], F32)
    rr = mem("f_rr", [128, 32], F32)
    rb16 = mem("f_rb16", [128, 32], BF16)
    sc2 = mem("f_sc2", [128, 4], F32)
    xres = mem("f_xres", [128, 1024], F32)
    id16, idtok = com.ident16()

    wt = []
    for k in range(8):
        wt.append(ctx.dma("gpsimd", f_wi[:, k * 4096:(k + 1) * 4096],
                          w["wiT"][k * 128:(k + 1) * 128, :]))
    for c in range(32):
        wt.append(ctx.dma("gpsimd", f_wo[:, c * 1024:(c + 1) * 1024],
                          w["woT"][c * 128:(c + 1) * 128, :]))
    for k in range(8):
        wt.append(ctx.dma("gpsimd", a_sb[:, k * 32:(k + 1) * 32],
                          w["A_lay"][k * 128:(k + 1) * 128, :]))
        wt.append(ctx.dma("gpsimd", gwt_sb[:, k * 8:(k + 1) * 8],
                          w["gWT"][k * 128:(k + 1) * 128, :]))
    wt.append(ctx.dma("gpsimd", b_sb[:], w["B_flat"][:]))
    wt.append(ctx.dma("sync", gb_sb[:], w["gate_b"][:]))
    wt.append(ctx.dma("sync", wib_sb[:], w["wi_b"][:]))
    wt.append(ctx.dma("sync", wob_sb[:], w["wo_b"][:]))
    wtok = [Tok(ctx.sems["gpsimd"], ctx.sems["gpsimd"].n),
            Tok(ctx.sems["sync"], ctx.sems["sync"].n)]

    xcons = [None]
    hb_cons = [None]
    rt_cons = [None]
    xres_cons = [None, None]
    NGRF = S // TG
    xslot_toks = [None, None]
    xslot_cons = [None, None]

    def xview(slot):
        return xnT[:, slot * 8 * TG:(slot + 1) * 8 * TG]

    def xprefetch(t):
        if t >= NGRF:
            return
        xslot_toks[t % 2] = load_xnT(ctx, xview(t % 2), xn16_ap, t,
                                     waits=[xslot_cons[t % 2]], tg=TG)
    xprefetch(0)
    for g in range(NGRF):
        slot = g % 2
        xtoks = xslot_toks[slot]
        xv = xview(slot)
        for ti in range(TG // 128):
            xsl = lambda k, ti=ti: xv[:, k * TG + ti * 128: k * TG + (ti + 1) * 128]
            pb, pi = ctx.bank()
            pairs = [(xsl(k), gwt_sb[:, k * 8:(k + 1) * 8]) for k in range(8)]
            pairs.append((com.ones_row[:, 0:128], gb_sb[:]))
            mt = ctx.mm_chain(pb[:, 0:8], pairs,
                              waits=xtoks + wtok + [ctx.ptok[pi], com.init_tok])
            e1 = ctx.op("vector", lambda e, pb=pb: e.tensor_copy(gl[:], pb[:, 0:8]),
                        waits=[mt])
            ctx.ptok[pi] = e1
            hm = emit_first_match(ctx, f"g{g}t{ti}", gl[:], hmask[:], sc2[:],
                                  waits=[e1])
            pb2, pi2 = ctx.bank()
            pairs2 = [(xsl(k), a_sb[:, k * 32:(k + 1) * 32]) for k in range(8)]
            mt2 = ctx.mm_chain(pb2[:, 0:32], pairs2,
                               waits=xtoks + wtok + [ctx.ptok[pi2]])
            e2 = ctx.op("vector", lambda e, pb2=pb2:
                        e.tensor_copy(rr[:], pb2[:, 0:32]), waits=[mt2])
            ctx.ptok[pi2] = e2
            e3 = ctx.op("vector", lambda e: e.tensor_tensor(
                rb16[:].rearrange("p (e r) -> p e r", e=8),
                rr[:].rearrange("p (e r) -> p e r", e=8),
                hmask[:, :, None].broadcast_to([128, 8, 4]), OP.mult),
                waits=[e2, hm])
            pb3, pi3 = ctx.bank()
            ptr = pb3[0:32, 0:64].bitcast(BF16)
            mt3 = ctx.op("tensor", lambda e, ptr=ptr: e.transpose(
                ptr, rb16[:], id16[:]), waits=[e3, idtok, ctx.ptok[pi3]])
            e4 = ctx.op("vector", lambda e, ptr=ptr, ti=ti: e.tensor_copy(
                rtT[:, ti * 128:(ti + 1) * 128], ptr),
                waits=[mt3, rt_cons[0] if ti == 0 else None])
            ctx.ptok[pi3] = e4
        rt_tok = Tok(ctx.sems["vector"], ctx.sems["vector"].n)
        up_toks = []
        for fc in range(32):
            pb, pi = ctx.bank()
            pairs = [(f_wi[:, k * 4096 + fc * 128: k * 4096 + (fc + 1) * 128],
                      xv[:, k * TG:(k + 1) * TG]) for k in range(8)]
            pairs.append((b_sb[:, fc * 128:(fc + 1) * 128], rtT[:]))
            mt = ctx.mm_chain(pb[:, 0:TG], pairs,
                              waits=xtoks + wtok + [ctx.ptok[pi], rt_tok,
                                                    hb_cons[0], com.init_tok])
            r1 = ctx.op("scalar", lambda e, pb=pb, fc=fc: e.activation(
                hbuf[:, fc * TG:(fc + 1) * TG], pb[:, 0:TG], AF.Relu,
                bias=wib_sb[:, fc:fc + 1]),
                waits=[mt])
            ctx.ptok[pi] = r1
            up_toks.append(r1)
        xslot_cons[slot] = Tok(ctx.sems["tensor"], ctx.sems["tensor"].n)
        xprefetch(g + 1)
        for ti in range(TG // 128):
            tile = g * (TG // 128) + ti
            t0 = tile * 128
            xrv = xres[:]
            xl = ctx.dma("sync", xrv, x_ap[t0:t0 + 128, :],
                         waits=[xres_cons[0]])
            for ms in range(2):
                pb, pi = ctx.bank()
                pairs = [(hbuf[:, c * TG + ti * 128: c * TG + (ti + 1) * 128],
                          f_wo[:, c * 1024 + ms * 512: c * 1024 + (ms + 1) * 512])
                         for c in range(32)]
                pairs.append((com.ones_row[:, 0:128],
                              wob_sb[:, ms * 512:(ms + 1) * 512]))
                mt = ctx.mm_chain(pb[:], pairs,
                                  waits=up_toks + wtok + [ctx.ptok[pi]])
                a1 = ctx.op("vector", lambda e, pb=pb, xrv=xrv, ms=ms:
                            e.tensor_add(xrv[:, ms * 512:(ms + 1) * 512],
                                         xrv[:, ms * 512:(ms + 1) * 512], pb[:]),
                            waits=[mt, xl])
                ctx.ptok[pi] = a1
                if ms == 1:
                    so = ctx.dma_st("sync", xout_ap[t0:t0 + 128, :], xrv,
                                    waits=[a1])
                    xres_cons[0] = so
        hb_cons[0] = Tok(ctx.sems["tensor"], ctx.sems["tensor"].n)
        rt_cons[0] = hb_cons[0]
    return Tok(ctx.sems["sync"], ctx.sems["sync"].n)


def emit_l1_attn(ctx, com, x_ap, xn16_ap, w, xout_ap, mem=None, lnf=None):
    if mem is None:
        mem = ctx.sbuf
    nc = ctx.nc
    C = 128
    NCH = S // C
    id16, id16_tok = com.ident16()
    idf, idf_tok = com.const("ident128")
    stril, m1 = com.const("stril")
    sutril, m2 = com.const("sutril")
    triu0, m4 = com.const("triu0")
    Dstrict, m5 = com.const("Dstrict")
    DstrictT, m6 = com.const("DstrictT")
    DinclT, m7 = com.const("DinclT")
    iota, m8 = com.const("iota")
    e127, m9 = com.const("e127")
    ctoks = [m1, m2, m4, m5, m6, m7, m8, m9, idf_tok, id16_tok]

    pk = mem("t_pk", [128, 8 * 64], BF16)
    pv = mem("t_pv", [128, 8 * 64], BF16)
    pq = mem("t_pq", [128, 8 * 64], BF16)
    pir = mem("t_pir", [128, 8 * 64], BF16)
    ppw = mem("t_ppw", [128, 8 * 3], BF16)
    nm1 = mem("t_nm1", [64, 128], BF16)
    nm2 = mem("t_nm2", [128, 64], BF16)
    wo1 = mem("t_wo1", [64, 1024], BF16)
    rows = mem("t_rows", [1, 512], F32)
    fa_sb = mem("t_fa", [1, 2], F32)
    for k in range(8):
        ctx.dma("gpsimd", pk[:, k * 64:(k + 1) * 64],
                w["WkT1"][k * 128:(k + 1) * 128, :])
        ctx.dma("gpsimd", pv[:, k * 64:(k + 1) * 64],
                w["WvT1"][k * 128:(k + 1) * 128, :])
        ctx.dma("gpsimd", pq[:, k * 64:(k + 1) * 64],
                w["WqT1"][k * 128:(k + 1) * 128, :])
        ctx.dma("gpsimd", pir[:, k * 64:(k + 1) * 64],
                w["ir1T"][k * 128:(k + 1) * 128, :])
        ctx.dma("gpsimd", ppw[:, k * 3:(k + 1) * 3],
                w["pwT"][k * 128:(k + 1) * 128, :])
    ctx.dma("gpsimd", nm1[:], w["nm1T"][:])
    ctx.dma("gpsimd", nm2[:], w["nm2T"][:])
    ctx.dma("gpsimd", wo1[:], w["WoT1"][:])
    ctx.dma("sync", rows[:, 0:64], w["ir_b1"][:])
    ctx.dma("sync", rows[:, 64:192], w["nm_b1"][:])
    ctx.dma("sync", rows[:, 192:256], w["nm_b2"][:])
    ctx.dma("sync", rows[:, 256:320], w["ir_w2"][:])
    ctx.dma("sync", rows[:, 320:384], w["norm_w"][:])
    ctx.dma("sync", rows[:, 384:385], w["ir_b2"][:])
    ctx.dma("sync", fa_sb[:, 0:1], w["fa"][:])
    ctx.dma("sync", fa_sb[:, 1:2], w["sa"][:])
    wtok = [Tok(ctx.sems["gpsimd"], ctx.sems["gpsimd"].n),
            Tok(ctx.sems["sync"], ctx.sems["sync"].n)]

    def bcast_col(src_1x1, name, waits):
        col = mem("t_col_" + name, [128, 1], F32)
        pb, pi = ctx.bank()
        mt = ctx.op("tensor", lambda e, pb=pb: e.matmul(
            pb[:, 0:1], com.ones_row[:, 0:128], src_1x1, start=True, stop=True),
            waits=list(waits) + [com.init_tok, ctx.ptok[pi]])
        cp = ctx.op("vector", lambda e, pb=pb: e.tensor_copy(col[:], pb[:, 0:1]),
                    waits=[mt])
        ctx.ptok[pi] = cp
        return col, cp

    def bcast_rep(src_row, npart, name, waits, dtype=F32):
        n = src_row.shape[1]
        rep = mem("t_rep_" + name, [npart, n], dtype)
        pb, pi = ctx.bank()
        mt = ctx.op("tensor", lambda e, pb=pb: e.matmul(
            pb[0:npart, 0:n], com.ones_row[:, 0:npart], src_row,
            start=True, stop=True),
            waits=list(waits) + [com.init_tok, ctx.ptok[pi]])
        cp = ctx.op("vector", lambda e, pb=pb: e.tensor_copy(
            rep[:], pb[0:npart, 0:n]), waits=[mt])
        ctx.ptok[pi] = cp
        return rep, cp

    sc_a = mem("t_sca", [1, 16], F32)

    def alpha_pack(which, a_src, tag):
        P = {}
        base = which * 8
        lnA = sc_a[:, base:base + 1]
        aC = sc_a[:, base + 1:base + 2]
        t1 = ctx.op("scalar", lambda e: e.activation(lnA, a_src, AF.Ln),
                    waits=wtok + [com.init_tok])
        t2 = ctx.op("scalar", lambda e: e.activation(aC, lnA, AF.Exp,
                                                     scale=128.0), waits=[t1])
        lncol, c1 = bcast_col(lnA, tag + "_ln", [t1])
        aCcol, c2 = bcast_col(aC, tag + "_aC", [t2])
        for nm, Dc, msk in (("Ts", Dstrict, stril), ("TsT", DstrictT, sutril),
                            ("TiT", DinclT, triu0)):
            buf = mem(f"t_{tag}_{nm}", [128, 128], F32)
            a1 = ctx.op("scalar", lambda e, buf=buf, Dc=Dc: e.activation(
                buf[:], Dc[:], AF.Exp, scale=lncol[:]), waits=[c1] + ctoks)
            a2 = ctx.op("vector", lambda e, buf=buf, msk=msk: e.tensor_mul(
                buf[:], buf[:], msk[:]), waits=[a1])
            P[nm] = (buf, a2)
        apow = mem(f"t_{tag}_apow", [128, 3], F32)
        p1 = ctx.op("scalar", lambda e: e.activation(
            apow[:, 0:1], iota[:], AF.Exp, scale=lncol[:]), waits=[c1] + ctoks)
        p2 = ctx.op("scalar", lambda e: e.activation(
            apow[:, 1:2], iota[:], AF.Exp, scale=lncol[:], bias=lncol[:]),
            waits=[c1] + ctoks)
        m127 = sc_a[:, base + 2:base + 3]
        mneg = sc_a[:, base + 3:base + 4]
        q1 = ctx.op("vector", lambda e: e.tensor_scalar(
            m127, lnA, 127.0, None, OP.mult), waits=[t1])
        q2 = ctx.op("vector", lambda e: e.tensor_scalar(
            mneg, lnA, -1.0, None, OP.mult), waits=[t1])
        m127c, c3 = bcast_col(m127, tag + "_127", [q1])
        mnegc, c4 = bcast_col(mneg, tag + "_neg", [q2])
        p3 = ctx.op("scalar", lambda e: e.activation(
            apow[:, 2:3], iota[:], AF.Exp, scale=mnegc[:], bias=m127c[:]),
            waits=[c3, c4] + ctoks)
        P["apow"] = (apow, [p1, p2, p3])
        omb = sc_a[:, base + 4:base + 5]
        b1 = ctx.op("vector", lambda e: e.tensor_scalar(
            omb, a_src, -1.0, 1.0, OP.mult, OP.add), waits=wtok)
        ombcol, c5 = bcast_col(omb, tag + "_omb", [b1])
        P["ombcol"] = (ombcol, c5)
        nomb = mem(f"t_{tag}_nomb", [128, 1], F32)
        n1 = ctx.op("vector", lambda e: e.tensor_scalar(
            nomb[:], ombcol[:], -1.0, None, OP.mult), waits=[c5])
        P["nomb"] = (nomb, n1)
        iac = mem(f"t_{tag}_iac", [64, 64], BF16)
        i1 = ctx.op("vector", lambda e: e.tensor_scalar(
            iac[:], idf[0:64, 0:64], aCcol[0:64, :], None, OP.mult),
            waits=[c2] + ctoks)
        P["iac"] = (iac, i1)
        rsf = mem(f"t_{tag}_rsf", [128, 1], F32)
        r1 = ctx.op("vector", lambda e: e.scalar_tensor_tensor(
            rsf[:], apow[:, 0:1], -1.0, ombcol[:], OP.mult, OP.mult),
            waits=[p1, c5])
        P["rsf"] = (rsf, r1)
        prow = mem(f"t_{tag}_prow", [1, 2 * 128], F32)
        etoks = []
        for jj in range(2):
            pb, pi = ctx.bank()
            tr = ctx.op("tensor", lambda e, pb=pb, jj=jj: e.transpose(
                pb[0:1, 0:128], apow[:, jj:jj + 1], idf[:]),
                waits=[p1, p2, ctx.ptok[pi]] + ctoks)
            e1 = ctx.op("vector", lambda e, pb=pb, jj=jj: e.tensor_copy(
                prow[:, jj * 128:(jj + 1) * 128], pb[0:1, 0:128]), waits=[tr])
            ctx.ptok[pi] = e1
            etoks.append(e1)
        rep0, e2 = bcast_rep(prow[:, 0:128], 64, tag + "_ap0", etoks, BF16)
        rep1, e3 = bcast_rep(prow[:, 128:256], 64, tag + "_ap1", etoks, BF16)
        P["aprep"] = ((rep0, rep1), [e2, e3])
        return P

    PF = alpha_pack(0, fa_sb[:, 0:1], "fa")
    PS = alpha_pack(1, fa_sb[:, 1:2], "sa")
    omsa_col, omsa_tok = PS["ombcol"]
    w2rep, w2tok = bcast_rep(rows[:, 256:320], 128, "w2", wtok)
    nwrep, nwtok = bcast_rep(rows[:, 320:384], 128, "nw", wtok)
    b2col, b2tok = bcast_col(rows[:, 384:385], "irb2", wtok)
    ptoks = ([w2tok, nwtok, b2tok, omsa_tok, com.init_tok] + ctoks + wtok)

    Ocf = mem("t_Ocf", [128, NCH * 64], F32)
    Ocs = mem("t_Ocs", [128, NCH * 64], F32)
    nu_all = mem("t_nu", [128, NCH * 64], F32)
    v_all = mem("t_v", [128, NCH * 64], F32)
    pw_all = mem("t_pwall", [128, NCH * 3], F32)
    u01_all = mem("t_u01", [128, NCH], F32)
    Ssb = mem("t_S", [64, 4 * 64], BF16)
    sz = ctx.op("gpsimd", lambda e: e.memset(Ssb[:], 0.0))

    xnT = mem("t_xnT", [128, 2 * 8 * 512], BF16)
    NB = 2
    kT = mem("t_kT", [64, NB * 128], BF16)
    qT = mem("t_qT", [64, NB * 128], BF16)
    vT = mem("t_vT", [64, NB * 128], BF16)
    k16 = mem("t_k16", [128, NB * 64], BF16)
    q16 = mem("t_q16", [128, NB * 64], BF16)
    v16 = mem("t_v16", [128, NB * 64], BF16)
    qTs = mem("t_qTs", [64, NB * 4 * 128], BF16)
    Ksc = mem("t_Ksc", [128, NB * 2 * 64], BF16)
    Vb = mem("t_Vb", [128, NB * 2 * 64], F32)
    rs_s = mem("t_rss", [128, NB], F32)
    bscol = mem("t_bscol", [128, NB], F32)
    nbscol = mem("t_nbscol", [128, NB], F32)
    bsrow = mem("t_bsrow", [1, NB * 128], F32)
    brep = mem("t_brep", [128, NB * 128], F32)
    Gm = mem("t_G", [128, NB * 128], F32)
    MT = mem("t_MT", [128, NB * 4 * 128], BF16)
    Bp = mem("t_Bp", [128, NB * 2 * 128], BF16)
    BpT = mem("t_BpT", [128, NB * 2 * 128], BF16)
    PTb = mem("t_PTb", [128, NB * 2 * 128], BF16)
    scr = mem("t_scr", [128, NB * 128], F32)
    scr2 = mem("t_scr2", [128, NB * 128], F32)
    ev1 = mem("t_ev1", [128, NB * 64], F32)
    ev2 = mem("t_ev2", [128, NB * 64], F32)
    sm = mem("t_sm", [128, NB * 8], F32)
    pws = mem("t_pws", [128, NB * 8], F32)
    h1g16 = mem("t_h1g16", [128, NB * 128], BF16)
    h1gT = mem("t_h1gT", [128, NB * 128], BF16)
    U16 = mem("t_U16", [128, NB * 2 * 64], BF16)
    Z16 = mem("t_Z16", [128, NB * 2 * 64], BF16)
    Ofp = mem("t_Ofp", [128, NB * 64], F32)
    errb = mem("t_errb", [128, NB * 64], F32)

    lu = {}
    xg_toks = [None, None]
    xg_cons = [None, None]

    def xgview(slot):
        return xnT[:, slot * 4096:(slot + 1) * 4096]

    def xgprefetch(g):
        if g >= NG:
            return
        xg_toks[g % 2] = load_xnT(ctx, xgview(g % 2), xn16_ap, g,
                                  waits=[xg_cons[g % 2]])
    xgprefetch(0)

    def P0(c):
        b = c % NB
        g, ti = divmod(c, 4)
        xslot = g % 2
        xtoks = xg_toks[xslot]
        xv = xgview(xslot)
        xofc = (ti // 2) * 2048 + (ti % 2) * 128
        xsl = lambda k: xv[:, xofc + k * 256: xofc + k * 256 + 128]

        def proj(wtiles, n, bias_row=None):
            pb, pi = ctx.bank()
            pairs = [(xsl(k), wtiles[:, k * n:(k + 1) * n]) for k in range(8)]
            if bias_row is not None:
                pairs.append((com.ones_row[:, 0:128], bias_row))
            mt = ctx.mm_chain(pb[:, 0:n], pairs,
                              waits=xtoks + ptoks + [ctx.ptok[pi]])
            return pb, pi, mt

        def l2norm16(pb, pi, mt, out16, war16):
            ff = ev1[:, b * 64:(b + 1) * 64]
            smv = sm[:, b * 8:(b + 1) * 8]
            c1 = ctx.op("vector", lambda e, pb=pb, ff=ff: e.tensor_copy(
                ff, pb[:, 0:64]), waits=[mt, lu.get(("ev1", b))])
            ctx.ptok[pi] = c1
            s1 = ctx.op("scalar", lambda e, ff=ff: e.activation(
                errb[:, b * 64:(b + 1) * 64], ff, AF.Square),
                waits=[c1, lu.get(("errb", b))])
            s2 = ctx.op("vector", lambda e, smv=smv: e.reduce_sum(
                smv[:, 0:1], errb[:, b * 64:(b + 1) * 64], axis=AXX),
                waits=[s1])
            s3 = ctx.op("scalar", lambda e, smv=smv: e.activation(
                smv[:, 1:2], smv[:, 0:1], AF.Sqrt), waits=[s2])
            s4 = ctx.op("vector", lambda e, smv=smv: e.tensor_single_scalar(
                smv[:, 2:3], smv[:, 1:2], 1e-12, OP.max), waits=[s3])
            s5 = ctx.op("vector", lambda e, smv=smv: e.reciprocal(
                smv[:, 3:4], smv[:, 2:3]), waits=[s4])
            s6 = ctx.op("vector", lambda e, ff=ff, smv=smv, out16=out16:
                        e.tensor_scalar(out16, ff, smv[:, 3:4], None, OP.mult),
                        waits=[s5, war16])
            lu[("ev1", b)] = s6
            lu[("errb", b)] = s2
            return s6

        def tr64(src16, dstT, deps, dst_key, src_key):
            pb2, pi2 = ctx.bank()
            ptr = pb2[0:64, 0:64].bitcast(BF16)
            t = ctx.op("tensor", lambda e, ptr=ptr, src16=src16: e.transpose(
                ptr, src16, id16[:]), waits=deps + [ctx.ptok[pi2], id16_tok])
            cc = ctx.op("vector", lambda e, ptr=ptr, dstT=dstT: e.tensor_copy(
                dstT, ptr), waits=[t, lu.get(dst_key)])
            ctx.ptok[pi2] = cc
            lu[src_key] = t
            return t, cc

        pb, pi, mt = proj(pk, 64)
        k16v = k16[:, b * 64:(b + 1) * 64]
        kk = l2norm16(pb, pi, mt, k16v, lu.get(("k16", b)))
        kTv = kT[:, b * 128:(b + 1) * 128]
        _tt, kTt = tr64(k16v, kTv, [kk], ("kT", b), ("k16", b))
        ksf = Ksc[:, (b * 2) * 64:(b * 2 + 1) * 64]
        kss = Ksc[:, (b * 2 + 1) * 64:(b * 2 + 2) * 64]
        kc1 = ctx.op("vector", lambda e: e.tensor_scalar(
            ksf, k16v, PF["apow"][0][:, 2:3], None, OP.mult),
            waits=[kk, lu.get(("Ksc", b))] + PF["apow"][1])
        kc2 = ctx.op("vector", lambda e: e.tensor_scalar(
            kss, k16v, PS["apow"][0][:, 2:3], None, OP.mult),
            waits=[kk] + PS["apow"][1])
        lu[("Ksc_w", b)] = kc2
        pb, pi, mt = proj(pq, 64)
        q16v = q16[:, b * 64:(b + 1) * 64]
        qq = l2norm16(pb, pi, mt, q16v, lu.get(("q16", b)))
        qTv = qT[:, b * 128:(b + 1) * 128]
        _tt, qTt = tr64(q16v, qTv, [qq], ("qT", b), ("q16", b))
        qs_toks = []
        for st in range(2):
            PX = PF if st == 0 else PS
            for jj in range(2):
                dst = qTs[:, (b * 4 + st * 2 + jj) * 128:
                          (b * 4 + st * 2 + jj + 1) * 128]
                tk = ctx.op("vector", lambda e, dst=dst, qTv=qTv, PX=PX, jj=jj:
                            e.tensor_mul(dst, qTv, PX["aprep"][0][jj][:]),
                            waits=[qTt, lu.get(("qTs", b))] + PX["aprep"][1])
                qs_toks.append(tk)
        lu[("qTs_w", b)] = qs_toks[-1]
        pb, pi, mt = proj(pv, 64)
        vav = v_all[:, c * 64:(c + 1) * 64]
        vc1 = ctx.op("vector", lambda e, pb=pb, vav=vav: e.tensor_copy(
            vav, pb[:, 0:64]), waits=[mt])
        v16v = v16[:, b * 64:(b + 1) * 64]
        vc2 = ctx.op("vector", lambda e, pb=pb, v16v=v16v: e.tensor_copy(
            v16v, pb[:, 0:64]), waits=[mt, lu.get(("v16", b))])
        ctx.ptok[pi] = vc2
        vTv = vT[:, b * 128:(b + 1) * 128]
        _tt, vTt = tr64(v16v, vTv, [vc2], ("vT", b), ("v16", b))
        vbf = Vb[:, (b * 2) * 64:(b * 2 + 1) * 64]
        vb1 = ctx.op("vector", lambda e, vav=vav, vbf=vbf: e.tensor_scalar(
            vbf, vav, PF["ombcol"][0][:], None, OP.mult),
            waits=[vc1, PF["ombcol"][1], lu.get(("Vb", b))])
        pb, pi, mt = proj(pir, 64, bias_row=rows[:, 0:64])
        hh = ev1[:, b * 64:(b + 1) * 64]
        i1 = ctx.op("scalar", lambda e, pb=pb, hh=hh: e.activation(
            hh, pb[:, 0:64], AF.Gelu), waits=[mt, lu.get(("ev1", b))])
        ctx.ptok[pi] = i1
        i2 = ctx.op("vector", lambda e, hh=hh: e.tensor_mul(
            ev2[:, b * 64:(b + 1) * 64], hh, w2rep[:]),
            waits=[i1, lu.get(("ev2", b))])
        lu[("ev1", b)] = i2
        smv = sm[:, b * 8:(b + 1) * 8]
        i3 = ctx.op("vector", lambda e, smv=smv: e.reduce_sum(
            smv[:, 4:5], ev2[:, b * 64:(b + 1) * 64], axis=AXX), waits=[i2])
        lu[("ev2", b)] = i3
        i4 = ctx.op("vector", lambda e, smv=smv: e.tensor_add(
            smv[:, 5:6], smv[:, 4:5], b2col[:]), waits=[i3])
        i5 = ctx.op("scalar", lambda e, smv=smv: e.activation(
            smv[:, 6:7], smv[:, 5:6], AF.Sigmoid), waits=[i4])
        i6 = ctx.op("vector", lambda e, smv=smv: e.tensor_single_scalar(
            smv[:, 7:8], smv[:, 6:7], 0.5, OP.is_gt), waits=[i5])
        bsv = bscol[:, b:b + 1]
        i7 = ctx.op("vector", lambda e, smv=smv, bsv=bsv: e.tensor_scalar(
            bsv, smv[:, 7:8], omsa_col[:], None, OP.mult),
            waits=[i6, lu.get(("bscol", b))])
        i8 = ctx.op("vector", lambda e, bsv=bsv: e.tensor_scalar(
            nbscol[:, b:b + 1], bsv, -1.0, None, OP.mult), waits=[i7])
        i9 = ctx.op("vector", lambda e, bsv=bsv: e.scalar_tensor_tensor(
            rs_s[:, b:b + 1], PS["apow"][0][:, 0:1], -1.0, bsv,
            OP.mult, OP.mult), waits=[i7] + PS["apow"][1])
        vbs = Vb[:, (b * 2 + 1) * 64:(b * 2 + 2) * 64]
        i10 = ctx.op("vector", lambda e, vbs=vbs, vav=vav, bsv=bsv:
                     e.tensor_scalar(vbs, vav, bsv, None, OP.mult),
                     waits=[i7, vc1])
        lu[("Vb_w", b)] = i10
        pb, pi = ctx.bank()
        mtn = ctx.mm_chain(pb[:, 0:128],
                           [(vTv, nm1[:]), (com.ones_row[:, 0:128],
                                            rows[:, 64:192])],
                           waits=[vTt, ctx.ptok[pi]] + ptoks)
        lu[("vT", b)] = mtn
        n1 = ctx.op("scalar", lambda e, pb=pb: e.activation(
            h1g16[:, b * 128:(b + 1) * 128], pb[:, 0:128], AF.Gelu),
            waits=[mtn, lu.get(("h1g16", b))])
        ctx.ptok[pi] = n1
        pb2, pi2 = ctx.bank()
        ptr = pb2[:, 0:64].bitcast(BF16)
        n2 = ctx.op("tensor", lambda e, ptr=ptr: e.transpose(
            ptr, h1g16[:, b * 128:(b + 1) * 128], id16[:]),
            waits=[n1, ctx.ptok[pi2], id16_tok])
        n3 = ctx.op("vector", lambda e, ptr=ptr: e.tensor_copy(
            h1gT[:, b * 128:(b + 1) * 128], ptr),
            waits=[n2, lu.get(("h1gT", b))])
        ctx.ptok[pi2] = n3
        lu[("h1g16", b)] = n2
        pb3, pi3 = ctx.bank()
        mtn2 = ctx.mm_chain(pb3[:, 0:64],
                            [(h1gT[:, b * 128:(b + 1) * 128], nm2[:]),
                             (com.ones_row[:, 0:128], rows[:, 192:256])],
                            waits=[n3, ctx.ptok[pi3]] + ptoks)
        n4 = ctx.op("vector", lambda e, pb3=pb3: e.tensor_copy(
            nu_all[:, c * 64:(c + 1) * 64], pb3[:, 0:64]), waits=[mtn2])
        ctx.ptok[pi3] = n4
        lu[("h1gT", b)] = mtn2
        pb, pi, mt = proj(ppw, 3)
        pv3 = pws[:, b * 8:b * 8 + 3]
        pm = pws[:, b * 8 + 3:b * 8 + 4]
        g1 = ctx.op("vector", lambda e, pb=pb, pv3=pv3: e.tensor_copy(
            pv3, pb[:, 0:3]), waits=[mt, lu.get(("pws", b))])
        ctx.ptok[pi] = g1
        g2 = ctx.op("vector", lambda e, pv3=pv3, pm=pm: e.reduce_max(
            pm, pv3, axis=AXX), waits=[g1])
        g3 = ctx.op("vector", lambda e, pv3=pv3, pm=pm: e.tensor_scalar(
            pv3, pv3, pm, None, OP.subtract), waits=[g2])
        g4 = ctx.op("scalar", lambda e, pv3=pv3: e.activation(
            pv3, pv3, AF.Exp), waits=[g3])
        g5 = ctx.op("vector", lambda e, pv3=pv3, pm=pm: e.reduce_sum(
            pm, pv3, axis=AXX), waits=[g4])
        g6 = ctx.op("vector", lambda e, pm=pm: e.reciprocal(pm, pm),
                    waits=[g5])
        g7 = ctx.op("vector", lambda e, pv3=pv3, pm=pm, c=c: e.tensor_scalar(
            pw_all[:, c * 3:(c + 1) * 3], pv3, pm, None, OP.mult),
            waits=[g6])
        lu[("pws", b)] = g7
        pb, pi = ctx.bank()
        mtg = ctx.mm_chain(pb[:, 0:128], [(kTv, kTv)],
                           waits=[kTt, ctx.ptok[pi]])
        gg = ctx.op("vector", lambda e, pb=pb: e.tensor_copy(
            Gm[:, b * 128:(b + 1) * 128], pb[:, 0:128]),
            waits=[mtg, lu.get(("Gm", b))])
        ctx.ptok[pi] = gg
        pb2, pi2 = ctx.bank()
        mtq = ctx.mm_chain(pb2[:, 0:128], [(kTv, qTv)],
                           waits=[kTt, qTt, ctx.ptok[pi2]])
        gq = scr[:, b * 128:(b + 1) * 128]
        gqe = ctx.op("scalar", lambda e, gq=gq, pb2=pb2: e.activation(
            gq, pb2[:, 0:128], AF.Copy), waits=[mtq, lu.get(("scr", b))])
        ctx.ptok[pi2] = gqe
        mt_toks = []
        for st in range(2):
            PX = PF if st == 0 else PS
            for jj, nm in ((0, "TiT"), (1, "TsT")):
                dst = MT[:, (b * 4 + st * 2 + jj) * 128:
                         (b * 4 + st * 2 + jj + 1) * 128]
                tk = ctx.op("gpsimd", lambda e, dst=dst, gq=gq, PX=PX, nm=nm:
                            e.tensor_tensor(dst, gq, PX[nm][0][:], OP.mult),
                            waits=[gqe, PX[nm][1], lu.get(("MT", b))])
                mt_toks.append(tk)
        lu[("scr", b)] = mt_toks[-1]
        lu[("qT", b)] = mtq
        bsl = lambda buf, st: buf[:, (b * 2 + st) * 128:(b * 2 + st + 1) * 128]
        Gv = Gm[:, b * 128:(b + 1) * 128]
        wpair = [None, None]
        for st in range(2):
            Bv, BTv = bsl(Bp, st), bsl(BpT, st)
            if st == 0:
                w1 = ctx.op("vector", lambda e, Bv=Bv, Gv=Gv: e.scalar_tensor_tensor(
                    Bv, Gv, PF["nomb"][0][:], PF["Ts"][0][:],
                    OP.mult, OP.mult),
                    waits=[gg, PF["nomb"][1], PF["Ts"][1], lu.get(("Bp", b, st))])
                w2 = ctx.op("vector", lambda e, BTv=BTv, Gv=Gv: e.scalar_tensor_tensor(
                    BTv, Gv, PF["nomb"][0][:], PF["TsT"][0][:],
                    OP.mult, OP.mult),
                    waits=[gg, PF["nomb"][1], PF["TsT"][1], lu.get(("BpT", b, st))])
            else:
                w1 = ctx.op("vector", lambda e, Bv=Bv, Gv=Gv: e.scalar_tensor_tensor(
                    Bv, Gv, nbscol[:, b:b + 1], PS["Ts"][0][:],
                    OP.mult, OP.mult),
                    waits=[gg, i8, PS["Ts"][1], lu.get(("Bp", b, st))])
                pbr, pir_ = ctx.bank()
                tr2 = ctx.op("tensor", lambda e, pbr=pbr: e.transpose(
                    pbr[0:1, 0:128], bscol[:, b:b + 1], idf[:]),
                    waits=[i7, ctx.ptok[pir_]] + ctoks)
                bsr = bsrow[:, b * 128:(b + 1) * 128]
                c1_ = ctx.op("vector", lambda e, pbr=pbr, bsr=bsr: e.tensor_copy(
                    bsr, pbr[0:1, 0:128]), waits=[tr2, lu.get(("bsrow", b))])
                ctx.ptok[pir_] = c1_
                pbr2, pir2 = ctx.bank()
                mtr = ctx.op("tensor", lambda e, pbr2=pbr2, bsr=bsr: e.matmul(
                    pbr2[:, 0:128], com.ones_row[:, 0:128], bsr,
                    start=True, stop=True),
                    waits=[c1_, com.init_tok, ctx.ptok[pir2]])
                brv = brep[:, b * 128:(b + 1) * 128]
                c2_ = ctx.op("vector", lambda e, pbr2=pbr2, brv=brv: e.tensor_copy(
                    brv, pbr2[:, 0:128]), waits=[mtr, lu.get(("brep", b))])
                ctx.ptok[pir2] = c2_
                lu[("bsrow", b)] = c2_
                scv = scr2[:, b * 128:(b + 1) * 128]
                u1 = ctx.op("gpsimd", lambda e, Gv=Gv, scv=scv: e.tensor_tensor(
                    scv, Gv, PS["TsT"][0][:], OP.mult),
                    waits=[gg, PS["TsT"][1], lu.get(("scr2", b))])
                w2 = ctx.op("vector", lambda e, BTv=BTv, brv=brv, scv=scv:
                            e.scalar_tensor_tensor(BTv, scv, -1.0, brv,
                                                   OP.mult, OP.mult),
                    waits=[u1, c2_, lu.get(("BpT", b, st))])
                lu[("brep", b)] = w2
                lu[("scr2", b)] = w2
            PTv = bsl(PTb, st)
            p1_ = ctx.op("vector", lambda e, PTv=PTv, BTv=BTv: e.tensor_add(
                PTv, BTv, id16[:]), waits=[w2, id16_tok, lu.get(("PTb", b, st))])
            wpair[st] = (w1, w2, p1_)
        lastB = [wpair[0][0], wpair[1][0]]
        lastBT = [wpair[0][1], wpair[1][1]]
        lastP = [wpair[0][2], wpair[1][2]]
        inv_done = [None, None]
        for kk_ in range(6):
            stage = []
            for st in range(2):
                Bv, BTv, PTv = bsl(Bp, st), bsl(BpT, st), bsl(PTb, st)
                pbA, piA = ctx.bank()
                mA = ctx.mm_chain(pbA[:, 0:128], [(BTv, Bv)],
                                  waits=[lastB[st], lastBT[st], ctx.ptok[piA]])
                pbB, piB = ctx.bank()
                mB = ctx.mm_chain(pbB[:, 0:128], [(Bv, BTv)],
                                  waits=[lastB[st], lastBT[st], ctx.ptok[piB]])
                stage.append((st, Bv, BTv, PTv, pbA, piA, mA, pbB, piB, mB))
            for (st, Bv, BTv, PTv, pbA, piA, mA, pbB, piB, mB) in stage:
                cA = ctx.op("scalar", lambda e, pbA=pbA, Bv=Bv: e.activation(
                    Bv, pbA[:, 0:128], AF.Copy), waits=[mA, mB])
                ctx.ptok[piA] = cA
                cB = ctx.op("vector", lambda e, pbB=pbB, BTv=BTv: e.tensor_copy(
                    BTv, pbB[:, 0:128]), waits=[mB, mA])
                ctx.ptok[piB] = cB
                lastB[st], lastBT[st] = cA, cB
            for (st, Bv, BTv, PTv, pbA, piA, mA, pbB, piB, mB) in stage:
                pbC, piC = ctx.bank()
                mC = ctx.mm_chain(pbC[:, 0:128],
                                  [(id16[:], PTv), (Bv, PTv)],
                                  waits=[lastB[st], lastP[st], ctx.ptok[piC],
                                         id16_tok])
                cC = ctx.op("vector", lambda e, pbC=pbC, PTv=PTv:
                            e.tensor_copy(PTv, pbC[:, 0:128]), waits=[mC])
                ctx.ptok[piC] = cC
                lastP[st] = cC
        for st in range(2):
            lu[("Bp", b, st)] = lastP[st]
            lu[("BpT", b, st)] = lastP[st]
            inv_done[st] = lastP[st]
        if ti == 3:
            xg_cons[xslot] = Tok(ctx.sems["tensor"], ctx.sems["tensor"].n)
            xgprefetch(g + 1)
        return dict(kT=kTt, Ksc=[kc1, kc2], qTs=qs_toks, Vb=[vb1, i10],
                    rs=[PF["rsf"][1], i9], inv=inv_done,
                    MT=mt_toks, v=vc1)

    def SEQ(c, d):
        b = c % NB
        ph = c % 2
        kTv = kT[:, b * 128:(b + 1) * 128]
        o_toks = {}
        for st in range(2):
            PX = PF if st == 0 else PS
            Sold = Ssb[:, (st * 2 + ph) * 64:(st * 2 + ph + 1) * 64]
            Snew = Ssb[:, (st * 2 + (ph ^ 1)) * 64:(st * 2 + (ph ^ 1) + 1) * 64]
            PTv = PTb[:, (b * 2 + st) * 128:(b * 2 + st + 1) * 128]
            Uv = U16[:, (b * 2 + st) * 64:(b * 2 + st + 1) * 64]
            Zv = Z16[:, (b * 2 + st) * 64:(b * 2 + st + 1) * 64]
            Vbv = Vb[:, (b * 2 + st) * 64:(b * 2 + st + 1) * 64]
            rsv = PF["rsf"][0][:] if st == 0 else rs_s[:, b:b + 1]
            Kscv = Ksc[:, (b * 2 + st) * 64:(b * 2 + st + 1) * 64]
            pb, pi = ctx.bank()
            m1_ = ctx.mm_chain(pb[:, 0:64], [(kTv, Sold)],
                               waits=[d["kT"], lu.get(("S", st, ph), sz),
                                      ctx.ptok[pi]])
            z1 = ctx.op("vector", lambda e, pb=pb, Zv=Zv, rsv=rsv, Vbv=Vbv:
                        e.scalar_tensor_tensor(Zv, pb[:, 0:64], rsv, Vbv,
                                               OP.mult, OP.add),
                        waits=[m1_, d["Vb"][st], d["rs"][st],
                               lu.get(("Z16", b, st))])
            ctx.ptok[pi] = z1
            pb2, pi2 = ctx.bank()
            m2_ = ctx.mm_chain(pb2[:, 0:64], [(PTv, Zv)],
                               waits=[z1, d["inv"][st], ctx.ptok[pi2]])
            u1 = ctx.op("vector", lambda e, pb2=pb2, Uv=Uv: e.tensor_copy(
                Uv, pb2[:, 0:64]), waits=[m2_, lu.get(("U16", b, st))])
            ctx.ptok[pi2] = u1
            lu[("Z16", b, st)] = m2_
            pb3, pi3 = ctx.bank()
            m3_ = ctx.mm_chain(pb3[0:64, 0:64],
                               [(PX["iac"][0][:], Sold), (Kscv, Uv)],
                               waits=[u1, PX["iac"][1], d["Ksc"][st],
                                      lu.get(("S", st, ph), sz),
                                      ctx.ptok[pi3]])
            s1 = ctx.op("vector", lambda e, pb3=pb3, Snew=Snew: e.tensor_copy(
                Snew, pb3[0:64, 0:64]),
                waits=[m3_, lu.get(("Snew", st, ph ^ 1), sz)])
            ctx.ptok[pi3] = s1
            lu[("S", st, ph ^ 1)] = s1
            qcur = qTs[:, (b * 4 + st * 2 + 1) * 128:(b * 4 + st * 2 + 2) * 128]
            qprev = qTs[:, (b * 4 + st * 2) * 128:(b * 4 + st * 2 + 1) * 128]
            MTv = MT[:, (b * 4 + st * 2) * 128:(b * 4 + st * 2 + 1) * 128]
            MpTv = MT[:, (b * 4 + st * 2 + 1) * 128:(b * 4 + st * 2 + 2) * 128]
            pb4, pi4 = ctx.bank()
            m4_ = ctx.mm_chain(pb4[:, 0:64], [(qcur, Sold), (MTv, Uv)],
                               waits=[u1, d["qTs"][st * 2 + 1],
                                      d["MT"][st * 2],
                                      lu.get(("S", st, ph), sz),
                                      ctx.ptok[pi4]])
            dst = (Ocf if st == 0 else Ocs)[:, c * 64:(c + 1) * 64]
            o1 = ctx.op("vector", lambda e, pb4=pb4, dst=dst: e.tensor_copy(
                dst, pb4[:, 0:64]), waits=[m4_])
            ctx.ptok[pi4] = o1
            pb5, pi5 = ctx.bank()
            m5_ = ctx.mm_chain(pb5[:, 0:64], [(qprev, Sold), (MpTv, Uv)],
                               waits=[u1, d["qTs"][st * 2],
                                      d["MT"][st * 2 + 1],
                                      lu.get(("S", st, ph), sz),
                                      ctx.ptok[pi5]])
            if st == 0:
                of = Ofp[:, b * 64:(b + 1) * 64]
                o2 = ctx.op("vector", lambda e, pb5=pb5, of=of: e.tensor_copy(
                    of, pb5[:, 0:64]), waits=[m5_, lu.get(("Ofp", b))])
                ctx.ptok[pi5] = o2
                o_toks["ofp"] = o2
            else:
                o_toks["osp_mt"] = m5_
                o_toks["osp_pb"] = pb5
                o_toks["osp_pi"] = pi5
            lu[("Snew", st, ph)] = m5_
            lu[("kT", b)] = m1_
            lu[("Ksc", b)] = m3_
            lu[("qTs", b)] = m5_
            lu[("MT", b)] = m5_
            lu[("U16", b, st)] = m5_
            lu[("PTb", b, st)] = m2_
        b_ = b
        ebv = errb[:, b_ * 64:(b_ + 1) * 64]
        pb5 = o_toks["osp_pb"]
        e1_ = ctx.op("vector", lambda e, ebv=ebv, pb5=pb5: e.tensor_add(
            ebv, Ofp[:, b_ * 64:(b_ + 1) * 64], pb5[:, 0:64]),
            waits=[o_toks["ofp"], o_toks["osp_mt"], lu.get(("errb", b_))])
        ctx.ptok[o_toks["osp_pi"]] = e1_
        lu[("Ofp", b_)] = e1_
        e2_ = ctx.op("vector", lambda e, ebv=ebv, c=c: e.scalar_tensor_tensor(
            ebv, ebv, -0.5, v_all[:, c * 64:(c + 1) * 64], OP.mult, OP.add),
            waits=[e1_, d["v"]])
        e3_ = ctx.op("scalar", lambda e, ebv=ebv: e.activation(
            ebv, ebv, AF.Square), waits=[e2_])
        smv = sm[:, b_ * 8:(b_ + 1) * 8]
        e4_ = ctx.op("vector", lambda e, ebv=ebv, smv=smv: e.reduce_sum(
            smv[:, 4:5], ebv, axis=AXX), waits=[e3_, lu.get(("sm45", b_))])
        lu[("errb", b_)] = e4_
        e5_ = ctx.op("scalar", lambda e, smv=smv: e.activation(
            smv[:, 5:6], smv[:, 4:5], AF.Sigmoid, scale=1.0 / (1.0 + 1e-6)),
            waits=[e4_])
        e6_ = ctx.op("vector", lambda e, smv=smv: e.tensor_single_scalar(
            smv[:, 6:7], smv[:, 5:6], 0.7, OP.is_gt), waits=[e5_])
        e7_ = ctx.op("vector", lambda e, smv=smv, c=c: e.tensor_scalar(
            u01_all[:, c:c + 1], smv[:, 6:7], 0.1, None, OP.mult),
            waits=[e6_])
        lu[("sm45", b_)] = e7_

    dprev = None
    for c in range(NCH):
        d = P0(c)
        if dprev is not None:
            SEQ(c - 1, dprev)
        dprev = d
    SEQ(NCH - 1, dprev)

    ns0 = mem("t_ns0", [1, 2 * 64], F32)
    nz = ctx.op("gpsimd", lambda e: e.memset(ns0[:], 0.0))
    nsc = mem("t_nsc", [128, 8], F32)
    cprow = mem("t_cprow", [1, 2 * 128], F32)
    PT16 = mem("t_PT16", [128, 128], BF16)
    Dm16 = mem("t_Dm16", [128, 64], BF16)
    NSf = mem("t_NSf", [128, 64], F32)
    ob1 = mem("t_ob1", [128, 64], F32)
    osq = mem("t_osq", [128, 64], F32)
    o16 = mem("t_o16", [128, 64], BF16)
    oT = mem("t_oT", [64, 128], BF16)
    xres = mem("t_xres", [128, 2 * 1024], F32)
    xres_cons = [None, None]
    prev = {}
    for c in range(NCH):
        ph = c % 2
        u01c = u01_all[:, c:c + 1]
        a1 = ctx.op("vector", lambda e, u01c=u01c: e.tensor_scalar(
            nsc[:, 0:1], u01c, -1.0, 1.0, OP.mult, OP.add),
            waits=[Tok(ctx.sems["vector"], ctx.sems["vector"].n)
                   if c == 0 else prev.get("a1w")])
        a2 = ctx.op("scalar", lambda e: e.activation(
            nsc[:, 1:2], nsc[:, 0:1], AF.Ln), waits=[a1])
        pb, pi = ctx.bank()
        a3 = ctx.mm_chain(pb[:, 0:1], [(triu0[:], nsc[:, 1:2])],
                          waits=[a2, ctx.ptok[pi], m4])
        a4 = ctx.op("scalar", lambda e, pb=pb: e.activation(
            nsc[:, 2:3], pb[:, 0:1], AF.Exp), waits=[a3])
        ctx.ptok[pi] = a4
        a5 = ctx.op("vector", lambda e: e.reciprocal(nsc[:, 3:4], nsc[:, 2:3]),
                    waits=[a4])
        a7 = None
        for jj in range(2):
            pb2, pi2 = ctx.bank()
            a6 = ctx.op("tensor", lambda e, pb2=pb2, jj=jj: e.transpose(
                pb2[0:1, 0:128], nsc[:, 2 + jj:3 + jj], idf[:]),
                waits=[a5, ctx.ptok[pi2]] + ctoks)
            a7 = ctx.op("vector", lambda e, pb2=pb2, jj=jj: e.tensor_copy(
                cprow[:, jj * 128:(jj + 1) * 128], pb2[0:1, 0:128]),
                waits=[a6, prev.get("cprow")])
            ctx.ptok[pi2] = a7
        pb3, pi3 = ctx.bank()
        a8 = ctx.op("tensor", lambda e, pb3=pb3: e.matmul(
            pb3[:, 0:128], cprow[:, 128:256], cprow[:, 0:128],
            start=True, stop=True), waits=[a7, ctx.ptok[pi3]])
        a9 = ctx.op("vector", lambda e, pb3=pb3: e.tensor_tensor(
            PT16[:], pb3[:, 0:128], triu0[:], OP.mult),
            waits=[a8, m4, prev.get("PT16")])
        ctx.ptok[pi3] = a9
        a10 = ctx.op("vector", lambda e, c=c, u01c=u01c: e.tensor_scalar(
            Dm16[:], nu_all[:, c * 64:(c + 1) * 64], u01c, None, OP.mult),
            waits=[a1, prev.get("Dm16")])
        pb4, pi4 = ctx.bank()
        ns_old = ns0[:, ph * 64:(ph + 1) * 64]
        ns_new = ns0[:, (ph ^ 1) * 64:((ph ^ 1) + 1) * 64]
        a11 = ctx.mm_chain(pb4[:, 0:64],
                           [(PT16[:], Dm16[:]), (cprow[:, 0:128], ns_old)],
                           waits=[a9, a10, nz, prev.get("ns_old"),
                                  ctx.ptok[pi4]])
        a12 = ctx.op("vector", lambda e, pb4=pb4: e.tensor_copy(
            NSf[:], pb4[:, 0:64]), waits=[a11, prev.get("NSf")])
        ctx.ptok[pi4] = a12
        pb5, pi5 = ctx.bank()
        a13 = ctx.op("tensor", lambda e, pb5=pb5: e.matmul(
            pb5[0:1, 0:64], e127[:], NSf[:], start=True, stop=True),
            waits=[a12, m9, ctx.ptok[pi5]])
        a14 = ctx.op("vector", lambda e, pb5=pb5, ns_new=ns_new: e.tensor_copy(
            ns_new, pb5[0:1, 0:64]), waits=[a13, prev.get("ns_new")])
        ctx.ptok[pi5] = a14
        prev["ns_old"] = a11
        prev["ns_new"] = a14
        prev["a1w"] = a2
        prev["cprow"] = a11
        prev["PT16"] = a11
        prev["Dm16"] = a11
        prev["NSf"] = a13
        p0c = pw_all[:, c * 3:c * 3 + 1]
        p1c = pw_all[:, c * 3 + 1:c * 3 + 2]
        p2c = pw_all[:, c * 3 + 2:c * 3 + 3]
        b1_ = ctx.op("vector", lambda e, c=c, p0c=p0c: e.tensor_scalar(
            ob1[:], Ocf[:, c * 64:(c + 1) * 64], p0c, None, OP.mult),
            waits=[a12, prev.get("ob1")])
        b2_ = ctx.op("vector", lambda e, c=c, p1c=p1c: e.scalar_tensor_tensor(
            ob1[:], Ocs[:, c * 64:(c + 1) * 64], p1c, ob1[:],
            OP.mult, OP.add), waits=[b1_])
        b3_ = ctx.op("vector", lambda e, p2c=p2c: e.scalar_tensor_tensor(
            ob1[:], NSf[:], p2c, ob1[:], OP.mult, OP.add), waits=[b2_, a12])
        b4_ = ctx.op("scalar", lambda e: e.activation(osq[:], ob1[:], AF.Square),
                     waits=[b3_, prev.get("osq")])
        b5_ = ctx.op("vector", lambda e: e.reduce_sum(nsc[:, 4:5], osq[:],
                                                      axis=AXX), waits=[b4_])
        prev["osq"] = b5_
        b6_ = ctx.op("vector", lambda e: e.tensor_scalar(
            nsc[:, 5:6], nsc[:, 4:5], 1.0 / 64.0, 1e-6, OP.mult, OP.add),
            waits=[b5_])
        b7_ = ctx.op("scalar", lambda e: e.activation(
            nsc[:, 6:7], nsc[:, 5:6], AF.Sqrt), waits=[b6_])
        b8_ = ctx.op("vector", lambda e: e.reciprocal(nsc[:, 7:8], nsc[:, 6:7]),
                     waits=[b7_])
        b9_ = ctx.op("vector", lambda e: e.tensor_scalar(
            ob1[:], ob1[:], nsc[:, 7:8], None, OP.mult), waits=[b8_])
        b10 = ctx.op("vector", lambda e: e.tensor_mul(o16[:], ob1[:], nwrep[:]),
                     waits=[b9_, nwtok, prev.get("o16")])
        prev["ob1"] = b10
        pb6, pi6 = ctx.bank()
        ptro = pb6[0:64, 0:64].bitcast(BF16)
        b11 = ctx.op("tensor", lambda e, ptro=ptro: e.transpose(
            ptro, o16[:], id16[:]), waits=[b10, id16_tok, ctx.ptok[pi6]])
        prev["o16"] = b11
        b12 = ctx.op("vector", lambda e, ptro=ptro: e.tensor_copy(oT[:], ptro),
                     waits=[b11, prev.get("oT")])
        ctx.ptok[pi6] = b12
        t0 = c * 128
        p_ = c % 2
        xrv = xres[:, p_ * 1024:(p_ + 1) * 1024]
        xl = ctx.dma("sync", xrv, x_ap[t0:t0 + 128, :], waits=[xres_cons[p_]])
        last_add = None
        for ms in range(2):
            pb7, pi7 = ctx.bank()
            b13 = ctx.mm_chain(pb7[:], [(oT[:], wo1[:, ms * 512:(ms + 1) * 512])],
                               waits=[b12, ctx.ptok[pi7]] + wtok)
            b14 = ctx.op("vector", lambda e, pb7=pb7, xrv=xrv, ms=ms:
                         e.tensor_add(xrv[:, ms * 512:(ms + 1) * 512],
                                      xrv[:, ms * 512:(ms + 1) * 512], pb7[:]),
                         waits=[b13, xl])
            ctx.ptok[pi7] = b14
            last_add = b14
        so = ctx.dma_st("sync", xout_ap[t0:t0 + 128, :], xrv, waits=[last_add])
        if lnf is not None:
            lt = emit_lnf_tile(ctx, com, "T", lnf[0], xrv, c, lnf[1],
                               [last_add])
            xres_cons[p_] = lt
        else:
            xres_cons[p_] = so
        prev["oT"] = b13
    return Tok(ctx.sems["sync"], ctx.sems["sync"].n)


def emit_l1_ffn(ctx, com, x_ap, xn16_ap, w, acc_ap, xout_ap, mem=None):
    if mem is None:
        mem = ctx.sbuf
    nc = ctx.nc
    TG = 256
    NGR = S // TG
    wr_sb = mem("m_wr", [128, 8 * 8], BF16)
    w1buf = mem("m_w1", [128, 8 * 4096], BF16)
    w2buf = mem("m_w2", [128, 32 * 1024], BF16)
    b1_sb = mem("m_b1", [128, 32], F32)
    b2_sb = mem("m_b2", [1, 1024], F32)
    xnT = mem("m_xnT", [128, 2 * 8 * TG], BF16)
    hbuf = mem("m_hbuf", [128, 32 * TG], BF16)
    h_all = mem("m_hall", [128, NT * 8], F32)
    pmax_all = mem("m_pmax", [128, NT], F32)
    gl = mem("m_gl", [128, 8], F32)
    sc2 = mem("m_sc2", [128, 8], F32)
    ybuf = mem("m_ybuf", [128, 2 * 512], F32)
    xres = mem("m_xres", [128, 1024], F32)
    accb = mem("m_accb", [128, 1024], F32)

    for k in range(8):
        ctx.dma("gpsimd", wr_sb[:, k * 8:(k + 1) * 8],
                w["Wr"][k * 128:(k + 1) * 128, :])
    wtok = [Tok(ctx.sems["gpsimd"], ctx.sems["gpsimd"].n)]

    xcons = [None]
    gl_cons = [None]
    xnT0 = xnT[:, 0:8 * TG]
    for g in range(NGR):
        xtoks = load_xnT(ctx, xnT0, xn16_ap, g, waits=[xcons[0]], tg=TG)
        for ti in range(TG // 128):
            tile = g * (TG // 128) + ti
            xsl = lambda k, ti=ti: xnT0[:, k * TG + ti * 128: k * TG + (ti + 1) * 128]
            pb, pi = ctx.bank()
            pairs = [(xsl(k), wr_sb[:, k * 8:(k + 1) * 8]) for k in range(8)]
            mt = ctx.mm_chain(pb[:, 0:8], pairs,
                              waits=xtoks + wtok + [ctx.ptok[pi]])
            e1 = ctx.op("vector", lambda e, pb=pb: e.tensor_copy(gl[:], pb[:, 0:8]),
                        waits=[mt, gl_cons[0]])
            ctx.ptok[pi] = e1
            hv = h_all[:, tile * 8:(tile + 1) * 8]
            hm = emit_first_match(ctx, f"m{tile}", gl[:], hv, sc2[:], waits=[e1])
            s1 = ctx.op("vector", lambda e: e.reduce_max(sc2[:, 3:4], gl[:],
                                                         axis=AXX), waits=[e1])
            s2 = ctx.op("vector", lambda e: e.tensor_scalar(
                sc2[:, 4:5], sc2[:, 3:4], -1.0, None, OP.mult), waits=[s1])
            s3 = ctx.op("scalar", lambda e: e.activation(
                sc2[:, 5:6], gl[:, 0:1], AF.Exp, bias=sc2[:, 4:5]),
                waits=[s2])
            s4 = ctx.op("scalar", lambda e: e.activation(
                gl[:], gl[:], AF.Exp, bias=sc2[:, 4:5]), waits=[s2, hm])
            s5 = ctx.op("vector", lambda e: e.reduce_sum(sc2[:, 6:7], gl[:],
                                                         axis=AXX), waits=[s4])
            s6 = ctx.op("vector", lambda e, tile=tile: e.reciprocal(
                pmax_all[:, tile:tile + 1], sc2[:, 6:7]), waits=[s5])
            gl_cons[0] = s4
        xcons[0] = Tok(ctx.sems["tensor"], ctx.sems["tensor"].n)

    w1_cons = [None]
    w2_cons = [None]
    hb_cons = [None]
    yb_cons = [None, None]
    acc_view = acc_ap
    rt_snap = Tok(ctx.sems["tensor"], ctx.sems["tensor"].n)
    xslot_toks = [None, None]
    xslot_cons = [rt_snap, rt_snap]
    NITER = 8 * NGR


    def xview(slot):
        return xnT[:, slot * 8 * TG:(slot + 1) * 8 * TG]

    def xprefetch(t):
        if t >= NITER:
            return
        slot = t % 2
        g = t % NGR
        xslot_toks[slot] = load_xnT(ctx, xview(slot), xn16_ap, g,
                                    waits=[xslot_cons[slot]], tg=TG)
    xprefetch(0)
    for e_ in range(8):
        w1t = []
        for k in range(8):
            w1t.append(ctx.dma("gpsimd", w1buf[:, k * 4096:(k + 1) * 4096],
                               w["W1"][e_, k * 128:(k + 1) * 128, :],
                               waits=[w1_cons[0]]))
        w2t = []
        for c in range(32):
            w2t.append(ctx.dma("gpsimd", w2buf[:, c * 1024:(c + 1) * 1024],
                               w["W2"][e_, c * 128:(c + 1) * 128, :],
                               waits=[w2_cons[0]], sem_name="w2s"))
        bt1 = ctx.dma("sync", b1_sb[:], w["b1"][e_, :, :],
                      waits=[w1_cons[0]])
        bt2 = ctx.dma("sync", b2_sb[:], w["b2"][e_:e_ + 1, :],
                      waits=[w2_cons[0]])
        wtoks = [Tok(ctx.sems["gpsimd"], ctx.sems["gpsimd"].n), bt1,
                 com.init_tok]
        w2toks = [Tok(ctx.sems["w2s"], ctx.sems["w2s"].n), bt2]
        for g in range(NGR):
            t_ = e_ * NGR + g
            slot = t_ % 2
            xtoks = xslot_toks[slot]
            xv = xview(slot)
            up_toks = []
            for fc in range(32):
                pb, pi = ctx.bank()
                pairs = [(w1buf[:, k * 4096 + fc * 128: k * 4096 + (fc + 1) * 128],
                          xv[:, k * TG:(k + 1) * TG]) for k in range(8)]
                mt = ctx.mm_chain(pb[:, 0:TG], pairs,
                                  waits=xtoks + wtoks + [ctx.ptok[pi],
                                                         hb_cons[0]])
                r1 = ctx.op("scalar", lambda e, pb=pb, fc=fc: e.activation(
                    hbuf[:, fc * TG:(fc + 1) * TG], pb[:, 0:TG], AF.Gelu,
                    bias=b1_sb[:, fc:fc + 1]),
                    waits=[mt])
                ctx.ptok[pi] = r1
                up_toks.append(r1)
            xslot_cons[slot] = Tok(ctx.sems["tensor"], ctx.sems["tensor"].n)
            if g == NGR - 1:
                w1_cons[0] = Tok(ctx.sems["scalar"], ctx.sems["scalar"].n)
            xprefetch(t_ + 1)
            for ti in range(TG // 128):
                tile = g * (TG // 128) + ti
                t0 = tile * 128
                hcol = h_all[:, tile * 8 + e_: tile * 8 + e_ + 1]
                for ms in range(2):
                    pb, pi = ctx.bank()
                    pairs = [(hbuf[:, c * TG + ti * 128: c * TG + (ti + 1) * 128],
                              w2buf[:, c * 1024 + ms * 512:
                                    c * 1024 + (ms + 1) * 512])
                             for c in range(32)]
                    pairs.append((com.ones_row[:, 0:128],
                                  b2_sb[:, ms * 512:(ms + 1) * 512]))
                    mt = ctx.mm_chain(pb[:], pairs,
                                      waits=up_toks + wtoks + w2toks
                                      + [ctx.ptok[pi]])
                    p = ms
                    ybv = ybuf[:, p * 512:(p + 1) * 512]
                    y1 = ctx.op("vector", lambda e, pb=pb, ybv=ybv, hcol=hcol:
                                e.tensor_scalar(ybv, pb[:], hcol, None,
                                                OP.mult),
                                waits=[mt, yb_cons[p]])
                    ctx.ptok[pi] = y1
                    dst = acc_view[t0:t0 + 128, ms * 512:(ms + 1) * 512]
                    if e_ == 0:
                        d1 = ctx.dma("gpsimd", dst, ybv, waits=[y1],
                                     sem_name="acc")
                    else:
                        d1 = ctx.op("gpsimd", lambda e, dst=dst, ybv=ybv:
                                    e.dma_start(out=dst, in_=ybv,
                                                accum_op=OP.add),
                                    waits=[y1], amt=16, sem_name="acc")
                    yb_cons[p] = d1
            hb_cons[0] = Tok(ctx.sems["tensor"], ctx.sems["tensor"].n)
        w2_cons[0] = hb_cons[0]
    acc_done = Tok(ctx.sems["acc"], ctx.sems["acc"].n)

    fin_cons = [None]
    for tile in range(NT):
        t0 = tile * 128
        xl = ctx.dma("sync", xres[:], x_ap[t0:t0 + 128, :],
                     waits=[fin_cons[0]])
        al = ctx.dma("sync", accb[:], acc_view[t0:t0 + 128, :],
                     waits=[acc_done, fin_cons[0]])
        f1 = ctx.op("vector", lambda e, tile=tile: e.scalar_tensor_tensor(
            xres[:], accb[:], pmax_all[:, tile:tile + 1], xres[:],
            OP.mult, OP.add), waits=[xl, al])
        so = ctx.dma_st("sync", xout_ap[t0:t0 + 128, :], xres[:], waits=[f1])
        fin_cons[0] = so
    return Tok(ctx.sems["sync"], ctx.sems["sync"].n)


def emit_l1_ffn_g(ctx, com, x_pad, xnT4_ap, xtok_pad, w, tid_d, pm_tok,
                  y_pad, y_out, mem=None):
    if mem is None:
        mem = ctx.sbuf
    nc = ctx.nc
    Ce = 1024
    TG = 512
    triu0, ctk1 = com.const("triu0")
    e127, ctk2 = com.const("e127")
    iota, ctk3 = com.const("iota")
    id16, id16_tok = com.ident16()

    wr_sb = mem("m_wr", [128, 8 * 8], BF16)
    w1buf = mem("m_w1", [128, 8 * 4096], BF16)
    w2buf = mem("m_w2", [128, 32 * 1024], BF16)
    b1_sb = mem("m_b1", [128, 32], F32)
    b2_sb = mem("m_b2", [1, 1024], F32)
    xnT = mem("m_xnT", [128, 8 * 256], BF16)
    xgT = mem("m_xgT", [128, 8 * TG], BF16)
    hbuf = mem("m_hbuf", [128, 32 * TG], BF16)
    h_all = mem("m_hall", [128, NT * 8], F32)
    gl = mem("m_gl", [128, 8], F32)
    sc2 = mem("m_sc2", [128, 16], F32)
    carry = mem("m_carry", [1, 8], F32)
    i8rep = mem("m_i8rep", [128, 8], F32)
    stile = mem("m_stile", [128, 65], I32)
    zrow16 = mem("m_zrow16", [1, 1024], BF16)
    idxc = mem("m_idxc", [128, 8], I32)
    xg = mem("m_xg", [128, 2 * 1024], BF16)
    yb = mem("m_yb", [128, 2 * 1024], F32)
    pmcol = mem("m_pmcol", [128, 2], F32)
    xrs = mem("m_xrs", [128, 2 * 1024], F32)
    ypd = mem("m_ypd", [128, 2 * 1024], F32)

    for k in range(8):
        ctx.dma("gpsimd", wr_sb[:, k * 8:(k + 1) * 8],
                w["Wr"][k * 128:(k + 1) * 128, :])
    wtok0 = [Tok(ctx.sems["gpsimd"], ctx.sems["gpsimd"].n)]

    sf1 = ctx.op("gpsimd", lambda e: e.memset(stile[:], 4096))
    sf2 = ctx.dma_st("sync", tid_d.rearrange("(a b) one -> a (b one)", a=128),
                     stile[:], waits=[sf1])
    zf1 = ctx.op("gpsimd", lambda e: e.memset(zrow16[:], 0.0))
    zf2 = ctx.dma_st("sync", xtok_pad[4096:4097, :], zrow16[:], waits=[zf1])
    cz = ctx.op("gpsimd", lambda e: e.memset(carry[:], 0.0))
    gi_cons = [None]

    idf, idf_tok = com.const("ident128")
    pb, pi = ctx.bank()
    t1 = ctx.op("tensor", lambda e, pb=pb: e.transpose(
        pb[0:1, 0:8], iota[0:8, 0:1], idf[0:8, 0:8]),
        waits=[ctk3, ctx.ptok[pi], idf_tok])
    i8row = mem("m_i8row", [1, 8], F32)
    t2 = ctx.op("vector", lambda e, pb=pb: e.tensor_copy(i8row[:], pb[0:1, 0:8]),
                waits=[t1])
    ctx.ptok[pi] = t2
    pb, pi = ctx.bank()
    t3 = ctx.op("tensor", lambda e, pb=pb: e.matmul(
        pb[:, 0:8], com.ones_row[:, 0:128], i8row[:], start=True, stop=True),
        waits=[t2, com.init_tok, ctx.ptok[pi]])
    t4 = ctx.op("vector", lambda e, pb=pb: e.tensor_copy(i8rep[:], pb[:, 0:8]),
                waits=[t3])
    ctx.ptok[pi] = t4

    xcons = [None]
    gl_cons = [None]
    pm_cons = [None]
    carry_mm = [cz]
    for g in range(16):
        xtoks = load_xnT(ctx, xnT, xnT4_ap, g, waits=[xcons[0]], tg=256)
        for ti in range(2):
            tile = g * 2 + ti
            xsl = lambda k, ti=ti: xnT[:, k * 256 + ti * 128: k * 256 + (ti + 1) * 128]
            pb, pi = ctx.bank()
            pairs = [(xsl(k), wr_sb[:, k * 8:(k + 1) * 8]) for k in range(8)]
            mt = ctx.mm_chain(pb[:, 0:8], pairs,
                              waits=xtoks + wtok0 + [ctx.ptok[pi]])
            e1 = ctx.op("vector", lambda e, pb=pb: e.tensor_copy(gl[:], pb[:, 0:8]),
                        waits=[mt, gl_cons[0]])
            ctx.ptok[pi] = e1
            hv = h_all[:, tile * 8:(tile + 1) * 8]
            hm = emit_first_match(ctx, f"m{tile}", gl[:], hv, sc2[:, 0:3],
                                  waits=[e1])
            s1 = ctx.op("vector", lambda e: e.reduce_max(sc2[:, 3:4], gl[:],
                                                         axis=AXX), waits=[e1])
            s2 = ctx.op("vector", lambda e: e.tensor_scalar(
                sc2[:, 4:5], sc2[:, 3:4], -1.0, None, OP.mult), waits=[s1])
            s4 = ctx.op("scalar", lambda e: e.activation(
                gl[:], gl[:], AF.Exp, bias=sc2[:, 4:5]), waits=[s2, hm])
            s5 = ctx.op("vector", lambda e: e.reduce_sum(sc2[:, 5:6], gl[:],
                                                         axis=AXX), waits=[s4])
            s6 = ctx.op("vector", lambda e: e.reciprocal(
                sc2[:, 6:7], sc2[:, 5:6]), waits=[s5, pm_cons[0]])
            pm_cons[0] = ctx.dma_st("sync", pm_tok[tile * 128:(tile + 1) * 128, :],
                                    sc2[:, 6:7], waits=[s6])
            gl_cons[0] = s4
            pbA, piA = ctx.bank()
            mA = ctx.op("tensor", lambda e, pbA=pbA, hv=hv: e.matmul(
                pbA[:, 0:8], triu0[:], hv, start=True, stop=True),
                waits=[hm, ctk1, ctx.ptok[piA]])
            pbB, piB = ctx.bank()
            mB = ctx.op("tensor", lambda e, pbB=pbB, hv=hv: e.matmul(
                pbB[0:1, 0:8], com.ones_col[:], hv, start=True, stop=True),
                waits=[hm, com.init_tok, ctx.ptok[piB]])
            pbC, piC = ctx.bank()
            mC = ctx.op("tensor", lambda e, pbC=pbC: e.matmul(
                pbC[:, 0:8], com.ones_row[:, 0:128], carry[:],
                start=True, stop=True),
                waits=[carry_mm[0], com.init_tok, ctx.ptok[piC]])
            c1 = ctx.op("vector", lambda e, pbA=pbA, hv=hv: e.tensor_tensor(
                sc2[:, 8:16], hv, pbA[:, 0:8], OP.mult), waits=[mA])
            ctx.ptok[piA] = c1
            c2 = ctx.op("vector", lambda e: e.reduce_sum(
                sc2[:, 3:4], sc2[:, 8:16], axis=AXX), waits=[c1])
            c3 = ctx.op("vector", lambda e, pbC=pbC, hv=hv: e.tensor_tensor(
                sc2[:, 8:16], hv, pbC[:, 0:8], OP.mult), waits=[mC, c2])
            ctx.ptok[piC] = c3
            c4 = ctx.op("vector", lambda e: e.reduce_sum(
                sc2[:, 4:5], sc2[:, 8:16], axis=AXX), waits=[c3, s4])
            c5 = ctx.op("vector", lambda e, hv=hv: e.tensor_tensor(
                sc2[:, 8:16], hv, i8rep[:], OP.mult), waits=[t4, c4, hm])
            c6 = ctx.op("vector", lambda e: e.reduce_sum(
                sc2[:, 5:6], sc2[:, 8:16], axis=AXX), waits=[c5])
            c7 = ctx.op("vector", lambda e: e.tensor_add(
                sc2[:, 7:8], sc2[:, 3:4], sc2[:, 4:5]), waits=[c4, c2])
            c8 = ctx.op("vector", lambda e: e.tensor_scalar(
                sc2[:, 7:8], sc2[:, 7:8], -1.0, None, OP.add), waits=[c7])
            c9 = ctx.op("vector", lambda e: e.scalar_tensor_tensor(
                sc2[:, 3:4], sc2[:, 5:6], float(Ce), sc2[:, 7:8],
                OP.mult, OP.add), waits=[c8, c6])
            c10 = ctx.op("vector", lambda e: e.tensor_single_scalar(
                sc2[:, 4:5], sc2[:, 7:8], Ce - 0.5, OP.is_ge), waits=[c9])
            c11 = ctx.op("vector", lambda e: e.scalar_tensor_tensor(
                sc2[:, 3:4], sc2[:, 4:5], 1e6, sc2[:, 3:4],
                OP.mult, OP.add), waits=[c10])
            c12 = ctx.op("vector", lambda e: e.tensor_single_scalar(
                sc2[:, 3:4], sc2[:, 3:4], float(8 * Ce), OP.min), waits=[c11])
            c13 = ctx.op("vector", lambda e, tile=tile: e.tensor_scalar(
                sc2[:, 4:5], iota[:], float(tile * 128), None, OP.add),
                waits=[ctk3, c11])
            c14 = ctx.op("vector", lambda e: e.tensor_copy(
                idxc[:, 0:1], sc2[:, 3:4]), waits=[c12, gi_cons[0]])
            c15 = ctx.op("vector", lambda e: e.tensor_copy(
                idxc[:, 1:2], sc2[:, 4:5]), waits=[c13])
            g1 = ctx.op("gpsimd", lambda e: e.indirect_dma_start(
                tid_d[:], IndirectOffsetOnAxis(idxc[:, 0:1], 0),
                idxc[:, 1:2], None), waits=[c14, c15, sf2], amt=16,
                sem_name="acc")
            gi_cons[0] = g1
            u1 = ctx.op("vector", lambda e, pbB=pbB: e.tensor_add(
                carry[:], carry[:], pbB[0:1, 0:8]), waits=[mB, mC])
            ctx.ptok[piB] = u1
            carry_mm[0] = u1
        xcons[0] = Tok(ctx.sems["tensor"], ctx.sems["tensor"].n)

    w1_cons = [None]
    w2_cons = [None]
    hb_cons = [Tok(ctx.sems["tensor"], ctx.sems["tensor"].n)]
    idx_cons = [None] * 4
    xg_cons = [None, None]
    yb_cons = [None, None]
    gth_cons = [Tok(ctx.sems["acc"], ctx.sems["acc"].n)]
    for e_ in range(8):
        w1t = []
        for k in range(8):
            w1t.append(ctx.dma("gpsimd", w1buf[:, k * 4096:(k + 1) * 4096],
                               w["W1"][e_, k * 128:(k + 1) * 128, :],
                               waits=[w1_cons[0]]))
        w2t = []
        for c in range(32):
            w2t.append(ctx.dma("gpsimd", w2buf[:, c * 1024:(c + 1) * 1024],
                               w["W2"][e_, c * 128:(c + 1) * 128, :],
                               waits=[w2_cons[0]], sem_name="w2s"))
        bt1 = ctx.dma("sync", b1_sb[:], w["b1"][e_, :, :], waits=[w1_cons[0]])
        bt2 = ctx.dma("sync", b2_sb[:], w["b2"][e_:e_ + 1, :], waits=[w2_cons[0]])
        wtoks = [Tok(ctx.sems["gpsimd"], ctx.sems["gpsimd"].n), bt1,
                 com.init_tok]
        w2toks = [Tok(ctx.sems["w2s"], ctx.sems["w2s"].n), bt2]
        for g in range(Ce // TG):
            stoks = []
            for s_ in range(4):
                base = e_ * Ce + g * TG + s_ * 128
                sl2 = s_ % 2
                ic = idxc[:, 4 + s_:5 + s_]
                il = ctx.dma("sync", ic,
                             tid_d[base:base + 128, :], waits=[idx_cons[s_],
                                                              gth_cons[0]])
                gx = ctx.op("gpsimd", lambda e, sl2=sl2, ic=ic: e.indirect_dma_start(
                    xg[:, sl2 * 1024:(sl2 + 1) * 1024], None, xtok_pad[:],
                    IndirectOffsetOnAxis(ic, 0)),
                    waits=[il, zf2, xg_cons[sl2]], amt=16, sem_name="acc")
                cps = []
                for k in range(8):
                    pb, pi = ctx.bank()
                    ptr = pb[:, 0:64].bitcast(BF16)
                    tr = ctx.op("tensor", lambda e, ptr=ptr, sl2=sl2, k=k:
                                e.transpose(ptr, xg[:, sl2 * 1024 + k * 128:
                                                    sl2 * 1024 + (k + 1) * 128],
                                            id16[:]),
                                waits=[gx, id16_tok, ctx.ptok[pi]])
                    cp = ctx.op("vector", lambda e, ptr=ptr, s_=s_, k=k:
                                e.tensor_copy(
                                    xgT[:, k * TG + s_ * 128: k * TG + (s_ + 1) * 128],
                                    ptr),
                                waits=[tr, hb_cons[0]])
                    ctx.ptok[pi] = cp
                    cps.append(cp)
                xg_cons[sl2] = cps[-1]
                stoks.append(cps[-1])
            up_toks = []
            for fc in range(32):
                pb, pi = ctx.bank()
                pairs = [(w1buf[:, k * 4096 + fc * 128: k * 4096 + (fc + 1) * 128],
                          xgT[:, k * TG:(k + 1) * TG]) for k in range(8)]
                mt = ctx.mm_chain(pb[:], pairs,
                                  waits=stoks + wtoks + w1t + [ctx.ptok[pi],
                                                               hb_cons[0]])
                r1 = ctx.op("scalar", lambda e, pb=pb, fc=fc: e.activation(
                    hbuf[:, fc * TG:(fc + 1) * TG], pb[:], AF.Gelu,
                    bias=b1_sb[:, fc:fc + 1]), waits=[mt])
                ctx.ptok[pi] = r1
                up_toks.append(r1)
            if e_ < 7 and g == Ce // TG - 1:
                w1_cons[0] = Tok(ctx.sems["tensor"], ctx.sems["tensor"].n)
            for s_ in range(4):
                sl2 = s_ % 2
                last = None
                for ms in range(2):
                    pb, pi = ctx.bank()
                    pairs = [(hbuf[:, c * TG + s_ * 128: c * TG + (s_ + 1) * 128],
                              w2buf[:, c * 1024 + ms * 512:
                                    c * 1024 + (ms + 1) * 512])
                             for c in range(32)]
                    pairs.append((com.ones_row[:, 0:128],
                                  b2_sb[:, ms * 512:(ms + 1) * 512]))
                    mt = ctx.mm_chain(pb[:], pairs,
                                      waits=up_toks + wtoks + w2toks
                                      + [ctx.ptok[pi]])
                    y1 = ctx.op("vector", lambda e, pb=pb, sl2=sl2, ms=ms:
                                e.tensor_copy(
                                    yb[:, sl2 * 1024 + ms * 512:
                                       sl2 * 1024 + (ms + 1) * 512], pb[:]),
                                waits=[mt, yb_cons[sl2]])
                    ctx.ptok[pi] = y1
                    last = y1
                sc_ = ctx.op("gpsimd", lambda e, s_=s_, sl2=sl2: e.indirect_dma_start(
                    y_pad[:], IndirectOffsetOnAxis(idxc[:, 4 + s_:5 + s_], 0),
                    yb[:, sl2 * 1024:(sl2 + 1) * 1024], None),
                    waits=[last], amt=16, sem_name="acc")
                yb_cons[sl2] = sc_
                idx_cons[s_] = sc_
            hb_cons[0] = Tok(ctx.sems["tensor"], ctx.sems["tensor"].n)
        w2_cons[0] = hb_cons[0]
        if e_ == 7:
            w1_cons[0] = hb_cons[0]
    scat_done = Tok(ctx.sems["acc"], ctx.sems["acc"].n)
    wend = Tok(ctx.sems["tensor"], ctx.sems["tensor"].n)
    fin_cons = [wend, wend]
    for tile in range(NT):
        t0 = tile * 128
        p = tile % 2
        xl = ctx.dma("sync", xrs[:, p * 1024:(p + 1) * 1024],
                     x_pad[t0:t0 + 128, :], waits=[fin_cons[p]])
        yl = ctx.dma("sync", ypd[:, p * 1024:(p + 1) * 1024],
                     y_pad[t0:t0 + 128, :], waits=[scat_done, fin_cons[p]])
        pml = ctx.dma("sync", pmcol[:, p:p + 1],
                      pm_tok[t0:t0 + 128, :], waits=[fin_cons[p]])
        f1 = ctx.op("vector", lambda e, p=p: e.scalar_tensor_tensor(
            xrs[:, p * 1024:(p + 1) * 1024],
            ypd[:, p * 1024:(p + 1) * 1024], pmcol[:, p:p + 1],
            xrs[:, p * 1024:(p + 1) * 1024], OP.mult, OP.add),
            waits=[xl, yl, pml])
        so = ctx.dma_st("sync", y_out[t0:t0 + 128, :],
                        xrs[:, p * 1024:(p + 1) * 1024], waits=[f1])
        fin_cons[p] = so
    return Tok(ctx.sems["st"], ctx.sems["st"].n)


def prep_weights(inp):
    g = {}
    f32c = lambda a: np.ascontiguousarray(np.asarray(a, np.float32))
    g["Wq"] = f32c(inp["L0_attn_Wq"])
    g["Wk"] = f32c(inp["L0_attn_Wk"])
    g["Wv"] = f32c(inp["L0_attn_Wv"])
    fo = np.arange(4096)
    qpar = fo // 2048
    i2 = (fo // 64) % 32
    j = fo % 64
    ref_row = (2 * i2 + qpar) * 64 + j
    g["WoP"] = f32c(np.asarray(inp["L0_attn_Wo"], np.float32)[ref_row, :])
    g["bo"] = f32c(inp["L0_attn_bo"]).reshape(1, 1024)
    g["gWT"] = f32c(np.asarray(inp["L0_ffn_gate_W"], np.float32).T)
    g["gate_b"] = f32c(inp["L0_ffn_gate_b"]).reshape(1, 8)
    g["wiT"] = f32c(np.asarray(inp["L0_ffn_wi_W"], np.float32).T)
    g["wi_b"] = f32c(np.asarray(inp["L0_ffn_wi_b"], np.float32)
                     .reshape(32, 128).T)
    g["A_lay"] = f32c(np.asarray(inp["L0_ffn_lora_A"], np.float32)
                      .reshape(32, 1024).T)
    g["B_flat"] = f32c(np.transpose(np.asarray(inp["L0_ffn_lora_B"], np.float32),
                                    (0, 2, 1)).reshape(32, 4096))
    g["woT"] = f32c(np.asarray(inp["L0_ffn_wo_W"], np.float32).T)
    g["wo_b"] = f32c(inp["L0_ffn_wo_b"]).reshape(1, 1024)
    g["WkT1"] = f32c(np.asarray(inp["L1_attn_Wk"], np.float32).T)
    g["WvT1"] = f32c(np.asarray(inp["L1_attn_Wv"], np.float32).T)
    g["WqT1"] = f32c(np.asarray(inp["L1_attn_Wq"], np.float32).T)
    g["fa"] = f32c(inp["L1_attn_fast_alpha"]).reshape(1, 1)
    g["sa"] = f32c(inp["L1_attn_slow_alpha"]).reshape(1, 1)
    g["nm1T"] = f32c(np.asarray(inp["L1_attn_nm_w1"], np.float32).T)
    g["nm_b1"] = f32c(inp["L1_attn_nm_b1"]).reshape(1, 128)
    g["nm2T"] = f32c(np.asarray(inp["L1_attn_nm_w2"], np.float32).T)
    g["nm_b2"] = f32c(inp["L1_attn_nm_b2"]).reshape(1, 64)
    g["ir1T"] = f32c(np.asarray(inp["L1_attn_ir_w1"], np.float32).T)
    g["ir_b1"] = f32c(inp["L1_attn_ir_b1"]).reshape(1, 64)
    g["ir_w2"] = f32c(inp["L1_attn_ir_w2"]).reshape(1, 64)
    g["ir_b2"] = f32c(inp["L1_attn_ir_b2"]).reshape(1, 1)
    g["pwT"] = f32c(np.asarray(inp["L1_attn_pw"], np.float32).T)
    g["WoT1"] = f32c(np.asarray(inp["L1_attn_Wo"], np.float32).T)
    g["norm_w"] = f32c(inp["L1_attn_norm_w"]).reshape(1, 64)
    g["Wr"] = f32c(inp["L1_ffn_Wr"])
    g["W1"] = f32c(inp["L1_ffn_W1"])
    g["b1"] = f32c(np.transpose(np.asarray(inp["L1_ffn_b1"], np.float32)
                                .reshape(8, 32, 128), (0, 2, 1)))
    g["W2"] = f32c(inp["L1_ffn_W2"])
    g["b2"] = f32c(inp["L1_ffn_b2"])
    return g


WSHAPES = {
    "Wq": [1024, 4096], "Wk": [1024, 4096], "Wv": [1024, 4096],
    "WoP": [4096, 1024], "bo": [1, 1024],
    "gWT": [1024, 8], "gate_b": [1, 8], "wiT": [1024, 4096],
    "wi_b": [128, 32], "A_lay": [1024, 32], "B_flat": [32, 4096],
    "woT": [4096, 1024], "wo_b": [1, 1024],
    "WkT1": [1024, 64], "WvT1": [1024, 64], "WqT1": [1024, 64],
    "fa": [1, 1], "sa": [1, 1], "nm1T": [64, 128], "nm_b1": [1, 128],
    "nm2T": [128, 64], "nm_b2": [1, 64], "ir1T": [1024, 64],
    "ir_b1": [1, 64], "ir_w2": [1, 64], "ir_b2": [1, 1],
    "pwT": [1024, 3], "WoT1": [64, 1024], "norm_w": [1, 64],
    "Wr": [1024, 8], "W1": [8, 1024, 4096], "b1": [8, 128, 32],
    "W2": [8, 4096, 1024], "b2": [8, 1024],
}

BLOCK_W = {
    "l0a": ["Wq", "Wk", "Wv", "WoP", "bo"],
    "l0f": ["gWT", "gate_b", "wiT", "wi_b", "A_lay", "B_flat", "woT", "wo_b"],
    "l1a": ["WkT1", "WvT1", "WqT1", "fa", "sa", "nm1T", "nm_b1", "nm2T",
            "nm_b2", "ir1T", "ir_b1", "ir_w2", "ir_b2", "pwT", "WoT1",
            "norm_w"],
    "l1f": ["Wr", "W1", "b1", "W2", "b2"],
}


def decl_weights(nc, names):
    return {n: nc.dram_tensor("w_" + n, WSHAPES[n], F32,
                              kind="ExternalInput").ap() for n in names}


def build_l0a_nc(dbg=False):
    nc = bass.Bass("TRN2", target_bir_lowering=False)
    x = nc.dram_tensor("x", [S, D], F32, kind="ExternalInput").ap()
    y = nc.dram_tensor("y", [S, D], F32, kind="ExternalOutput").ap()
    caps = decl_consts(nc)
    w = decl_weights(nc, BLOCK_W["l0a"])
    kd = "ExternalOutput" if dbg else "Internal"
    xn16 = nc.dram_tensor("xnT4", [8, 128, S], BF16, kind="Internal").ap()
    sc = {
        "q4": nc.dram_tensor("q4", [S, 4096], BF16, kind=kd).ap(),
        "v4d": nc.dram_tensor("v4d", [S, 8192], BF16, kind=kd).ap(),
        "of4": nc.dram_tensor("of4", [S, 4096], BF16, kind=kd).ap(),
    }
    es = contextlib.ExitStack()
    with es:
        ctx = Ctx(nc, es)
        com = Common(ctx, caps)
        emit_ln(ctx, com, x, xn16, "ln0")
        emit_l0_attn(ctx, com, x, xn16, w, sc, y)
        nc_dbg = dbg
        fin = Tok(ctx.sems["sync"], ctx.sems["sync"].n)
        ctx.op("sync", lambda e: e.nop(), waits=[fin])
        block = es.enter_context(nc.Block())
        ctx.finalize(block)
    return nc


def build_l0f_nc(dbg=False):
    nc = bass.Bass("TRN2", target_bir_lowering=False)
    x = nc.dram_tensor("x", [S, D], F32, kind="ExternalInput").ap()
    y = nc.dram_tensor("y", [S, D], F32, kind="ExternalOutput").ap()
    caps = decl_consts(nc)
    w = decl_weights(nc, BLOCK_W["l0f"])
    xn16 = nc.dram_tensor("xnT4", [8, 128, S], BF16, kind="Internal").ap()
    es = contextlib.ExitStack()
    with es:
        ctx = Ctx(nc, es)
        com = Common(ctx, caps)
        emit_ln(ctx, com, x, xn16, "ln1")
        emit_l0_ffn(ctx, com, x, xn16, w, y)
        ctx.op("sync", lambda e: e.nop(),
               waits=[Tok(ctx.sems["st"], ctx.sems["st"].n)])
        block = es.enter_context(nc.Block())
        ctx.finalize(block)
    return nc


def build_l1a_nc(dbg=False):
    nc = bass.Bass("TRN2", target_bir_lowering=False)
    x = nc.dram_tensor("x", [S, D], F32, kind="ExternalInput").ap()
    y = nc.dram_tensor("y", [S, D], F32, kind="ExternalOutput").ap()
    caps = decl_consts(nc)
    w = decl_weights(nc, BLOCK_W["l1a"])
    xn16 = nc.dram_tensor("xnT4", [8, 128, S], BF16, kind="Internal").ap()
    es = contextlib.ExitStack()
    with es:
        ctx = Ctx(nc, es)
        com = Common(ctx, caps)
        emit_ln(ctx, com, x, xn16, "ln2")
        emit_l1_attn(ctx, com, x, xn16, w, y)
        ctx.op("sync", lambda e: e.nop(),
               waits=[Tok(ctx.sems["st"], ctx.sems["st"].n)])
        block = es.enter_context(nc.Block())
        ctx.finalize(block)
    return nc


def build_l1f_nc(dbg=False):
    nc = bass.Bass("TRN2", target_bir_lowering=False)
    x = nc.dram_tensor("x", [S, D], F32, kind="ExternalInput").ap()
    y = nc.dram_tensor("y", [S, D], F32, kind="ExternalOutput").ap()
    caps = decl_consts(nc)
    w = decl_weights(nc, BLOCK_W["l1f"])
    xn16 = nc.dram_tensor("xnT4", [8, 128, S], BF16, kind="Internal").ap()
    acc = nc.dram_tensor("acc", [S, D], F32, kind="Internal").ap()
    es = contextlib.ExitStack()
    with es:
        ctx = Ctx(nc, es)
        com = Common(ctx, caps)
        emit_ln(ctx, com, x, xn16, "ln3")
        emit_l1_ffn(ctx, com, x, xn16, w, acc, y)
        ctx.op("sync", lambda e: e.nop(),
               waits=[Tok(ctx.sems["st"], ctx.sems["st"].n),
                      Tok(ctx.sems["acc"], ctx.sems["acc"].n)])
        block = es.enter_context(nc.Block())
        ctx.finalize(block)
    return nc


class Carve:

    def __init__(self, region):
        self.r = region
        self.off = 0
        self.cap = region.shape[1]

    def __call__(self, name, shape, dtype):
        n = 1
        for s_ in shape[1:]:
            n *= s_
        cols = n * (2 if dtype == F32 else 1)
        off = (self.off + 15) // 16 * 16
        self.off = off + cols
        assert self.off <= self.cap, (name, self.off, self.cap)
        v = self.r[0:shape[0], off:off + cols]
        if dtype == F32:
            v = v.bitcast(F32)
        return v


def make_mem(ctx, table, carves):
    def mem(name, shape, dtype):
        if name in table:
            return table[name]
        for pred, cv in carves:
            if pred(name):
                return cv(name, shape, dtype)
        return ctx.sbuf(name, shape, dtype)
    return mem


def barrier(ctx):
    allk = list(ENGINES) + ["st", "acc", "trq", "w2s", "w1s", "tr2"]
    snaps = {e: Tok(ctx.sems[e], ctx.sems[e].n) for e in allk}
    for e in ENGINES:
        ctx.op(e, lambda en: en.nop(),
               waits=[snaps[o] for o in allk if o != e])


def build_fused_nc(dbg=False):
    nc = bass.Bass("TRN2", target_bir_lowering=False)
    x = nc.dram_tensor("x", [S, D], F32, kind="ExternalInput").ap()
    y = nc.dram_tensor("y", [S, D], F32, kind="ExternalOutput").ap()
    caps = decl_consts(nc)
    w = decl_weights(nc, BLOCK_W["l0a"] + BLOCK_W["l0f"] + BLOCK_W["l1a"]
                     + BLOCK_W["l1f"])
    kd = "ExternalOutput" if dbg else "Internal"
    xn16 = nc.dram_tensor("xnT4", [16, 128, 2048], BF16, kind="Internal").ap()
    xA_pad = nc.dram_tensor("xA", [S + 1, D], F32, kind=kd).ap()
    xA = xA_pad[0:S, :]
    xB = nc.dram_tensor("xB", [S, D], F32, kind=kd).ap()
    xtok_pad = nc.dram_tensor("xtokp", [S + 1, D], BF16, kind="Internal").ap()
    tid_d = nc.dram_tensor("tid_d", [8 * 1024 + 128, 1], I32,
                           kind=kd).ap()
    pm_tok = nc.dram_tensor("pm_tok", [S, 1], F32, kind=kd).ap()
    y_pad = nc.dram_tensor("y_pad", [S + 1, D], F32, kind="Internal").ap()
    acc = nc.dram_tensor("acc", [S, D], F32, kind="Internal").ap()
    sc = {
        "q4": nc.dram_tensor("q4", [S, 4096], BF16, kind=kd).ap(),
        "v4d": nc.dram_tensor("v4d", [S, 8192], BF16, kind="Internal").ap(),
        "of4": nc.dram_tensor("of4", [S, 4096], BF16, kind="Internal").ap(),
    }
    es = contextlib.ExitStack()
    with es:
        ctx = Ctx(nc, es)
        com = Common(ctx, caps)
        wbig = ctx.sbuf("sh_wbig", [128, 32768], BF16)
        wbig2 = ctx.sbuf("sh_wbig2", [128, 32768], BF16)
        hbuf = ctx.sbuf("sh_hbuf", [128, 16384], BF16)
        xnTs = ctx.sbuf("sh_xnT", [128, 4096], BF16)

        ln_mem = make_mem(ctx, {}, [(lambda n: n.startswith("ln_"),
                                     Carve(wbig[:, 0:16384]))])
        l0a_tab = {
            "a_wbig": wbig[:],
            "a_qh": wbig2[:, 0:8192], "a_vd": wbig2[:, 8192:24576],
            "a_ofT": wbig2[:],
            "a_ksum": hbuf[:, 0:4096].bitcast(F32),
            "a_xnT": hbuf[:, 6144:10240],
            "a_ev": hbuf[:, 10240:14336].bitcast(F32),
            "a_ob": hbuf[:, 14336:16384],
            "a_xres": hbuf[:, 0:4096].bitcast(F32),
        }
        l0a_mem = make_mem(ctx, l0a_tab, [])
        l0f_tab = {"f_wi": wbig[:], "f_wo": wbig2[:],
                   "f_hbuf": hbuf[:, 0:8192], "f_xnT": xnTs[:]}
        l0f_mem = make_mem(ctx, l0f_tab,
                           [(lambda n: n.startswith("f_") and n != "f_wob",
                             Carve(hbuf[:, 8192:16384]))])
        l1a_tab = {"t_xnT": hbuf[:, 0:8192]}
        carveA = Carve(wbig[:])
        carveB = Carve(wbig2[:])
        big4 = {"t_Ocf", "t_Ocs", "t_nu", "t_v"}
        l1a_mem = make_mem(ctx, l1a_tab,
                           [(lambda n: n in big4, carveA),
                            (lambda n: n.startswith("t_"), carveB)])
        l1f_tab = {"m_w1": wbig[:], "m_w2": wbig2[:],
                   "m_hbuf": hbuf[:, 0:8192], "m_xnT": xnTs[:]}
        l1f_mem = make_mem(ctx, l1f_tab,
                           [(lambda n: n.startswith("m_") and n != "m_b2",
                             Carve(hbuf[:, 8192:16384]))])

        xn16b = nc.dram_tensor("xnT4b", [16, 128, 2048], BF16,
                               kind="Internal").ap()
        lnfA = make_mem(ctx, {}, [(lambda n: n.startswith("lnf_A"),
                                   Carve(hbuf[:, 4096:13000]))])
        lnfT = make_mem(ctx, {}, [(lambda n: n.startswith("lnf_T"),
                                   Carve(hbuf[:, 0:8192]))])
        emit_ln(ctx, com, x, xn16, "ln0", mem=ln_mem)
        barrier(ctx)
        emit_l0_attn(ctx, com, x, xn16, w, sc, xA, mem=l0a_mem)
        barrier(ctx)
        emit_ln(ctx, com, xA, xn16b, "ln1")
        barrier(ctx)
        emit_l0_ffn(ctx, com, xA, xn16b, w, xB, mem=l0f_mem)
        barrier(ctx)
        emit_ln(ctx, com, xB, xn16, "ln2")
        barrier(ctx)
        emit_l1_attn(ctx, com, xB, xn16, w, xA, mem=l1a_mem)
        barrier(ctx)
        emit_ln(ctx, com, xA, xn16b, "ln3b")
        barrier(ctx)
        emit_l1_ffn(ctx, com, xA, xn16b, w, acc, y, mem=l1f_mem)
        ctx.op("sync", lambda e: e.nop(),
               waits=[Tok(ctx.sems["st"], ctx.sems["st"].n),
                      Tok(ctx.sems["acc"], ctx.sems["acc"].n)])
        block = es.enter_context(nc.Block())
        ctx.finalize(block)
    return nc


_BUILDERS = {"l0a": build_l0a_nc, "l0f": build_l0f_nc,
             "l1a": build_l1a_nc, "l1f": build_l1f_nc}


def kernel(**inputs):
    g = prep_weights(inputs)
    consts = {"c_" + k: v for k, v in host_consts().items()}
    x = np.ascontiguousarray(np.asarray(inputs["L0_attn_x"], np.float32))
    B = x.shape[0]
    nc = build_fused_nc()
    allw = BLOCK_W["l0a"] + BLOCK_W["l0f"] + BLOCK_W["l1a"] + BLOCK_W["l1f"]
    in_maps = []
    for b in range(B):
        im = {"x": np.ascontiguousarray(x[b])}
        im.update(consts)
        for n in allw:
            im["w_" + n] = g[n]
        in_maps.append(im)
    res = run_bass_kernel_spmd(nc, in_maps, core_ids=list(range(B)))
    return np.stack([np.asarray(r["y"]) for r in res.results], axis=0)


def kernel_unfused(**inputs):
    g = prep_weights(inputs)
    consts = {"c_" + k: v for k, v in host_consts().items()}
    x = np.ascontiguousarray(np.asarray(inputs["L0_attn_x"], np.float32))
    B = x.shape[0]
    cur = [np.ascontiguousarray(x[b]) for b in range(B)]
    for blk in ("l0a", "l0f", "l1a", "l1f"):
        nc = _BUILDERS[blk]()
        in_maps = []
        for b in range(B):
            im = {"x": cur[b]}
            im.update(consts)
            for n in BLOCK_W[blk]:
                im["w_" + n] = g[n]
            in_maps.append(im)
        res = run_bass_kernel_spmd(nc, in_maps, core_ids=list(range(B)))
        cur = [np.asarray(r["y"]) for r in res.results]
    return np.stack(cur, axis=0)


# version 25
# speedup vs baseline: 1.3566x; 1.3566x over previous
import contextlib
import numpy as np
import ml_dtypes
import concourse.bass as bass
import concourse.mybir as mybir
from concourse.bass import IndirectOffsetOnAxis
from concourse.bass_utils import run_bass_kernel_spmd
from concourse.alu_op_type import AluOpType as OP

F32 = mybir.dt.float32
BF16 = mybir.dt.bfloat16
I32 = mybir.dt.int32
AF = mybir.ActivationFunctionType
AXX = mybir.AxisListType.X

S = 4096
D = 1024
NT = S // 128
NG = S // 512
ENGINES = ("sync", "gpsimd", "vector", "scalar", "tensor")


class _Sem:
    def __init__(self, handle):
        self.h = handle
        self.n = 0


class Tok:
    __slots__ = ("sem", "val")

    def __init__(self, sem, val):
        self.sem = sem
        self.val = val


class Ctx:

    def __init__(self, nc, es):
        self.nc = nc
        self.es = es
        self.streams = {e: [] for e in ENGINES}
        self.sems = {e: _Sem(es.enter_context(nc.semaphore(f"sem_{e}")))
                     for e in ENGINES}
        for extra in ("st", "acc", "gpmem", "w2s", "w1s", "trq", "tr2"):
            self.sems[extra] = _Sem(es.enter_context(nc.semaphore("sem_" + extra)))
        self._dma_sems = {id(self.sems[k]) for k in
                          ("sync", "gpsimd", "st", "acc", "w2s", "w1s",
                           "trq", "tr2")}
        self.psum = [es.enter_context(nc.psum_tensor(f"pb{i}", [128, 512], F32))
                     for i in range(8)]
        self.ptok = [None] * 8
        self._pnext = 0
        self._names = set()

    def sbuf(self, name, shape, dtype):
        assert name not in self._names, name
        self._names.add(name)
        return self.es.enter_context(self.nc.sbuf_tensor(name, shape, dtype))

    def op(self, eng, fn, waits=(), amt=1, sem_name=None):
        if sem_name is None and eng == "gpsimd" and amt == 1:
            sem_name = "gpmem"
        sem = self.sems[sem_name or eng]
        frozen = []
        best = {}
        for wt in waits:
            if wt is None:
                continue
            k = id(wt.sem)
            v = wt.sem.n if k in self._dma_sems else wt.val
            if best.get(k, -1) < v:
                best[k] = v
        for k, v in best.items():
            pass
        seen = set()
        for wt in waits:
            if wt is None:
                continue
            k = id(wt.sem)
            if k in seen:
                continue
            seen.add(k)
            frozen.append((wt.sem.h, best[k]))
        sem.n += amt
        tok = Tok(sem, sem.n)
        semh = sem.h

        def run(e, frozen=tuple(frozen), fn=fn, semh=semh, amt=amt):
            for h, v in frozen:
                e.wait_ge(h, v)
            ins = fn(e)
            ins.then_inc(semh, amt)

        self.streams[eng].append(run)
        return tok

    def dma(self, q, out, in_, waits=(), sem_name=None):
        return self.op(q, lambda e: e.dma_start(out=out, in_=in_), waits,
                       amt=16, sem_name=sem_name)

    def dma_st(self, q, out, in_, waits=()):
        return self.dma(q, out, in_, waits, sem_name="st")

    def dma_tr(self, q, out, in_, waits=()):
        return self.op("scalar", lambda e: e.dma_start_transpose(out, in_),
                       waits, amt=16, sem_name="trq")

    def bank(self):
        i = self._pnext
        self._pnext = (self._pnext + 1) % 8
        return self.psum[i], i

    def mm_chain(self, pb, pairs, waits=(), tile_position=None):
        def fn(e, pb=pb, pairs=tuple(pairs), tp=tile_position):
            ins = None
            n = len(pairs)
            for i, (l, r) in enumerate(pairs):
                kw = {}
                if tp is not None:
                    kw["tile_position"] = tp
                ins = e.matmul(pb, l, r, start=(i == 0), stop=(i == n - 1), **kw)
            return ins
        return self.op("tensor", fn, waits)

    def finalize(self, block):
        for e in ENGINES:
            fns = self.streams[e]
            if not fns:
                continue

            def body(eng, fns=fns):
                for f in fns:
                    f(eng)
            getattr(block, e)(body)


def host_consts():
    ii = np.arange(128)
    c = {}
    c["ident128"] = np.eye(128, dtype=np.float32)
    c["stril"] = (ii[:, None] > ii[None, :]).astype(np.float32)
    c["triu0"] = (ii[:, None] <= ii[None, :]).astype(np.float32)
    c["sutril"] = (ii[:, None] < ii[None, :]).astype(np.float32)
    c["tril0"] = (ii[:, None] >= ii[None, :]).astype(np.float32)
    dstr = ii[:, None] - 1 - ii[None, :]
    c["Dstrict"] = np.where(dstr >= 0, dstr, 0).astype(np.float32)
    c["DstrictT"] = c["Dstrict"].T.copy()
    dinc = ii[:, None] - ii[None, :]
    c["Dincl"] = np.where(dinc >= 0, dinc, 0).astype(np.float32)
    c["DinclT"] = c["Dincl"].T.copy()
    c["iota"] = ii.astype(np.float32).reshape(128, 1)
    c["e127"] = np.eye(128, dtype=np.float32)[:, 127:128].copy()
    c["j512"] = (np.arange(8, dtype=np.float32) * 512.0).reshape(1, 8)
    c["idx8v"] = (np.arange(8)[None, :] * 128 + ii[:, None]).astype(np.float32)
    c["idx32v"] = (np.arange(32)[None, :] * 128 + ii[:, None]).astype(np.float32)
    return c


def decl_consts(nc):
    return {k: nc.dram_tensor("c_" + k, list(v.shape), F32,
                              kind="ExternalInput").ap()
            for k, v in host_consts().items()}


class Common:
    def __init__(self, ctx, caps):
        self.ctx = ctx
        self.caps = caps
        self.ones_row = ctx.sbuf("ones_row", [1, 512], F32)
        self.ones_col = ctx.sbuf("ones_col", [128, 1], F32)
        self.eps_ln = ctx.sbuf("eps_ln", [128, 1], F32)
        t0 = ctx.op("gpsimd", lambda e: e.memset(self.ones_row[:], 1.0))
        t1 = ctx.op("gpsimd", lambda e: e.memset(self.ones_col[:], 1.0),
                    waits=[t0])
        self.init_tok = ctx.op("gpsimd", lambda e: e.memset(self.eps_ln[:], 1e-5),
                               waits=[t1])
        self._cbuf = {}

    def ident16(self):
        if not hasattr(self, "_id16"):
            ctx = self.ctx
            buf, tok = self.const("ident128")
            b16 = ctx.sbuf("cb_ident16", [128, 128], BF16)
            t = ctx.op("vector", lambda e: e.tensor_copy(b16[:], buf[:]),
                       waits=[tok])
            self._id16 = (b16, t)
        return self._id16

    def const(self, name):
        if name not in self._cbuf:
            ctx = self.ctx
            v = host_consts()[name]
            buf = ctx.sbuf("cb_" + name, list(v.shape), F32)
            tok = ctx.dma("sync", buf[:], self.caps[name][:])
            self._cbuf[name] = (buf, tok)
        return self._cbuf[name]


def emit_ln(ctx, com, x_ap, xnT4_ap, tag, mem=None, tok_ap=None):
    if mem is None:
        mem = ctx.sbuf
    id16, id16_tok = com.ident16()
    if not hasattr(com, "ln_bufs"):
        xt = mem("ln_x", [128, 2 * 1024], F32)
        sq = mem("ln_sq", [128, 2 * 1024], F32)
        sm = mem("ln_sm", [128, 2 * 8], F32)
        xnb = mem("ln_out", [128, 2 * 1024], BF16)
        xts = mem("ln_xts", [128, 2 * 2048], BF16)
        com.ln_bufs = (xt, sq, sm, xnb, xts)
    xt, sq, sm, xnb, xts = com.ln_bufs
    fin = [None, None]
    tfin = [None, None]
    prev_cps = [None]
    out_toks = []
    for g in range(NT):
        p = g % 2
        xv = xt[:, p * 1024:(p + 1) * 1024]
        sqv = sq[:, p * 1024:(p + 1) * 1024]
        ov = xnb[:, p * 1024:(p + 1) * 1024]
        sslot = (g // 2) % 2
        tv = xts[:, sslot * 2048:(sslot + 1) * 2048]
        smv = sm[:, p * 8:(p + 1) * 8]
        ld = ctx.dma("sync", xv, x_ap[g * 128:(g + 1) * 128, :],
                     waits=list(fin[p] or []))
        t1 = ctx.op("vector", lambda e, xv=xv, smv=smv:
                    e.reduce_sum(smv[:, 0:1], xv, axis=AXX), waits=[ld])
        t2 = ctx.op("vector", lambda e, smv=smv: e.tensor_scalar(
            smv[:, 1:2], smv[:, 0:1], -1.0 / 1024.0, None, OP.mult), waits=[t1])
        t3 = ctx.op("scalar", lambda e, xv=xv, sqv=sqv:
                    e.activation(sqv, xv, AF.Square), waits=[ld])
        t4 = ctx.op("vector", lambda e, sqv=sqv, smv=smv:
                    e.reduce_sum(smv[:, 2:3], sqv, axis=AXX), waits=[t3])
        t5 = ctx.op("vector", lambda e, smv=smv: e.tensor_mul(
            smv[:, 3:4], smv[:, 1:2], smv[:, 1:2]), waits=[t2])
        t6 = ctx.op("vector", lambda e, smv=smv: e.scalar_tensor_tensor(
            smv[:, 4:5], smv[:, 2:3], 1.0 / 1024.0, smv[:, 3:4],
            OP.mult, OP.subtract), waits=[t4, t5])
        t7 = ctx.op("scalar", lambda e, smv=smv: e.activation(
            smv[:, 5:6], smv[:, 4:5], AF.Sqrt, bias=com.eps_ln[:]),
            waits=[t6, com.init_tok])
        t8 = ctx.op("vector", lambda e, smv=smv:
                    e.reciprocal(smv[:, 6:7], smv[:, 5:6]), waits=[t7])
        t9 = ctx.op("vector", lambda e, xv=xv, ov=ov, smv=smv: e.tensor_scalar(
            ov, xv, smv[:, 1:2], smv[:, 6:7], OP.add, OP.mult),
            waits=[t8] + list(fin[p] or []))
        cps = []
        for k in range(8):
            pb, pi = ctx.bank()
            ptr = pb[:, 0:64].bitcast(BF16)
            tr = ctx.op("tensor", lambda e, ptr=ptr, ov=ov, k=k: e.transpose(
                ptr, ov[:, k * 128:(k + 1) * 128], id16[:]),
                waits=[t9, id16_tok, ctx.ptok[pi]])
            cp = ctx.op("vector", lambda e, ptr=ptr, tv=tv, k=k, g=g:
                        e.tensor_copy(
                tv[:, k * 256 + (g % 2) * 128: k * 256 + (g % 2) * 128 + 128],
                ptr),
                waits=[tr, tfin[sslot]])
            ctx.ptok[pi] = cp
            cps.append(cp)
        rd = [cps[-1]]
        if tok_ap is not None:
            rd.append(ctx.dma_st("sync", tok_ap[g * 128:(g + 1) * 128, :], ov,
                                 waits=[t9]))
        if g % 2 == 1:
            stt = ctx.dma_st("sync", xnT4_ap[g // 2, :, :], tv,
                             waits=cps + [prev_cps[0]])
            tfin[sslot] = stt
            out_toks.append(stt)
        prev_cps[0] = cps[-1]
        fin[p] = rd
    return out_toks[-1]


def emit_lnf_tile(ctx, com, key, carve_mem, xrv, tile, xnT4_ap, waits):
    id16, id16_tok = com.ident16()
    st_ = getattr(com, "lnf_" + key, None)
    if st_ is None:
        st_ = {
            "sm": carve_mem(f"lnf_{key}_sm", [128, 16], F32),
            "xnb": carve_mem(f"lnf_{key}_xnb", [128, 2 * 1024], BF16),
            "xts": carve_mem(f"lnf_{key}_xts", [128, 2 * 2048], BF16),
            "fin": [None, None],
            "tfin": [None, None],
            "pcps": [None],
        }
        setattr(com, "lnf_" + key, st_)
    p = tile % 2
    sslot = (tile // 2) % 2
    sm = st_["sm"][:, p * 8:(p + 1) * 8]
    ov = st_["xnb"][:, p * 1024:(p + 1) * 1024]
    tv = st_["xts"][:, sslot * 2048:(sslot + 1) * 2048]
    w0 = list(waits)
    t1 = ctx.op("vector", lambda e: e.reduce_sum(sm[:, 0:1], xrv, axis=AXX),
                waits=w0)
    t2 = ctx.op("vector", lambda e: e.tensor_scalar(
        sm[:, 1:2], sm[:, 0:1], -1.0 / 1024.0, None, OP.mult), waits=[t1])
    pb, pi = ctx.bank()
    t3a = ctx.op("scalar", lambda e, pb=pb: e.activation(
        pb[:], xrv[:, 0:512], AF.Square), waits=w0 + [ctx.ptok[pi]])
    pb2, pi2 = ctx.bank()
    t3b = ctx.op("scalar", lambda e, pb2=pb2: e.activation(
        pb2[:], xrv[:, 512:1024], AF.Square), waits=w0 + [ctx.ptok[pi2]])
    t4a = ctx.op("vector", lambda e, pb=pb: e.reduce_sum(
        sm[:, 2:3], pb[:], axis=AXX), waits=[t3a])
    ctx.ptok[pi] = t4a
    t4b = ctx.op("vector", lambda e, pb2=pb2: e.reduce_sum(
        sm[:, 3:4], pb2[:], axis=AXX), waits=[t3b])
    ctx.ptok[pi2] = t4b
    t4c = ctx.op("vector", lambda e: e.tensor_add(
        sm[:, 2:3], sm[:, 2:3], sm[:, 3:4]), waits=[t4a, t4b])
    t5 = ctx.op("vector", lambda e: e.tensor_mul(
        sm[:, 4:5], sm[:, 1:2], sm[:, 1:2]), waits=[t2])
    t6 = ctx.op("vector", lambda e: e.scalar_tensor_tensor(
        sm[:, 5:6], sm[:, 2:3], 1.0 / 1024.0, sm[:, 4:5],
        OP.mult, OP.subtract), waits=[t4c, t5])
    t7 = ctx.op("scalar", lambda e: e.activation(
        sm[:, 6:7], sm[:, 5:6], AF.Sqrt, bias=com.eps_ln[:]),
        waits=[t6, com.init_tok])
    t8 = ctx.op("vector", lambda e: e.reciprocal(sm[:, 7:8], sm[:, 6:7]),
                waits=[t7])
    t9 = ctx.op("vector", lambda e: e.tensor_scalar(
        ov, xrv, sm[:, 1:2], sm[:, 7:8], OP.add, OP.mult),
        waits=[t8] + ([st_["fin"][p]] if st_["fin"][p] else []))
    cps = []
    for k in range(8):
        pb3, pi3 = ctx.bank()
        ptr = pb3[:, 0:64].bitcast(BF16)
        tr = ctx.op("tensor", lambda e, ptr=ptr, k=k: e.transpose(
            ptr, ov[:, k * 128:(k + 1) * 128], id16[:]),
            waits=[t9, id16_tok, ctx.ptok[pi3]])
        cp = ctx.op("vector", lambda e, ptr=ptr, k=k: e.tensor_copy(
            tv[:, k * 256 + (tile % 2) * 128: k * 256 + (tile % 2) * 128 + 128],
            ptr),
            waits=[tr, st_["tfin"][sslot]])
        ctx.ptok[pi3] = cp
        cps.append(cp)
    if tile % 2 == 1:
        stt = ctx.dma_st("sync", xnT4_ap[tile // 2, :, :], tv,
                         waits=cps + [st_["pcps"][0]])
        st_["tfin"][sslot] = stt
    st_["pcps"][0] = cps[-1]
    st_["fin"][p] = cps[-1]
    return cps[-1]


def load_xnT(ctx, xnT, xnT4_ap, g, waits=(), tg=512):
    if tg == 256:
        return [ctx.dma("sync", xnT[:], xnT4_ap[g, :, :], waits=list(waits))]
    toks = []
    for h in range(2):
        toks.append(ctx.dma("sync", xnT[:, h * 2048:(h + 1) * 2048],
                            xnT4_ap[g * 2 + h, :, :], waits=list(waits)))
    return toks


def emit_l0_attn(ctx, com, x_ap, xn16_ap, w, sc, xout_ap, mem=None,
                 lnf=None):
    if mem is None:
        mem = ctx.sbuf
    nc = ctx.nc
    q4, v4d, of4 = sc["q4"], sc["v4d"], sc["of4"]
    SCALE = 64.0 ** -0.5

    wbig = mem("a_wbig", [128, 8 * 4096], BF16)
    xnT = mem("a_xnT", [128, 8 * 512], BF16)
    ksum = mem("a_ksum", [128, NT * 64], F32)
    ev = mem("a_ev", [128, 4 * 512], F32)
    ss = mem("a_ss", [128, 4 * 16], F32)
    ob = mem("a_ob", [128, 4 * 512], BF16)
    eps_qn = mem("a_epsqn", [128, 1], F32)
    ctx.op("gpsimd", lambda e: e.memset(eps_qn[:], 1e-6))

    wtok = [None] * 8
    wcons = [None]

    def wload(name):
        for k in range(8):
            wtok[k] = ctx.dma("gpsimd", wbig[:, k * 4096:(k + 1) * 4096],
                              w[name][k * 128:(k + 1) * 128, :],
                              waits=[wcons[0]])

    xslot_toks = [None, None]
    xslot_cons = [None, None]
    xiter = [0]

    def xview(slot):
        return xnT[:]

    def xprefetch(t):
        if t >= 3 * NG:
            return
        xslot_toks[t % 2] = load_xnT(ctx, xnT[:], xn16_ap, t % NG,
                                     waits=[xslot_cons[0], xslot_cons[1]])
    xprefetch(0)

    def proj_pass(evac_cb):
        for g in range(NG):
            t_ = xiter[0]
            slot = t_ % 2
            xtoks = xslot_toks[slot]
            xv = xview(slot)
            for ti in range(4):
                tile = g * 4 + ti
                xof = (ti // 2) * 2048 + (ti % 2) * 128
                for fs in range(8):
                    pb, pi = ctx.bank()
                    pairs = [(xv[:, xof + k * 256: xof + k * 256 + 128],
                              wbig[:, k * 4096 + fs * 512: k * 4096 + (fs + 1) * 512])
                             for k in range(8)]
                    mt = ctx.mm_chain(pb[:], pairs,
                                      waits=xtoks + wtok + [ctx.ptok[pi]])
                    evac_cb(tile, fs, pb, pi, mt)
            xslot_cons[slot] = Tok(ctx.sems["tensor"], ctx.sems["tensor"].n)
            xiter[0] = t_ + 1
            xprefetch(t_ + 1)

    wload("Wk")
    kz = ctx.op("gpsimd", lambda e: e.memset(ksum[:], 0.0))
    ktmp = mem("a_ktmp", [128, 64], F32)
    ev_cons = {}

    def evac_k(tile, fs, pb, pi, mt):
        b = (tile * 8 + fs) % 4
        evv = ev[:, b * 512:(b + 1) * 512]
        r1 = ctx.op("scalar", lambda e, evv=evv, pb=pb:
                    e.activation(evv, pb[:], AF.Relu),
                    waits=[mt, ev_cons.get(b)])
        ctx.ptok[pi] = r1
        kv = ksum[:, tile * 64:(tile + 1) * 64]
        r2 = ctx.op("vector", lambda e, evv=evv: e.reduce_sum(
            ktmp[:], evv.rearrange("p (j d) -> p d j", j=8), axis=AXX),
            waits=[r1])
        r3 = ctx.op("vector", lambda e, kv=kv:
                    e.tensor_add(kv, kv, ktmp[:]), waits=[r2, kz])
        ev_cons[b] = r2

    proj_pass(evac_k)
    wcons[0] = Tok(ctx.sems["tensor"], ctx.sems["tensor"].n)

    wload("Wv")

    ob_cons = {}
    vstg = mem("a_vstg", [128, 2 * 1024], BF16)
    vstg_cons = {}

    def evac_v(tile, fs, pb, pi, mt):
        b = (tile * 8 + fs) % 4
        b2 = (tile * 8 + fs) % 2
        obv = ob[:, b * 512:(b + 1) * 512]
        c1 = ctx.op("vector", lambda e, obv=obv, pb=pb:
                    e.tensor_copy(obv, pb[:]), waits=[mt, ob_cons.get(b)])
        ctx.ptok[pi] = c1
        sv = vstg[:, b2 * 1024:(b2 + 1) * 1024]             .rearrange("p (j c d) -> p j c d", j=8, c=2)
        g1 = ctx.op("scalar", lambda e, sv=sv, obv=obv: e.copy(
            sv[:, :, 0:1, :].squeeze(2), obv.rearrange("p (j d) -> p j d", j=8)),
            waits=[c1, vstg_cons.get(b2)])
        g2 = ctx.op("scalar", lambda e, sv=sv, obv=obv: e.copy(
            sv[:, :, 1:2, :].squeeze(2), obv.rearrange("p (j d) -> p j d", j=8)),
            waits=[c1])
        ob_cons[b] = g2
        t0 = tile * 128
        d1 = ctx.dma_st("sync", v4d[t0:t0 + 128, fs * 1024:(fs + 1) * 1024],
                        vstg[:, b2 * 1024:(b2 + 1) * 1024], waits=[g2])
        vstg_cons[b2] = d1

    proj_pass(evac_v)
    wcons[0] = Tok(ctx.sems["tensor"], ctx.sems["tensor"].n)

    wload("Wq")

    evq_cons = {}
    obq_cons = {}

    def evac_q(tile, fs, pb, pi, mt):
        b = (tile * 8 + fs) % 4
        bo_ = b
        evv = ev[:, b * 512:(b + 1) * 512]
        obv = ob[:, bo_ * 512:(bo_ + 1) * 512]
        ssv = ss[:, b * 16:(b + 1) * 16]
        r1 = ctx.op("scalar", lambda e, evv=evv, pb=pb: e.activation(
            evv, pb[:], AF.Relu, scale=SCALE), waits=[mt, evq_cons.get(b)])
        ctx.ptok[pi] = r1
        kv = ksum[:, tile * 64:(tile + 1) * 64]
        r2 = ctx.op("vector", lambda e, evv=evv, kv=kv: e.tensor_tensor(
            evv.rearrange("p (i d) -> p i d", i=8),
            evv.rearrange("p (i d) -> p i d", i=8),
            kv[:, None, :].broadcast_to([128, 8, 64]), OP.mult), waits=[r1])
        r3 = ctx.op("vector", lambda e, evv=evv, ssv=ssv: e.reduce_sum(
            ssv[:, 0:8], evv.rearrange("p (i d) -> p i d", i=8), axis=AXX),
            waits=[r2])
        r4 = ctx.op("vector", lambda e, ssv=ssv: e.tensor_scalar(
            ssv[:, 8:16], ssv[:, 0:8], eps_qn[:], None, OP.add), waits=[r3])
        r5 = ctx.op("vector", lambda e, ssv=ssv:
                    e.reciprocal(ssv[:, 0:8], ssv[:, 8:16]), waits=[r4])
        r6 = ctx.op("vector", lambda e, evv=evv, obv=obv, ssv=ssv: e.tensor_tensor(
            obv.rearrange("p (i d) -> p i d", i=8),
            evv.rearrange("p (i d) -> p i d", i=8),
            ssv[:, 0:8, None].broadcast_to([128, 8, 64]), OP.mult),
            waits=[r5, obq_cons.get(bo_)])
        t0 = tile * 128
        dq = ctx.dma_st("sync", q4[t0:t0 + 128, fs * 512:(fs + 1) * 512], obv,
                        waits=[r6])
        evq_cons[b] = r6
        obq_cons[bo_] = dq

    proj_pass(evac_q)
    wcons[0] = Tok(ctx.sems["tensor"], ctx.sems["tensor"].n)

    pbsnap = [Tok(ctx.sems["st"], ctx.sems["st"].n)]
    qh = mem("a_qh", [128, 2 * 4096], BF16)
    vd = mem("a_vd", [128, 2 * 8192], BF16)
    qh_cons = [None, None]
    vd_cons = [None, None]
    ob_cons2 = [None] * 4
    q4v = q4.rearrange("t (i2 c) -> (t i2) c", c=128)
    v4v = v4d.rearrange("t (j c) -> (t j) c", c=128)
    for tile in range(NT):
        p = tile % 2
        qhv = qh[:, p * 4096:(p + 1) * 4096]
        vdv = vd[:, p * 8192:(p + 1) * 8192]
        t0 = tile * 128
        extra = pbsnap if tile < 2 else []
        qt = ctx.dma_tr("sync", qhv, q4v[t0 * 32:(t0 + 128) * 32, :],
                        waits=[qh_cons[p]] + extra)
        vt = ctx.dma_tr("sync", vdv, v4v[t0 * 64:(t0 + 128) * 64, :],
                        waits=[vd_cons[p]] + extra)
        for fill in range(8):
            pb, pi = ctx.bank()

            def fills(e, pb=pb, qhv=qhv, vdv=vdv, fill=fill):
                ins = None
                for tp in range(2):
                    for par in range(2):
                        rb = par * 64
                        ob_ = tp * 64 + par * 32
                        for slot in range(8):
                            tl = fill * 16 + slot * 2 + tp
                            lhsT = qhv[rb:rb + 64, tl * 32:(tl + 1) * 32]
                            rhs = vdv[rb:rb + 64, tl * 64:(tl + 1) * 64]
                            ins = e.matmul(
                                pb[ob_:ob_ + 32, slot * 64:(slot + 1) * 64],
                                lhsT, rhs, start=(slot == 0), stop=(slot == 7),
                                tile_position=(rb, ob_),
                                skip_group_check=True)
                return ins

            mt = ctx.op("tensor", fills, waits=[qt, vt, ctx.ptok[pi]])
            b = fill % 4
            obv = ob[:, b * 512:(b + 1) * 512]
            c1 = ctx.op("vector", lambda e, obv=obv, pb=pb:
                        e.tensor_copy(obv, pb[:]),
                        waits=[mt, ob_cons2[b]] + (pbsnap if tile == 0 else []))
            ctx.ptok[pi] = c1
            base2 = tile * 64 + fill * 8
            of2 = of4.rearrange("(t2 two) f -> t2 two f", two=2)
            for tp in range(2):
                dst = of2[base2:base2 + 8, tp:tp + 1, :].squeeze(1) \
                    .rearrange("t (f j) -> f t j", j=64)
                d = ctx.dma_st("sync", dst,
                               obv[tp * 64:(tp + 1) * 64, :]
                               .rearrange("f (s j) -> f s j", j=64),
                               waits=[c1])
                ob_cons2[b] = d
        qh_cons[p] = Tok(ctx.sems["tensor"], ctx.sems["tensor"].n)
        vd_cons[p] = qh_cons[p]

    wop_tok = []
    for cch in range(32):
        wop_tok.append(ctx.dma("gpsimd", wbig[:, cch * 1024:(cch + 1) * 1024],
                               w["WoP"][cch * 128:(cch + 1) * 128, :],
                               waits=[wcons[0]]))
    bo_sb = mem("a_bo", [1, 1024], F32)
    bo_tok = ctx.dma("sync", bo_sb[:], w["bo"][:])
    ofT = mem("a_ofT", [128, 2 * 32 * 512], BF16)
    xres = mem("a_xres", [128, 2 * 1024], F32)
    passB_end = Tok(ctx.sems["tensor"], ctx.sems["tensor"].n)
    passA_vec = Tok(ctx.sems["vector"], ctx.sems["vector"].n)
    oslot_toks = [None, None]
    oslot_cons = [passB_end, passB_end]

    def oview(slot):
        return ofT[:, slot * 16384:(slot + 1) * 16384]

    def oprefetch(g):
        if g >= NG:
            return
        toks = []
        ov = oview(g % 2)
        for cch in range(32):
            toks.append(ctx.dma_tr(
                "sync", ov[:, cch * 512:(cch + 1) * 512],
                of4[g * 512:(g + 1) * 512, cch * 128:(cch + 1) * 128],
                waits=[oslot_cons[g % 2], pbsnap[0]]))
        oslot_toks[g % 2] = toks
    oprefetch(0)
    xres_cons = [passA_vec, passA_vec]
    for g in range(NG):
        slot = g % 2
        ttoks = oslot_toks[slot]
        ov = oview(slot)
        oprefetch(g + 1)
        for ti in range(4):
            tile = g * 4 + ti
            t0 = tile * 128
            p = ti % 2
            xrv = xres[:, p * 1024:(p + 1) * 1024]
            xl = ctx.dma("sync", xrv, x_ap[t0:t0 + 128, :],
                         waits=[xres_cons[p]])
            for ms in range(2):
                pb, pi = ctx.bank()
                pairs = [(ov[:, cch * 512 + ti * 128: cch * 512 + (ti + 1) * 128],
                          wbig[:, cch * 1024 + ms * 512: cch * 1024 + (ms + 1) * 512])
                         for cch in range(32)]
                pairs.append((com.ones_row[:, 0:128],
                              bo_sb[:, ms * 512:(ms + 1) * 512]))
                mt = ctx.mm_chain(pb[:], pairs,
                                  waits=ttoks + wop_tok +
                                  [ctx.ptok[pi], bo_tok, com.init_tok])
                a1 = ctx.op("vector", lambda e, pb=pb, xrv=xrv, ms=ms:
                            e.tensor_add(xrv[:, ms * 512:(ms + 1) * 512],
                                         xrv[:, ms * 512:(ms + 1) * 512], pb[:]),
                            waits=[mt, xl])
                ctx.ptok[pi] = a1
                if ms == 1:
                    so = ctx.dma_st("sync", xout_ap[t0:t0 + 128, :], xrv,
                                    waits=[a1])
                    if lnf is not None:
                        lt = emit_lnf_tile(ctx, com, "A", lnf[0], xrv, tile,
                                           lnf[1], [a1])
                        xres_cons[p] = lt
                    else:
                        xres_cons[p] = so
        oslot_cons[slot] = Tok(ctx.sems["tensor"], ctx.sems["tensor"].n)
    return Tok(ctx.sems["sync"], ctx.sems["sync"].n)


def emit_first_match(ctx, name_tag, gl, h, scratch2, waits):
    mx = scratch2[:, 0:1]
    cc = scratch2[:, 1:2]
    tp_ = scratch2[:, 2:3]
    t1 = ctx.op("vector", lambda e: e.reduce_max(mx, gl, axis=AXX), waits=waits)
    t2 = ctx.op("vector", lambda e: e.tensor_scalar(
        h, gl, mx, None, OP.is_ge), waits=[t1])
    t3 = ctx.op("gpsimd", lambda e: e.memset(cc, 1.0), waits=[t2])
    last = t3
    for ee in range(8):
        a = ctx.op("vector", lambda e, ee=ee: e.tensor_mul(
            h[:, ee:ee + 1], h[:, ee:ee + 1], cc), waits=[last])
        if ee < 7:
            b = ctx.op("vector", lambda e, ee=ee: e.tensor_scalar(
                tp_, h[:, ee:ee + 1], -1.0, 1.0, OP.mult, OP.add), waits=[a])
            last = ctx.op("vector", lambda e: e.tensor_mul(cc, cc, tp_),
                          waits=[b])
        else:
            last = a
    return last


def emit_l0_ffn(ctx, com, x_ap, xn16_ap, w, xout_ap, mem=None):
    if mem is None:
        mem = ctx.sbuf
    nc = ctx.nc
    f_wi = mem("f_wi", [128, 8 * 4096], BF16)
    f_wo = mem("f_wo", [128, 32 * 1024], BF16)
    a_sb = mem("f_asb", [128, 8 * 32], BF16)
    b_sb = mem("f_bsb", [32, 4096], BF16)
    gwt_sb = mem("f_gwt", [128, 8 * 8], BF16)
    gb_sb = mem("f_gb", [1, 8], F32)
    wib_sb = mem("f_wib", [128, 32], F32)
    wob_sb = mem("f_wob", [1, 1024], F32)
    TG = 256
    xnT = mem("f_xnT", [128, 2 * 8 * TG], BF16)
    hbuf = mem("f_hbuf", [128, 32 * TG], BF16)
    rtT = mem("f_rtT", [32, TG], BF16)
    gl = mem("f_gl", [128, 8], F32)
    hmask = mem("f_h", [128, 8], F32)
    rr = mem("f_rr", [128, 32], F32)
    rb16 = mem("f_rb16", [128, 32], BF16)
    sc2 = mem("f_sc2", [128, 4], F32)
    xres = mem("f_xres", [128, 1024], F32)
    id16, idtok = com.ident16()

    wt = []
    for k in range(8):
        wt.append(ctx.dma("gpsimd", f_wi[:, k * 4096:(k + 1) * 4096],
                          w["wiT"][k * 128:(k + 1) * 128, :]))
    for c in range(32):
        wt.append(ctx.dma("gpsimd", f_wo[:, c * 1024:(c + 1) * 1024],
                          w["woT"][c * 128:(c + 1) * 128, :]))
    for k in range(8):
        wt.append(ctx.dma("gpsimd", a_sb[:, k * 32:(k + 1) * 32],
                          w["A_lay"][k * 128:(k + 1) * 128, :]))
        wt.append(ctx.dma("gpsimd", gwt_sb[:, k * 8:(k + 1) * 8],
                          w["gWT"][k * 128:(k + 1) * 128, :]))
    wt.append(ctx.dma("gpsimd", b_sb[:], w["B_flat"][:]))
    wt.append(ctx.dma("sync", gb_sb[:], w["gate_b"][:]))
    wt.append(ctx.dma("sync", wib_sb[:], w["wi_b"][:]))
    wt.append(ctx.dma("sync", wob_sb[:], w["wo_b"][:]))
    wtok = [Tok(ctx.sems["gpsimd"], ctx.sems["gpsimd"].n),
            Tok(ctx.sems["sync"], ctx.sems["sync"].n)]

    xcons = [None]
    hb_cons = [None]
    rt_cons = [None]
    xres_cons = [None, None]
    NGRF = S // TG
    xslot_toks = [None, None]
    xslot_cons = [None, None]

    def xview(slot):
        return xnT[:, slot * 8 * TG:(slot + 1) * 8 * TG]

    def xprefetch(t):
        if t >= NGRF:
            return
        xslot_toks[t % 2] = load_xnT(ctx, xview(t % 2), xn16_ap, t,
                                     waits=[xslot_cons[t % 2]], tg=TG)
    xprefetch(0)
    for g in range(NGRF):
        slot = g % 2
        xtoks = xslot_toks[slot]
        xv = xview(slot)
        for ti in range(TG // 128):
            xsl = lambda k, ti=ti: xv[:, k * TG + ti * 128: k * TG + (ti + 1) * 128]
            pb, pi = ctx.bank()
            pairs = [(xsl(k), gwt_sb[:, k * 8:(k + 1) * 8]) for k in range(8)]
            pairs.append((com.ones_row[:, 0:128], gb_sb[:]))
            mt = ctx.mm_chain(pb[:, 0:8], pairs,
                              waits=xtoks + wtok + [ctx.ptok[pi], com.init_tok])
            e1 = ctx.op("vector", lambda e, pb=pb: e.tensor_copy(gl[:], pb[:, 0:8]),
                        waits=[mt])
            ctx.ptok[pi] = e1
            hm = emit_first_match(ctx, f"g{g}t{ti}", gl[:], hmask[:], sc2[:],
                                  waits=[e1])
            pb2, pi2 = ctx.bank()
            pairs2 = [(xsl(k), a_sb[:, k * 32:(k + 1) * 32]) for k in range(8)]
            mt2 = ctx.mm_chain(pb2[:, 0:32], pairs2,
                               waits=xtoks + wtok + [ctx.ptok[pi2]])
            e2 = ctx.op("vector", lambda e, pb2=pb2:
                        e.tensor_copy(rr[:], pb2[:, 0:32]), waits=[mt2])
            ctx.ptok[pi2] = e2
            e3 = ctx.op("vector", lambda e: e.tensor_tensor(
                rb16[:].rearrange("p (e r) -> p e r", e=8),
                rr[:].rearrange("p (e r) -> p e r", e=8),
                hmask[:, :, None].broadcast_to([128, 8, 4]), OP.mult),
                waits=[e2, hm])
            pb3, pi3 = ctx.bank()
            ptr = pb3[0:32, 0:64].bitcast(BF16)
            mt3 = ctx.op("tensor", lambda e, ptr=ptr: e.transpose(
                ptr, rb16[:], id16[:]), waits=[e3, idtok, ctx.ptok[pi3]])
            e4 = ctx.op("vector", lambda e, ptr=ptr, ti=ti: e.tensor_copy(
                rtT[:, ti * 128:(ti + 1) * 128], ptr),
                waits=[mt3, rt_cons[0] if ti == 0 else None])
            ctx.ptok[pi3] = e4
        rt_tok = Tok(ctx.sems["vector"], ctx.sems["vector"].n)
        up_toks = []
        for fc in range(32):
            pb, pi = ctx.bank()
            pairs = [(f_wi[:, k * 4096 + fc * 128: k * 4096 + (fc + 1) * 128],
                      xv[:, k * TG:(k + 1) * TG]) for k in range(8)]
            pairs.append((b_sb[:, fc * 128:(fc + 1) * 128], rtT[:]))
            mt = ctx.mm_chain(pb[:, 0:TG], pairs,
                              waits=xtoks + wtok + [ctx.ptok[pi], rt_tok,
                                                    hb_cons[0], com.init_tok])
            r1 = ctx.op("scalar", lambda e, pb=pb, fc=fc: e.activation(
                hbuf[:, fc * TG:(fc + 1) * TG], pb[:, 0:TG], AF.Relu,
                bias=wib_sb[:, fc:fc + 1]),
                waits=[mt])
            ctx.ptok[pi] = r1
            up_toks.append(r1)
        xslot_cons[slot] = Tok(ctx.sems["tensor"], ctx.sems["tensor"].n)
        xprefetch(g + 1)
        for ti in range(TG // 128):
            tile = g * (TG // 128) + ti
            t0 = tile * 128
            xrv = xres[:]
            xl = ctx.dma("sync", xrv, x_ap[t0:t0 + 128, :],
                         waits=[xres_cons[0]])
            for ms in range(2):
                pb, pi = ctx.bank()
                pairs = [(hbuf[:, c * TG + ti * 128: c * TG + (ti + 1) * 128],
                          f_wo[:, c * 1024 + ms * 512: c * 1024 + (ms + 1) * 512])
                         for c in range(32)]
                pairs.append((com.ones_row[:, 0:128],
                              wob_sb[:, ms * 512:(ms + 1) * 512]))
                mt = ctx.mm_chain(pb[:], pairs,
                                  waits=up_toks + wtok + [ctx.ptok[pi]])
                a1 = ctx.op("vector", lambda e, pb=pb, xrv=xrv, ms=ms:
                            e.tensor_add(xrv[:, ms * 512:(ms + 1) * 512],
                                         xrv[:, ms * 512:(ms + 1) * 512], pb[:]),
                            waits=[mt, xl])
                ctx.ptok[pi] = a1
                if ms == 1:
                    so = ctx.dma_st("sync", xout_ap[t0:t0 + 128, :], xrv,
                                    waits=[a1])
                    xres_cons[0] = so
        hb_cons[0] = Tok(ctx.sems["tensor"], ctx.sems["tensor"].n)
        rt_cons[0] = hb_cons[0]
    return Tok(ctx.sems["sync"], ctx.sems["sync"].n)


def emit_l1_attn(ctx, com, x_ap, xn16_ap, w, xout_ap, mem=None, lnf=None):
    if mem is None:
        mem = ctx.sbuf
    nc = ctx.nc
    C = 128
    NCH = S // C
    id16, id16_tok = com.ident16()
    idf, idf_tok = com.const("ident128")
    stril, m1 = com.const("stril")
    sutril, m2 = com.const("sutril")
    triu0, m4 = com.const("triu0")
    Dstrict, m5 = com.const("Dstrict")
    DstrictT, m6 = com.const("DstrictT")
    DinclT, m7 = com.const("DinclT")
    iota, m8 = com.const("iota")
    e127, m9 = com.const("e127")
    ctoks = [m1, m2, m4, m5, m6, m7, m8, m9, idf_tok, id16_tok]

    pk = mem("t_pk", [128, 8 * 64], BF16)
    pv = mem("t_pv", [128, 8 * 64], BF16)
    pq = mem("t_pq", [128, 8 * 64], BF16)
    pir = mem("t_pir", [128, 8 * 64], BF16)
    ppw = mem("t_ppw", [128, 8 * 3], BF16)
    nm1 = mem("t_nm1", [64, 128], BF16)
    nm2 = mem("t_nm2", [128, 64], BF16)
    wo1 = mem("t_wo1", [64, 1024], BF16)
    rows = mem("t_rows", [1, 512], F32)
    fa_sb = mem("t_fa", [1, 2], F32)
    for k in range(8):
        ctx.dma("gpsimd", pk[:, k * 64:(k + 1) * 64],
                w["WkT1"][k * 128:(k + 1) * 128, :])
        ctx.dma("gpsimd", pv[:, k * 64:(k + 1) * 64],
                w["WvT1"][k * 128:(k + 1) * 128, :])
        ctx.dma("gpsimd", pq[:, k * 64:(k + 1) * 64],
                w["WqT1"][k * 128:(k + 1) * 128, :])
        ctx.dma("gpsimd", pir[:, k * 64:(k + 1) * 64],
                w["ir1T"][k * 128:(k + 1) * 128, :])
        ctx.dma("gpsimd", ppw[:, k * 3:(k + 1) * 3],
                w["pwT"][k * 128:(k + 1) * 128, :])
    ctx.dma("gpsimd", nm1[:], w["nm1T"][:])
    ctx.dma("gpsimd", nm2[:], w["nm2T"][:])
    ctx.dma("gpsimd", wo1[:], w["WoT1"][:])
    ctx.dma("sync", rows[:, 0:64], w["ir_b1"][:])
    ctx.dma("sync", rows[:, 64:192], w["nm_b1"][:])
    ctx.dma("sync", rows[:, 192:256], w["nm_b2"][:])
    ctx.dma("sync", rows[:, 256:320], w["ir_w2"][:])
    ctx.dma("sync", rows[:, 320:384], w["norm_w"][:])
    ctx.dma("sync", rows[:, 384:385], w["ir_b2"][:])
    ctx.dma("sync", fa_sb[:, 0:1], w["fa"][:])
    ctx.dma("sync", fa_sb[:, 1:2], w["sa"][:])
    wtok = [Tok(ctx.sems["gpsimd"], ctx.sems["gpsimd"].n),
            Tok(ctx.sems["sync"], ctx.sems["sync"].n)]

    def bcast_col(src_1x1, name, waits):
        col = mem("t_col_" + name, [128, 1], F32)
        pb, pi = ctx.bank()
        mt = ctx.op("tensor", lambda e, pb=pb: e.matmul(
            pb[:, 0:1], com.ones_row[:, 0:128], src_1x1, start=True, stop=True),
            waits=list(waits) + [com.init_tok, ctx.ptok[pi]])
        cp = ctx.op("vector", lambda e, pb=pb: e.tensor_copy(col[:], pb[:, 0:1]),
                    waits=[mt])
        ctx.ptok[pi] = cp
        return col, cp

    def bcast_rep(src_row, npart, name, waits, dtype=F32):
        n = src_row.shape[1]
        rep = mem("t_rep_" + name, [npart, n], dtype)
        pb, pi = ctx.bank()
        mt = ctx.op("tensor", lambda e, pb=pb: e.matmul(
            pb[0:npart, 0:n], com.ones_row[:, 0:npart], src_row,
            start=True, stop=True),
            waits=list(waits) + [com.init_tok, ctx.ptok[pi]])
        cp = ctx.op("vector", lambda e, pb=pb: e.tensor_copy(
            rep[:], pb[0:npart, 0:n]), waits=[mt])
        ctx.ptok[pi] = cp
        return rep, cp

    sc_a = mem("t_sca", [1, 16], F32)

    def alpha_pack(which, a_src, tag):
        P = {}
        base = which * 8
        lnA = sc_a[:, base:base + 1]
        aC = sc_a[:, base + 1:base + 2]
        t1 = ctx.op("scalar", lambda e: e.activation(lnA, a_src, AF.Ln),
                    waits=wtok + [com.init_tok])
        t2 = ctx.op("scalar", lambda e: e.activation(aC, lnA, AF.Exp,
                                                     scale=128.0), waits=[t1])
        lncol, c1 = bcast_col(lnA, tag + "_ln", [t1])
        aCcol, c2 = bcast_col(aC, tag + "_aC", [t2])
        for nm, Dc, msk in (("Ts", Dstrict, stril), ("TsT", DstrictT, sutril),
                            ("TiT", DinclT, triu0)):
            buf = mem(f"t_{tag}_{nm}", [128, 128], F32)
            a1 = ctx.op("scalar", lambda e, buf=buf, Dc=Dc: e.activation(
                buf[:], Dc[:], AF.Exp, scale=lncol[:]), waits=[c1] + ctoks)
            a2 = ctx.op("vector", lambda e, buf=buf, msk=msk: e.tensor_mul(
                buf[:], buf[:], msk[:]), waits=[a1])
            P[nm] = (buf, a2)
        apow = mem(f"t_{tag}_apow", [128, 3], F32)
        p1 = ctx.op("scalar", lambda e: e.activation(
            apow[:, 0:1], iota[:], AF.Exp, scale=lncol[:]), waits=[c1] + ctoks)
        p2 = ctx.op("scalar", lambda e: e.activation(
            apow[:, 1:2], iota[:], AF.Exp, scale=lncol[:], bias=lncol[:]),
            waits=[c1] + ctoks)
        m127 = sc_a[:, base + 2:base + 3]
        mneg = sc_a[:, base + 3:base + 4]
        q1 = ctx.op("vector", lambda e: e.tensor_scalar(
            m127, lnA, 127.0, None, OP.mult), waits=[t1])
        q2 = ctx.op("vector", lambda e: e.tensor_scalar(
            mneg, lnA, -1.0, None, OP.mult), waits=[t1])
        m127c, c3 = bcast_col(m127, tag + "_127", [q1])
        mnegc, c4 = bcast_col(mneg, tag + "_neg", [q2])
        p3 = ctx.op("scalar", lambda e: e.activation(
            apow[:, 2:3], iota[:], AF.Exp, scale=mnegc[:], bias=m127c[:]),
            waits=[c3, c4] + ctoks)
        P["apow"] = (apow, [p1, p2, p3])
        omb = sc_a[:, base + 4:base + 5]
        b1 = ctx.op("vector", lambda e: e.tensor_scalar(
            omb, a_src, -1.0, 1.0, OP.mult, OP.add), waits=wtok)
        ombcol, c5 = bcast_col(omb, tag + "_omb", [b1])
        P["ombcol"] = (ombcol, c5)
        nomb = mem(f"t_{tag}_nomb", [128, 1], F32)
        n1 = ctx.op("vector", lambda e: e.tensor_scalar(
            nomb[:], ombcol[:], -1.0, None, OP.mult), waits=[c5])
        P["nomb"] = (nomb, n1)
        iac = mem(f"t_{tag}_iac", [64, 64], BF16)
        i1 = ctx.op("vector", lambda e: e.tensor_scalar(
            iac[:], idf[0:64, 0:64], aCcol[0:64, :], None, OP.mult),
            waits=[c2] + ctoks)
        P["iac"] = (iac, i1)
        rsf = mem(f"t_{tag}_rsf", [128, 1], F32)
        r1 = ctx.op("vector", lambda e: e.scalar_tensor_tensor(
            rsf[:], apow[:, 0:1], -1.0, ombcol[:], OP.mult, OP.mult),
            waits=[p1, c5])
        P["rsf"] = (rsf, r1)
        prow = mem(f"t_{tag}_prow", [1, 2 * 128], F32)
        etoks = []
        for jj in range(2):
            pb, pi = ctx.bank()
            tr = ctx.op("tensor", lambda e, pb=pb, jj=jj: e.transpose(
                pb[0:1, 0:128], apow[:, jj:jj + 1], idf[:]),
                waits=[p1, p2, ctx.ptok[pi]] + ctoks)
            e1 = ctx.op("vector", lambda e, pb=pb, jj=jj: e.tensor_copy(
                prow[:, jj * 128:(jj + 1) * 128], pb[0:1, 0:128]), waits=[tr])
            ctx.ptok[pi] = e1
            etoks.append(e1)
        rep0, e2 = bcast_rep(prow[:, 0:128], 64, tag + "_ap0", etoks, BF16)
        rep1, e3 = bcast_rep(prow[:, 128:256], 64, tag + "_ap1", etoks, BF16)
        P["aprep"] = ((rep0, rep1), [e2, e3])
        return P

    PF = alpha_pack(0, fa_sb[:, 0:1], "fa")
    PS = alpha_pack(1, fa_sb[:, 1:2], "sa")
    omsa_col, omsa_tok = PS["ombcol"]
    w2rep, w2tok = bcast_rep(rows[:, 256:320], 128, "w2", wtok)
    nwrep, nwtok = bcast_rep(rows[:, 320:384], 128, "nw", wtok)
    b2col, b2tok = bcast_col(rows[:, 384:385], "irb2", wtok)
    ptoks = ([w2tok, nwtok, b2tok, omsa_tok, com.init_tok] + ctoks + wtok)

    Ocf = mem("t_Ocf", [128, NCH * 64], F32)
    Ocs = mem("t_Ocs", [128, NCH * 64], F32)
    nu_all = mem("t_nu", [128, NCH * 64], F32)
    v_all = mem("t_v", [128, NCH * 64], F32)
    pw_all = mem("t_pwall", [128, NCH * 3], F32)
    u01_all = mem("t_u01", [128, NCH], F32)
    Ssb = mem("t_S", [64, 4 * 64], BF16)
    sz = ctx.op("gpsimd", lambda e: e.memset(Ssb[:], 0.0))

    xnT = mem("t_xnT", [128, 2 * 8 * 512], BF16)
    NB = 2
    kT = mem("t_kT", [64, NB * 128], BF16)
    qT = mem("t_qT", [64, NB * 128], BF16)
    vT = mem("t_vT", [64, NB * 128], BF16)
    k16 = mem("t_k16", [128, NB * 64], BF16)
    q16 = mem("t_q16", [128, NB * 64], BF16)
    v16 = mem("t_v16", [128, NB * 64], BF16)
    qTs = mem("t_qTs", [64, NB * 4 * 128], BF16)
    Ksc = mem("t_Ksc", [128, NB * 2 * 64], BF16)
    Vb = mem("t_Vb", [128, NB * 2 * 64], F32)
    rs_s = mem("t_rss", [128, NB], F32)
    bscol = mem("t_bscol", [128, NB], F32)
    nbscol = mem("t_nbscol", [128, NB], F32)
    bsrow = mem("t_bsrow", [1, NB * 128], F32)
    brep = mem("t_brep", [128, NB * 128], F32)
    Gm = mem("t_G", [128, NB * 128], F32)
    MT = mem("t_MT", [128, NB * 4 * 128], BF16)
    Bp = mem("t_Bp", [128, NB * 2 * 128], BF16)
    BpT = mem("t_BpT", [128, NB * 2 * 128], BF16)
    PTb = mem("t_PTb", [128, NB * 2 * 128], BF16)
    scr = mem("t_scr", [128, NB * 128], F32)
    scr2 = mem("t_scr2", [128, NB * 128], F32)
    ev1 = mem("t_ev1", [128, NB * 64], F32)
    ev2 = mem("t_ev2", [128, NB * 64], F32)
    sm = mem("t_sm", [128, NB * 8], F32)
    pws = mem("t_pws", [128, NB * 8], F32)
    h1g16 = mem("t_h1g16", [128, NB * 128], BF16)
    h1gT = mem("t_h1gT", [128, NB * 128], BF16)
    U16 = mem("t_U16", [128, NB * 2 * 64], BF16)
    Z16 = mem("t_Z16", [128, NB * 2 * 64], BF16)
    Ofp = mem("t_Ofp", [128, NB * 64], F32)
    errb = mem("t_errb", [128, NB * 64], F32)

    lu = {}
    xg_toks = [None, None]
    xg_cons = [None, None]

    def xgview(slot):
        return xnT[:, slot * 4096:(slot + 1) * 4096]

    def xgprefetch(g):
        if g >= NG:
            return
        xg_toks[g % 2] = load_xnT(ctx, xgview(g % 2), xn16_ap, g,
                                  waits=[xg_cons[g % 2]])
    xgprefetch(0)

    def P0(c):
        b = c % NB
        g, ti = divmod(c, 4)
        xslot = g % 2
        xtoks = xg_toks[xslot]
        xv = xgview(xslot)
        xofc = (ti // 2) * 2048 + (ti % 2) * 128
        xsl = lambda k: xv[:, xofc + k * 256: xofc + k * 256 + 128]

        def proj(wtiles, n, bias_row=None):
            pb, pi = ctx.bank()
            pairs = [(xsl(k), wtiles[:, k * n:(k + 1) * n]) for k in range(8)]
            if bias_row is not None:
                pairs.append((com.ones_row[:, 0:128], bias_row))
            mt = ctx.mm_chain(pb[:, 0:n], pairs,
                              waits=xtoks + ptoks + [ctx.ptok[pi]])
            return pb, pi, mt

        def l2norm16(pb, pi, mt, out16, war16):
            ff = ev1[:, b * 64:(b + 1) * 64]
            smv = sm[:, b * 8:(b + 1) * 8]
            c1 = ctx.op("vector", lambda e, pb=pb, ff=ff: e.tensor_copy(
                ff, pb[:, 0:64]), waits=[mt, lu.get(("ev1", b))])
            ctx.ptok[pi] = c1
            s1 = ctx.op("scalar", lambda e, ff=ff: e.activation(
                errb[:, b * 64:(b + 1) * 64], ff, AF.Square),
                waits=[c1, lu.get(("errb", b))])
            s2 = ctx.op("vector", lambda e, smv=smv: e.reduce_sum(
                smv[:, 0:1], errb[:, b * 64:(b + 1) * 64], axis=AXX),
                waits=[s1])
            s3 = ctx.op("scalar", lambda e, smv=smv: e.activation(
                smv[:, 1:2], smv[:, 0:1], AF.Sqrt), waits=[s2])
            s4 = ctx.op("vector", lambda e, smv=smv: e.tensor_single_scalar(
                smv[:, 2:3], smv[:, 1:2], 1e-12, OP.max), waits=[s3])
            s5 = ctx.op("vector", lambda e, smv=smv: e.reciprocal(
                smv[:, 3:4], smv[:, 2:3]), waits=[s4])
            s6 = ctx.op("vector", lambda e, ff=ff, smv=smv, out16=out16:
                        e.tensor_scalar(out16, ff, smv[:, 3:4], None, OP.mult),
                        waits=[s5, war16])
            lu[("ev1", b)] = s6
            lu[("errb", b)] = s2
            return s6

        def tr64(src16, dstT, deps, dst_key, src_key):
            pb2, pi2 = ctx.bank()
            ptr = pb2[0:64, 0:64].bitcast(BF16)
            t = ctx.op("tensor", lambda e, ptr=ptr, src16=src16: e.transpose(
                ptr, src16, id16[:]), waits=deps + [ctx.ptok[pi2], id16_tok])
            cc = ctx.op("vector", lambda e, ptr=ptr, dstT=dstT: e.tensor_copy(
                dstT, ptr), waits=[t, lu.get(dst_key)])
            ctx.ptok[pi2] = cc
            lu[src_key] = t
            return t, cc

        pb, pi, mt = proj(pk, 64)
        k16v = k16[:, b * 64:(b + 1) * 64]
        kk = l2norm16(pb, pi, mt, k16v, lu.get(("k16", b)))
        kTv = kT[:, b * 128:(b + 1) * 128]
        _tt, kTt = tr64(k16v, kTv, [kk], ("kT", b), ("k16", b))
        ksf = Ksc[:, (b * 2) * 64:(b * 2 + 1) * 64]
        kss = Ksc[:, (b * 2 + 1) * 64:(b * 2 + 2) * 64]
        kc1 = ctx.op("vector", lambda e: e.tensor_scalar(
            ksf, k16v, PF["apow"][0][:, 2:3], None, OP.mult),
            waits=[kk, lu.get(("Ksc", b))] + PF["apow"][1])
        kc2 = ctx.op("vector", lambda e: e.tensor_scalar(
            kss, k16v, PS["apow"][0][:, 2:3], None, OP.mult),
            waits=[kk] + PS["apow"][1])
        lu[("Ksc_w", b)] = kc2
        pb, pi, mt = proj(pq, 64)
        q16v = q16[:, b * 64:(b + 1) * 64]
        qq = l2norm16(pb, pi, mt, q16v, lu.get(("q16", b)))
        qTv = qT[:, b * 128:(b + 1) * 128]
        _tt, qTt = tr64(q16v, qTv, [qq], ("qT", b), ("q16", b))
        qs_toks = []
        for st in range(2):
            PX = PF if st == 0 else PS
            for jj in range(2):
                dst = qTs[:, (b * 4 + st * 2 + jj) * 128:
                          (b * 4 + st * 2 + jj + 1) * 128]
                tk = ctx.op("vector", lambda e, dst=dst, qTv=qTv, PX=PX, jj=jj:
                            e.tensor_mul(dst, qTv, PX["aprep"][0][jj][:]),
                            waits=[qTt, lu.get(("qTs", b))] + PX["aprep"][1])
                qs_toks.append(tk)
        lu[("qTs_w", b)] = qs_toks[-1]
        pb, pi, mt = proj(pv, 64)
        vav = v_all[:, c * 64:(c + 1) * 64]
        vc1 = ctx.op("vector", lambda e, pb=pb, vav=vav: e.tensor_copy(
            vav, pb[:, 0:64]), waits=[mt])
        v16v = v16[:, b * 64:(b + 1) * 64]
        vc2 = ctx.op("vector", lambda e, pb=pb, v16v=v16v: e.tensor_copy(
            v16v, pb[:, 0:64]), waits=[mt, lu.get(("v16", b))])
        ctx.ptok[pi] = vc2
        vTv = vT[:, b * 128:(b + 1) * 128]
        _tt, vTt = tr64(v16v, vTv, [vc2], ("vT", b), ("v16", b))
        vbf = Vb[:, (b * 2) * 64:(b * 2 + 1) * 64]
        vb1 = ctx.op("vector", lambda e, vav=vav, vbf=vbf: e.tensor_scalar(
            vbf, vav, PF["ombcol"][0][:], None, OP.mult),
            waits=[vc1, PF["ombcol"][1], lu.get(("Vb", b))])
        pb, pi, mt = proj(pir, 64, bias_row=rows[:, 0:64])
        hh = ev1[:, b * 64:(b + 1) * 64]
        i1 = ctx.op("scalar", lambda e, pb=pb, hh=hh: e.activation(
            hh, pb[:, 0:64], AF.Gelu), waits=[mt, lu.get(("ev1", b))])
        ctx.ptok[pi] = i1
        i2 = ctx.op("vector", lambda e, hh=hh: e.tensor_mul(
            ev2[:, b * 64:(b + 1) * 64], hh, w2rep[:]),
            waits=[i1, lu.get(("ev2", b))])
        lu[("ev1", b)] = i2
        smv = sm[:, b * 8:(b + 1) * 8]
        i3 = ctx.op("vector", lambda e, smv=smv: e.reduce_sum(
            smv[:, 4:5], ev2[:, b * 64:(b + 1) * 64], axis=AXX), waits=[i2])
        lu[("ev2", b)] = i3
        i4 = ctx.op("vector", lambda e, smv=smv: e.tensor_add(
            smv[:, 5:6], smv[:, 4:5], b2col[:]), waits=[i3])
        i5 = ctx.op("scalar", lambda e, smv=smv: e.activation(
            smv[:, 6:7], smv[:, 5:6], AF.Sigmoid), waits=[i4])
        i6 = ctx.op("vector", lambda e, smv=smv: e.tensor_single_scalar(
            smv[:, 7:8], smv[:, 6:7], 0.5, OP.is_gt), waits=[i5])
        bsv = bscol[:, b:b + 1]
        i7 = ctx.op("vector", lambda e, smv=smv, bsv=bsv: e.tensor_scalar(
            bsv, smv[:, 7:8], omsa_col[:], None, OP.mult),
            waits=[i6, lu.get(("bscol", b))])
        i8 = ctx.op("vector", lambda e, bsv=bsv: e.tensor_scalar(
            nbscol[:, b:b + 1], bsv, -1.0, None, OP.mult), waits=[i7])
        i9 = ctx.op("vector", lambda e, bsv=bsv: e.scalar_tensor_tensor(
            rs_s[:, b:b + 1], PS["apow"][0][:, 0:1], -1.0, bsv,
            OP.mult, OP.mult), waits=[i7] + PS["apow"][1])
        vbs = Vb[:, (b * 2 + 1) * 64:(b * 2 + 2) * 64]
        i10 = ctx.op("vector", lambda e, vbs=vbs, vav=vav, bsv=bsv:
                     e.tensor_scalar(vbs, vav, bsv, None, OP.mult),
                     waits=[i7, vc1])
        lu[("Vb_w", b)] = i10
        pb, pi = ctx.bank()
        mtn = ctx.mm_chain(pb[:, 0:128],
                           [(vTv, nm1[:]), (com.ones_row[:, 0:128],
                                            rows[:, 64:192])],
                           waits=[vTt, ctx.ptok[pi]] + ptoks)
        lu[("vT", b)] = mtn
        n1 = ctx.op("scalar", lambda e, pb=pb: e.activation(
            h1g16[:, b * 128:(b + 1) * 128], pb[:, 0:128], AF.Gelu),
            waits=[mtn, lu.get(("h1g16", b))])
        ctx.ptok[pi] = n1
        pb2, pi2 = ctx.bank()
        ptr = pb2[:, 0:64].bitcast(BF16)
        n2 = ctx.op("tensor", lambda e, ptr=ptr: e.transpose(
            ptr, h1g16[:, b * 128:(b + 1) * 128], id16[:]),
            waits=[n1, ctx.ptok[pi2], id16_tok])
        n3 = ctx.op("vector", lambda e, ptr=ptr: e.tensor_copy(
            h1gT[:, b * 128:(b + 1) * 128], ptr),
            waits=[n2, lu.get(("h1gT", b))])
        ctx.ptok[pi2] = n3
        lu[("h1g16", b)] = n2
        pb3, pi3 = ctx.bank()
        mtn2 = ctx.mm_chain(pb3[:, 0:64],
                            [(h1gT[:, b * 128:(b + 1) * 128], nm2[:]),
                             (com.ones_row[:, 0:128], rows[:, 192:256])],
                            waits=[n3, ctx.ptok[pi3]] + ptoks)
        n4 = ctx.op("vector", lambda e, pb3=pb3: e.tensor_copy(
            nu_all[:, c * 64:(c + 1) * 64], pb3[:, 0:64]), waits=[mtn2])
        ctx.ptok[pi3] = n4
        lu[("h1gT", b)] = mtn2
        pb, pi, mt = proj(ppw, 3)
        pv3 = pws[:, b * 8:b * 8 + 3]
        pm = pws[:, b * 8 + 3:b * 8 + 4]
        g1 = ctx.op("vector", lambda e, pb=pb, pv3=pv3: e.tensor_copy(
            pv3, pb[:, 0:3]), waits=[mt, lu.get(("pws", b))])
        ctx.ptok[pi] = g1
        g2 = ctx.op("vector", lambda e, pv3=pv3, pm=pm: e.reduce_max(
            pm, pv3, axis=AXX), waits=[g1])
        g3 = ctx.op("vector", lambda e, pv3=pv3, pm=pm: e.tensor_scalar(
            pv3, pv3, pm, None, OP.subtract), waits=[g2])
        g4 = ctx.op("scalar", lambda e, pv3=pv3: e.activation(
            pv3, pv3, AF.Exp), waits=[g3])
        g5 = ctx.op("vector", lambda e, pv3=pv3, pm=pm: e.reduce_sum(
            pm, pv3, axis=AXX), waits=[g4])
        g6 = ctx.op("vector", lambda e, pm=pm: e.reciprocal(pm, pm),
                    waits=[g5])
        g7 = ctx.op("vector", lambda e, pv3=pv3, pm=pm, c=c: e.tensor_scalar(
            pw_all[:, c * 3:(c + 1) * 3], pv3, pm, None, OP.mult),
            waits=[g6])
        lu[("pws", b)] = g7
        pb, pi = ctx.bank()
        mtg = ctx.mm_chain(pb[:, 0:128], [(kTv, kTv)],
                           waits=[kTt, ctx.ptok[pi]])
        gg = ctx.op("vector", lambda e, pb=pb: e.tensor_copy(
            Gm[:, b * 128:(b + 1) * 128], pb[:, 0:128]),
            waits=[mtg, lu.get(("Gm", b))])
        ctx.ptok[pi] = gg
        pb2, pi2 = ctx.bank()
        mtq = ctx.mm_chain(pb2[:, 0:128], [(kTv, qTv)],
                           waits=[kTt, qTt, ctx.ptok[pi2]])
        gq = scr[:, b * 128:(b + 1) * 128]
        gqe = ctx.op("scalar", lambda e, gq=gq, pb2=pb2: e.activation(
            gq, pb2[:, 0:128], AF.Copy), waits=[mtq, lu.get(("scr", b))])
        ctx.ptok[pi2] = gqe
        mt_toks = []
        for st in range(2):
            PX = PF if st == 0 else PS
            for jj, nm in ((0, "TiT"), (1, "TsT")):
                dst = MT[:, (b * 4 + st * 2 + jj) * 128:
                         (b * 4 + st * 2 + jj + 1) * 128]
                tk = ctx.op("gpsimd", lambda e, dst=dst, gq=gq, PX=PX, nm=nm:
                            e.tensor_tensor(dst, gq, PX[nm][0][:], OP.mult),
                            waits=[gqe, PX[nm][1], lu.get(("MT", b))])
                mt_toks.append(tk)
        lu[("scr", b)] = mt_toks[-1]
        lu[("qT", b)] = mtq
        bsl = lambda buf, st: buf[:, (b * 2 + st) * 128:(b * 2 + st + 1) * 128]
        Gv = Gm[:, b * 128:(b + 1) * 128]
        wpair = [None, None]
        for st in range(2):
            Bv, BTv = bsl(Bp, st), bsl(BpT, st)
            if st == 0:
                w1 = ctx.op("vector", lambda e, Bv=Bv, Gv=Gv: e.scalar_tensor_tensor(
                    Bv, Gv, PF["nomb"][0][:], PF["Ts"][0][:],
                    OP.mult, OP.mult),
                    waits=[gg, PF["nomb"][1], PF["Ts"][1], lu.get(("Bp", b, st))])
                w2 = ctx.op("vector", lambda e, BTv=BTv, Gv=Gv: e.scalar_tensor_tensor(
                    BTv, Gv, PF["nomb"][0][:], PF["TsT"][0][:],
                    OP.mult, OP.mult),
                    waits=[gg, PF["nomb"][1], PF["TsT"][1], lu.get(("BpT", b, st))])
            else:
                w1 = ctx.op("vector", lambda e, Bv=Bv, Gv=Gv: e.scalar_tensor_tensor(
                    Bv, Gv, nbscol[:, b:b + 1], PS["Ts"][0][:],
                    OP.mult, OP.mult),
                    waits=[gg, i8, PS["Ts"][1], lu.get(("Bp", b, st))])
                pbr, pir_ = ctx.bank()
                tr2 = ctx.op("tensor", lambda e, pbr=pbr: e.transpose(
                    pbr[0:1, 0:128], bscol[:, b:b + 1], idf[:]),
                    waits=[i7, ctx.ptok[pir_]] + ctoks)
                bsr = bsrow[:, b * 128:(b + 1) * 128]
                c1_ = ctx.op("vector", lambda e, pbr=pbr, bsr=bsr: e.tensor_copy(
                    bsr, pbr[0:1, 0:128]), waits=[tr2, lu.get(("bsrow", b))])
                ctx.ptok[pir_] = c1_
                pbr2, pir2 = ctx.bank()
                mtr = ctx.op("tensor", lambda e, pbr2=pbr2, bsr=bsr: e.matmul(
                    pbr2[:, 0:128], com.ones_row[:, 0:128], bsr,
                    start=True, stop=True),
                    waits=[c1_, com.init_tok, ctx.ptok[pir2]])
                brv = brep[:, b * 128:(b + 1) * 128]
                c2_ = ctx.op("vector", lambda e, pbr2=pbr2, brv=brv: e.tensor_copy(
                    brv, pbr2[:, 0:128]), waits=[mtr, lu.get(("brep", b))])
                ctx.ptok[pir2] = c2_
                lu[("bsrow", b)] = c2_
                scv = scr2[:, b * 128:(b + 1) * 128]
                u1 = ctx.op("gpsimd", lambda e, Gv=Gv, scv=scv: e.tensor_tensor(
                    scv, Gv, PS["TsT"][0][:], OP.mult),
                    waits=[gg, PS["TsT"][1], lu.get(("scr2", b))])
                w2 = ctx.op("vector", lambda e, BTv=BTv, brv=brv, scv=scv:
                            e.scalar_tensor_tensor(BTv, scv, -1.0, brv,
                                                   OP.mult, OP.mult),
                    waits=[u1, c2_, lu.get(("BpT", b, st))])
                lu[("brep", b)] = w2
                lu[("scr2", b)] = w2
            PTv = bsl(PTb, st)
            p1_ = ctx.op("vector", lambda e, PTv=PTv, BTv=BTv: e.tensor_add(
                PTv, BTv, id16[:]), waits=[w2, id16_tok, lu.get(("PTb", b, st))])
            wpair[st] = (w1, w2, p1_)
        lastB = [wpair[0][0], wpair[1][0]]
        lastBT = [wpair[0][1], wpair[1][1]]
        lastP = [wpair[0][2], wpair[1][2]]
        inv_done = [None, None]
        for kk_ in range(6):
            stage = []
            for st in range(2):
                Bv, BTv, PTv = bsl(Bp, st), bsl(BpT, st), bsl(PTb, st)
                pbA, piA = ctx.bank()
                mA = ctx.mm_chain(pbA[:, 0:128], [(BTv, Bv)],
                                  waits=[lastB[st], lastBT[st], ctx.ptok[piA]])
                pbB, piB = ctx.bank()
                mB = ctx.mm_chain(pbB[:, 0:128], [(Bv, BTv)],
                                  waits=[lastB[st], lastBT[st], ctx.ptok[piB]])
                stage.append((st, Bv, BTv, PTv, pbA, piA, mA, pbB, piB, mB))
            for (st, Bv, BTv, PTv, pbA, piA, mA, pbB, piB, mB) in stage:
                cA = ctx.op("scalar", lambda e, pbA=pbA, Bv=Bv: e.activation(
                    Bv, pbA[:, 0:128], AF.Copy), waits=[mA, mB])
                ctx.ptok[piA] = cA
                cB = ctx.op("vector", lambda e, pbB=pbB, BTv=BTv: e.tensor_copy(
                    BTv, pbB[:, 0:128]), waits=[mB, mA])
                ctx.ptok[piB] = cB
                lastB[st], lastBT[st] = cA, cB
            for (st, Bv, BTv, PTv, pbA, piA, mA, pbB, piB, mB) in stage:
                pbC, piC = ctx.bank()
                mC = ctx.mm_chain(pbC[:, 0:128],
                                  [(id16[:], PTv), (Bv, PTv)],
                                  waits=[lastB[st], lastP[st], ctx.ptok[piC],
                                         id16_tok])
                cC = ctx.op("vector", lambda e, pbC=pbC, PTv=PTv:
                            e.tensor_copy(PTv, pbC[:, 0:128]), waits=[mC])
                ctx.ptok[piC] = cC
                lastP[st] = cC
        for st in range(2):
            lu[("Bp", b, st)] = lastP[st]
            lu[("BpT", b, st)] = lastP[st]
            inv_done[st] = lastP[st]
        if ti == 3:
            xg_cons[xslot] = Tok(ctx.sems["tensor"], ctx.sems["tensor"].n)
            xgprefetch(g + 1)
        return dict(kT=kTt, Ksc=[kc1, kc2], qTs=qs_toks, Vb=[vb1, i10],
                    rs=[PF["rsf"][1], i9], inv=inv_done,
                    MT=mt_toks, v=vc1)

    def SEQ(c, d):
        b = c % NB
        ph = c % 2
        kTv = kT[:, b * 128:(b + 1) * 128]
        o_toks = {}
        for st in range(2):
            PX = PF if st == 0 else PS
            Sold = Ssb[:, (st * 2 + ph) * 64:(st * 2 + ph + 1) * 64]
            Snew = Ssb[:, (st * 2 + (ph ^ 1)) * 64:(st * 2 + (ph ^ 1) + 1) * 64]
            PTv = PTb[:, (b * 2 + st) * 128:(b * 2 + st + 1) * 128]
            Uv = U16[:, (b * 2 + st) * 64:(b * 2 + st + 1) * 64]
            Zv = Z16[:, (b * 2 + st) * 64:(b * 2 + st + 1) * 64]
            Vbv = Vb[:, (b * 2 + st) * 64:(b * 2 + st + 1) * 64]
            rsv = PF["rsf"][0][:] if st == 0 else rs_s[:, b:b + 1]
            Kscv = Ksc[:, (b * 2 + st) * 64:(b * 2 + st + 1) * 64]
            pb, pi = ctx.bank()
            m1_ = ctx.mm_chain(pb[:, 0:64], [(kTv, Sold)],
                               waits=[d["kT"], lu.get(("S", st, ph), sz),
                                      ctx.ptok[pi]])
            z1 = ctx.op("vector", lambda e, pb=pb, Zv=Zv, rsv=rsv, Vbv=Vbv:
                        e.scalar_tensor_tensor(Zv, pb[:, 0:64], rsv, Vbv,
                                               OP.mult, OP.add),
                        waits=[m1_, d["Vb"][st], d["rs"][st],
                               lu.get(("Z16", b, st))])
            ctx.ptok[pi] = z1
            pb2, pi2 = ctx.bank()
            m2_ = ctx.mm_chain(pb2[:, 0:64], [(PTv, Zv)],
                               waits=[z1, d["inv"][st], ctx.ptok[pi2]])
            u1 = ctx.op("vector", lambda e, pb2=pb2, Uv=Uv: e.tensor_copy(
                Uv, pb2[:, 0:64]), waits=[m2_, lu.get(("U16", b, st))])
            ctx.ptok[pi2] = u1
            lu[("Z16", b, st)] = m2_
            pb3, pi3 = ctx.bank()
            m3_ = ctx.mm_chain(pb3[0:64, 0:64],
                               [(PX["iac"][0][:], Sold), (Kscv, Uv)],
                               waits=[u1, PX["iac"][1], d["Ksc"][st],
                                      lu.get(("S", st, ph), sz),
                                      ctx.ptok[pi3]])
            s1 = ctx.op("vector", lambda e, pb3=pb3, Snew=Snew: e.tensor_copy(
                Snew, pb3[0:64, 0:64]),
                waits=[m3_, lu.get(("Snew", st, ph ^ 1), sz)])
            ctx.ptok[pi3] = s1
            lu[("S", st, ph ^ 1)] = s1
            qcur = qTs[:, (b * 4 + st * 2 + 1) * 128:(b * 4 + st * 2 + 2) * 128]
            qprev = qTs[:, (b * 4 + st * 2) * 128:(b * 4 + st * 2 + 1) * 128]
            MTv = MT[:, (b * 4 + st * 2) * 128:(b * 4 + st * 2 + 1) * 128]
            MpTv = MT[:, (b * 4 + st * 2 + 1) * 128:(b * 4 + st * 2 + 2) * 128]
            pb4, pi4 = ctx.bank()
            m4_ = ctx.mm_chain(pb4[:, 0:64], [(qcur, Sold), (MTv, Uv)],
                               waits=[u1, d["qTs"][st * 2 + 1],
                                      d["MT"][st * 2],
                                      lu.get(("S", st, ph), sz),
                                      ctx.ptok[pi4]])
            dst = (Ocf if st == 0 else Ocs)[:, c * 64:(c + 1) * 64]
            o1 = ctx.op("vector", lambda e, pb4=pb4, dst=dst: e.tensor_copy(
                dst, pb4[:, 0:64]), waits=[m4_])
            ctx.ptok[pi4] = o1
            pb5, pi5 = ctx.bank()
            m5_ = ctx.mm_chain(pb5[:, 0:64], [(qprev, Sold), (MpTv, Uv)],
                               waits=[u1, d["qTs"][st * 2],
                                      d["MT"][st * 2 + 1],
                                      lu.get(("S", st, ph), sz),
                                      ctx.ptok[pi5]])
            if st == 0:
                of = Ofp[:, b * 64:(b + 1) * 64]
                o2 = ctx.op("vector", lambda e, pb5=pb5, of=of: e.tensor_copy(
                    of, pb5[:, 0:64]), waits=[m5_, lu.get(("Ofp", b))])
                ctx.ptok[pi5] = o2
                o_toks["ofp"] = o2
            else:
                o_toks["osp_mt"] = m5_
                o_toks["osp_pb"] = pb5
                o_toks["osp_pi"] = pi5
            lu[("Snew", st, ph)] = m5_
            lu[("kT", b)] = m1_
            lu[("Ksc", b)] = m3_
            lu[("qTs", b)] = m5_
            lu[("MT", b)] = m5_
            lu[("U16", b, st)] = m5_
            lu[("PTb", b, st)] = m2_
        b_ = b
        ebv = errb[:, b_ * 64:(b_ + 1) * 64]
        pb5 = o_toks["osp_pb"]
        e1_ = ctx.op("vector", lambda e, ebv=ebv, pb5=pb5: e.tensor_add(
            ebv, Ofp[:, b_ * 64:(b_ + 1) * 64], pb5[:, 0:64]),
            waits=[o_toks["ofp"], o_toks["osp_mt"], lu.get(("errb", b_))])
        ctx.ptok[o_toks["osp_pi"]] = e1_
        lu[("Ofp", b_)] = e1_
        e2_ = ctx.op("vector", lambda e, ebv=ebv, c=c: e.scalar_tensor_tensor(
            ebv, ebv, -0.5, v_all[:, c * 64:(c + 1) * 64], OP.mult, OP.add),
            waits=[e1_, d["v"]])
        e3_ = ctx.op("scalar", lambda e, ebv=ebv: e.activation(
            ebv, ebv, AF.Square), waits=[e2_])
        smv = sm[:, b_ * 8:(b_ + 1) * 8]
        e4_ = ctx.op("vector", lambda e, ebv=ebv, smv=smv: e.reduce_sum(
            smv[:, 4:5], ebv, axis=AXX), waits=[e3_, lu.get(("sm45", b_))])
        lu[("errb", b_)] = e4_
        e5_ = ctx.op("scalar", lambda e, smv=smv: e.activation(
            smv[:, 5:6], smv[:, 4:5], AF.Sigmoid, scale=1.0 / (1.0 + 1e-6)),
            waits=[e4_])
        e6_ = ctx.op("vector", lambda e, smv=smv: e.tensor_single_scalar(
            smv[:, 6:7], smv[:, 5:6], 0.7, OP.is_gt), waits=[e5_])
        e7_ = ctx.op("vector", lambda e, smv=smv, c=c: e.tensor_scalar(
            u01_all[:, c:c + 1], smv[:, 6:7], 0.1, None, OP.mult),
            waits=[e6_])
        lu[("sm45", b_)] = e7_

    dprev = None
    for c in range(NCH):
        d = P0(c)
        if dprev is not None:
            SEQ(c - 1, dprev)
        dprev = d
    SEQ(NCH - 1, dprev)

    ns0 = mem("t_ns0", [1, 2 * 64], F32)
    nz = ctx.op("gpsimd", lambda e: e.memset(ns0[:], 0.0))
    nsc = mem("t_nsc", [128, 8], F32)
    cprow = mem("t_cprow", [1, 2 * 128], F32)
    PT16 = mem("t_PT16", [128, 128], BF16)
    Dm16 = mem("t_Dm16", [128, 64], BF16)
    NSf = mem("t_NSf", [128, 64], F32)
    ob1 = mem("t_ob1", [128, 64], F32)
    osq = mem("t_osq", [128, 64], F32)
    o16 = mem("t_o16", [128, 64], BF16)
    oT = mem("t_oT", [64, 128], BF16)
    xres = mem("t_xres", [128, 2 * 1024], F32)
    xres_cons = [None, None]
    prev = {}
    for c in range(NCH):
        ph = c % 2
        u01c = u01_all[:, c:c + 1]
        a1 = ctx.op("vector", lambda e, u01c=u01c: e.tensor_scalar(
            nsc[:, 0:1], u01c, -1.0, 1.0, OP.mult, OP.add),
            waits=[Tok(ctx.sems["vector"], ctx.sems["vector"].n)
                   if c == 0 else prev.get("a1w")])
        a2 = ctx.op("scalar", lambda e: e.activation(
            nsc[:, 1:2], nsc[:, 0:1], AF.Ln), waits=[a1])
        pb, pi = ctx.bank()
        a3 = ctx.mm_chain(pb[:, 0:1], [(triu0[:], nsc[:, 1:2])],
                          waits=[a2, ctx.ptok[pi], m4])
        a4 = ctx.op("scalar", lambda e, pb=pb: e.activation(
            nsc[:, 2:3], pb[:, 0:1], AF.Exp), waits=[a3])
        ctx.ptok[pi] = a4
        a5 = ctx.op("vector", lambda e: e.reciprocal(nsc[:, 3:4], nsc[:, 2:3]),
                    waits=[a4])
        a7 = None
        for jj in range(2):
            pb2, pi2 = ctx.bank()
            a6 = ctx.op("tensor", lambda e, pb2=pb2, jj=jj: e.transpose(
                pb2[0:1, 0:128], nsc[:, 2 + jj:3 + jj], idf[:]),
                waits=[a5, ctx.ptok[pi2]] + ctoks)
            a7 = ctx.op("vector", lambda e, pb2=pb2, jj=jj: e.tensor_copy(
                cprow[:, jj * 128:(jj + 1) * 128], pb2[0:1, 0:128]),
                waits=[a6, prev.get("cprow")])
            ctx.ptok[pi2] = a7
        pb3, pi3 = ctx.bank()
        a8 = ctx.op("tensor", lambda e, pb3=pb3: e.matmul(
            pb3[:, 0:128], cprow[:, 128:256], cprow[:, 0:128],
            start=True, stop=True), waits=[a7, ctx.ptok[pi3]])
        a9 = ctx.op("vector", lambda e, pb3=pb3: e.tensor_tensor(
            PT16[:], pb3[:, 0:128], triu0[:], OP.mult),
            waits=[a8, m4, prev.get("PT16")])
        ctx.ptok[pi3] = a9
        a10 = ctx.op("vector", lambda e, c=c, u01c=u01c: e.tensor_scalar(
            Dm16[:], nu_all[:, c * 64:(c + 1) * 64], u01c, None, OP.mult),
            waits=[a1, prev.get("Dm16")])
        pb4, pi4 = ctx.bank()
        ns_old = ns0[:, ph * 64:(ph + 1) * 64]
        ns_new = ns0[:, (ph ^ 1) * 64:((ph ^ 1) + 1) * 64]
        a11 = ctx.mm_chain(pb4[:, 0:64],
                           [(PT16[:], Dm16[:]), (cprow[:, 0:128], ns_old)],
                           waits=[a9, a10, nz, prev.get("ns_old"),
                                  ctx.ptok[pi4]])
        a12 = ctx.op("vector", lambda e, pb4=pb4: e.tensor_copy(
            NSf[:], pb4[:, 0:64]), waits=[a11, prev.get("NSf")])
        ctx.ptok[pi4] = a12
        pb5, pi5 = ctx.bank()
        a13 = ctx.op("tensor", lambda e, pb5=pb5: e.matmul(
            pb5[0:1, 0:64], e127[:], NSf[:], start=True, stop=True),
            waits=[a12, m9, ctx.ptok[pi5]])
        a14 = ctx.op("vector", lambda e, pb5=pb5, ns_new=ns_new: e.tensor_copy(
            ns_new, pb5[0:1, 0:64]), waits=[a13, prev.get("ns_new")])
        ctx.ptok[pi5] = a14
        prev["ns_old"] = a11
        prev["ns_new"] = a14
        prev["a1w"] = a2
        prev["cprow"] = a11
        prev["PT16"] = a11
        prev["Dm16"] = a11
        prev["NSf"] = a13
        p0c = pw_all[:, c * 3:c * 3 + 1]
        p1c = pw_all[:, c * 3 + 1:c * 3 + 2]
        p2c = pw_all[:, c * 3 + 2:c * 3 + 3]
        b1_ = ctx.op("vector", lambda e, c=c, p0c=p0c: e.tensor_scalar(
            ob1[:], Ocf[:, c * 64:(c + 1) * 64], p0c, None, OP.mult),
            waits=[a12, prev.get("ob1")])
        b2_ = ctx.op("vector", lambda e, c=c, p1c=p1c: e.scalar_tensor_tensor(
            ob1[:], Ocs[:, c * 64:(c + 1) * 64], p1c, ob1[:],
            OP.mult, OP.add), waits=[b1_])
        b3_ = ctx.op("vector", lambda e, p2c=p2c: e.scalar_tensor_tensor(
            ob1[:], NSf[:], p2c, ob1[:], OP.mult, OP.add), waits=[b2_, a12])
        b4_ = ctx.op("scalar", lambda e: e.activation(osq[:], ob1[:], AF.Square),
                     waits=[b3_, prev.get("osq")])
        b5_ = ctx.op("vector", lambda e: e.reduce_sum(nsc[:, 4:5], osq[:],
                                                      axis=AXX), waits=[b4_])
        prev["osq"] = b5_
        b6_ = ctx.op("vector", lambda e: e.tensor_scalar(
            nsc[:, 5:6], nsc[:, 4:5], 1.0 / 64.0, 1e-6, OP.mult, OP.add),
            waits=[b5_])
        b7_ = ctx.op("scalar", lambda e: e.activation(
            nsc[:, 6:7], nsc[:, 5:6], AF.Sqrt), waits=[b6_])
        b8_ = ctx.op("vector", lambda e: e.reciprocal(nsc[:, 7:8], nsc[:, 6:7]),
                     waits=[b7_])
        b9_ = ctx.op("vector", lambda e: e.tensor_scalar(
            ob1[:], ob1[:], nsc[:, 7:8], None, OP.mult), waits=[b8_])
        b10 = ctx.op("vector", lambda e: e.tensor_mul(o16[:], ob1[:], nwrep[:]),
                     waits=[b9_, nwtok, prev.get("o16")])
        prev["ob1"] = b10
        pb6, pi6 = ctx.bank()
        ptro = pb6[0:64, 0:64].bitcast(BF16)
        b11 = ctx.op("tensor", lambda e, ptro=ptro: e.transpose(
            ptro, o16[:], id16[:]), waits=[b10, id16_tok, ctx.ptok[pi6]])
        prev["o16"] = b11
        b12 = ctx.op("vector", lambda e, ptro=ptro: e.tensor_copy(oT[:], ptro),
                     waits=[b11, prev.get("oT")])
        ctx.ptok[pi6] = b12
        t0 = c * 128
        p_ = c % 2
        xrv = xres[:, p_ * 1024:(p_ + 1) * 1024]
        xl = ctx.dma("sync", xrv, x_ap[t0:t0 + 128, :], waits=[xres_cons[p_]])
        last_add = None
        for ms in range(2):
            pb7, pi7 = ctx.bank()
            b13 = ctx.mm_chain(pb7[:], [(oT[:], wo1[:, ms * 512:(ms + 1) * 512])],
                               waits=[b12, ctx.ptok[pi7]] + wtok)
            b14 = ctx.op("vector", lambda e, pb7=pb7, xrv=xrv, ms=ms:
                         e.tensor_add(xrv[:, ms * 512:(ms + 1) * 512],
                                      xrv[:, ms * 512:(ms + 1) * 512], pb7[:]),
                         waits=[b13, xl])
            ctx.ptok[pi7] = b14
            last_add = b14
        so = ctx.dma_st("sync", xout_ap[t0:t0 + 128, :], xrv, waits=[last_add])
        if lnf is not None:
            lt = emit_lnf_tile(ctx, com, "T", lnf[0], xrv, c, lnf[1],
                               [last_add])
            xres_cons[p_] = lt
        else:
            xres_cons[p_] = so
        prev["oT"] = b13
    return Tok(ctx.sems["sync"], ctx.sems["sync"].n)


def emit_l1_ffn(ctx, com, x_ap, xn16_ap, w, acc_ap, xout_ap, mem=None):
    if mem is None:
        mem = ctx.sbuf
    nc = ctx.nc
    TG = 256
    NGR = S // TG
    wr_sb = mem("m_wr", [128, 8 * 8], BF16)
    w1buf = mem("m_w1", [128, 8 * 4096], BF16)
    w2buf = mem("m_w2", [128, 32 * 1024], BF16)
    b1_sb = mem("m_b1", [128, 32], F32)
    b2_sb = mem("m_b2", [1, 1024], F32)
    xnT = mem("m_xnT", [128, 2 * 8 * TG], BF16)
    hbuf = mem("m_hbuf", [128, 32 * TG], BF16)
    h_all = mem("m_hall", [128, NT * 8], F32)
    pmax_all = mem("m_pmax", [128, NT], F32)
    gl = mem("m_gl", [128, 8], F32)
    sc2 = mem("m_sc2", [128, 8], F32)
    ybuf = mem("m_ybuf", [128, 2 * 512], F32)
    xres = mem("m_xres", [128, 1024], F32)
    accb = mem("m_accb", [128, 1024], F32)

    for k in range(8):
        ctx.dma("gpsimd", wr_sb[:, k * 8:(k + 1) * 8],
                w["Wr"][k * 128:(k + 1) * 128, :])
    wtok = [Tok(ctx.sems["gpsimd"], ctx.sems["gpsimd"].n)]

    xcons = [None]
    gl_cons = [None]
    xnT0 = xnT[:, 0:8 * TG]
    for g in range(NGR):
        xtoks = load_xnT(ctx, xnT0, xn16_ap, g, waits=[xcons[0]], tg=TG)
        for ti in range(TG // 128):
            tile = g * (TG // 128) + ti
            xsl = lambda k, ti=ti: xnT0[:, k * TG + ti * 128: k * TG + (ti + 1) * 128]
            pb, pi = ctx.bank()
            pairs = [(xsl(k), wr_sb[:, k * 8:(k + 1) * 8]) for k in range(8)]
            mt = ctx.mm_chain(pb[:, 0:8], pairs,
                              waits=xtoks + wtok + [ctx.ptok[pi]])
            e1 = ctx.op("vector", lambda e, pb=pb: e.tensor_copy(gl[:], pb[:, 0:8]),
                        waits=[mt, gl_cons[0]])
            ctx.ptok[pi] = e1
            hv = h_all[:, tile * 8:(tile + 1) * 8]
            hm = emit_first_match(ctx, f"m{tile}", gl[:], hv, sc2[:], waits=[e1])
            s1 = ctx.op("vector", lambda e: e.reduce_max(sc2[:, 3:4], gl[:],
                                                         axis=AXX), waits=[e1])
            s2 = ctx.op("vector", lambda e: e.tensor_scalar(
                sc2[:, 4:5], sc2[:, 3:4], -1.0, None, OP.mult), waits=[s1])
            s3 = ctx.op("scalar", lambda e: e.activation(
                sc2[:, 5:6], gl[:, 0:1], AF.Exp, bias=sc2[:, 4:5]),
                waits=[s2])
            s4 = ctx.op("scalar", lambda e: e.activation(
                gl[:], gl[:], AF.Exp, bias=sc2[:, 4:5]), waits=[s2, hm])
            s5 = ctx.op("vector", lambda e: e.reduce_sum(sc2[:, 6:7], gl[:],
                                                         axis=AXX), waits=[s4])
            s6 = ctx.op("vector", lambda e, tile=tile: e.reciprocal(
                pmax_all[:, tile:tile + 1], sc2[:, 6:7]), waits=[s5])
            gl_cons[0] = s4
        xcons[0] = Tok(ctx.sems["tensor"], ctx.sems["tensor"].n)

    w1_cons = [None]
    w2_cons = [None]
    hb_cons = [None]
    yb_cons = [None, None]
    acc_view = acc_ap
    rt_snap = Tok(ctx.sems["tensor"], ctx.sems["tensor"].n)
    xslot_toks = [None, None]
    xslot_cons = [rt_snap, rt_snap]
    NITER = 8 * NGR


    def xview(slot):
        return xnT[:, slot * 8 * TG:(slot + 1) * 8 * TG]

    def xprefetch(t):
        if t >= NITER:
            return
        slot = t % 2
        g = t % NGR
        xslot_toks[slot] = load_xnT(ctx, xview(slot), xn16_ap, g,
                                    waits=[xslot_cons[slot]], tg=TG)
    xprefetch(0)
    for e_ in range(8):
        w1t = []
        for k in range(8):
            w1t.append(ctx.dma("gpsimd", w1buf[:, k * 4096:(k + 1) * 4096],
                               w["W1"][e_, k * 128:(k + 1) * 128, :],
                               waits=[w1_cons[0]]))
        w2t = []
        for c in range(32):
            w2t.append(ctx.dma("gpsimd", w2buf[:, c * 1024:(c + 1) * 1024],
                               w["W2"][e_, c * 128:(c + 1) * 128, :],
                               waits=[w2_cons[0]], sem_name="w2s"))
        bt1 = ctx.dma("sync", b1_sb[:], w["b1"][e_, :, :],
                      waits=[w1_cons[0]])
        bt2 = ctx.dma("sync", b2_sb[:], w["b2"][e_:e_ + 1, :],
                      waits=[w2_cons[0]])
        wtoks = [Tok(ctx.sems["gpsimd"], ctx.sems["gpsimd"].n), bt1,
                 com.init_tok]
        w2toks = [Tok(ctx.sems["w2s"], ctx.sems["w2s"].n), bt2]
        for g in range(NGR):
            t_ = e_ * NGR + g
            slot = t_ % 2
            xtoks = xslot_toks[slot]
            xv = xview(slot)
            up_toks = []
            for fc in range(32):
                pb, pi = ctx.bank()
                pairs = [(w1buf[:, k * 4096 + fc * 128: k * 4096 + (fc + 1) * 128],
                          xv[:, k * TG:(k + 1) * TG]) for k in range(8)]
                mt = ctx.mm_chain(pb[:, 0:TG], pairs,
                                  waits=xtoks + wtoks + [ctx.ptok[pi],
                                                         hb_cons[0]])
                r1 = ctx.op("scalar", lambda e, pb=pb, fc=fc: e.activation(
                    hbuf[:, fc * TG:(fc + 1) * TG], pb[:, 0:TG], AF.Gelu,
                    bias=b1_sb[:, fc:fc + 1]),
                    waits=[mt])
                ctx.ptok[pi] = r1
                up_toks.append(r1)
            xslot_cons[slot] = Tok(ctx.sems["tensor"], ctx.sems["tensor"].n)
            if g == NGR - 1:
                w1_cons[0] = Tok(ctx.sems["scalar"], ctx.sems["scalar"].n)
            xprefetch(t_ + 1)
            for ti in range(TG // 128):
                tile = g * (TG // 128) + ti
                t0 = tile * 128
                hcol = h_all[:, tile * 8 + e_: tile * 8 + e_ + 1]
                for ms in range(2):
                    pb, pi = ctx.bank()
                    pairs = [(hbuf[:, c * TG + ti * 128: c * TG + (ti + 1) * 128],
                              w2buf[:, c * 1024 + ms * 512:
                                    c * 1024 + (ms + 1) * 512])
                             for c in range(32)]
                    pairs.append((com.ones_row[:, 0:128],
                                  b2_sb[:, ms * 512:(ms + 1) * 512]))
                    mt = ctx.mm_chain(pb[:], pairs,
                                      waits=up_toks + wtoks + w2toks
                                      + [ctx.ptok[pi]])
                    p = ms
                    ybv = ybuf[:, p * 512:(p + 1) * 512]
                    y1 = ctx.op("vector", lambda e, pb=pb, ybv=ybv, hcol=hcol:
                                e.tensor_scalar(ybv, pb[:], hcol, None,
                                                OP.mult),
                                waits=[mt, yb_cons[p]])
                    ctx.ptok[pi] = y1
                    dst = acc_view[t0:t0 + 128, ms * 512:(ms + 1) * 512]
                    if e_ == 0:
                        d1 = ctx.dma("gpsimd", dst, ybv, waits=[y1],
                                     sem_name="acc")
                    else:
                        d1 = ctx.op("gpsimd", lambda e, dst=dst, ybv=ybv:
                                    e.dma_start(out=dst, in_=ybv,
                                                accum_op=OP.add),
                                    waits=[y1], amt=16, sem_name="acc")
                    yb_cons[p] = d1
            hb_cons[0] = Tok(ctx.sems["tensor"], ctx.sems["tensor"].n)
        w2_cons[0] = hb_cons[0]
    acc_done = Tok(ctx.sems["acc"], ctx.sems["acc"].n)

    fin_cons = [None]
    for tile in range(NT):
        t0 = tile * 128
        xl = ctx.dma("sync", xres[:], x_ap[t0:t0 + 128, :],
                     waits=[fin_cons[0]])
        al = ctx.dma("sync", accb[:], acc_view[t0:t0 + 128, :],
                     waits=[acc_done, fin_cons[0]])
        f1 = ctx.op("vector", lambda e, tile=tile: e.scalar_tensor_tensor(
            xres[:], accb[:], pmax_all[:, tile:tile + 1], xres[:],
            OP.mult, OP.add), waits=[xl, al])
        so = ctx.dma_st("sync", xout_ap[t0:t0 + 128, :], xres[:], waits=[f1])
        fin_cons[0] = so
    return Tok(ctx.sems["sync"], ctx.sems["sync"].n)


def emit_l1_ffn_g(ctx, com, x_pad, xnT4_ap, xtok_pad, w, tid_d, pm_tok,
                  y_pad, xg_d, y_out, mem=None):
    if mem is None:
        mem = ctx.sbuf
    nc = ctx.nc
    TG = 256
    GW = 15
    NSLOT = GW * 512
    triu0, ctk1 = com.const("triu0")
    sutril, ctk2 = com.const("sutril")
    iota, ctk3 = com.const("iota")
    j512, ctk5 = com.const("j512")
    idx8v, ctk6 = com.const("idx8v")
    idx32v, ctk7 = com.const("idx32v")
    id16, id16_tok = com.ident16()
    idf, idf_tok = com.const("ident128")

    wr_sb = mem("m_wr", [128, 8 * 8], BF16)
    w1buf = mem("m_w1", [128, 8 * 4096], BF16)
    w2buf = mem("m_w2", [128, 32 * 1024], BF16)
    b1_sb = mem("m_b1", [128, 32], F32)
    b2_sb = mem("m_b2", [2, 1024], F32)
    xnT = mem("m_xnT", [128, 8 * 256], BF16)
    xgT = mem("m_xgT", [128, 8 * TG], BF16)
    hbuf = mem("m_hbuf", [128, 32 * TG], BF16)
    h_all = mem("m_hall", [128, NT * 8], F32)
    gl = mem("m_gl", [128, 8], F32)
    sc2 = mem("m_sc2", [128, 16], F32)
    carry = mem("m_carry", [1, 8], F32)
    m_pos = mem("m_pos", [128, NT], F32)
    stile = mem("m_stile", [128, 65], I32)
    zrow16 = mem("m_zrow16", [1, 1024], BF16)
    idxc = mem("m_idxc", [128, 8], I32)
    xg = mem("m_xg", [128, 2 * 1024], BF16)
    yb = mem("m_yb", [128, 2 * 1024], F32)
    pmcol = mem("m_pmcol", [128, 2], F32)
    xrs = mem("m_xrs", [128, 2 * 1024], F32)
    ypd = mem("m_ypd", [128, 2 * 1024], F32)
    sidx = mem("m_sidx", [128, 32], F32)
    cumrep = mem("m_cumrep", [128, 8], F32)
    grow = mem("m_grow", [1, 24], F32)
    ecol = mem("m_ecol", [128, 4], F32)
    w1f = mem("m_w1f", [128, 8], F32)
    w2f = mem("m_w2f", [128, 32], F32)
    w1i = mem("m_w1i", [128, 8], I32)
    w2i = mem("m_w2i", [128, 32], I32)
    b1i = mem("m_b1i", [128, 1], I32)
    b2i = mem("m_b2i", [2, 1], I32)
    W1flat = w["W1"].rearrange("e d f -> (e d) f")
    W2flat = w["W2"].rearrange("e f d -> (e f) d")
    b1flat = w["b1"].rearrange("e p c -> (e p) c")

    for k in range(8):
        ctx.dma("gpsimd", wr_sb[:, k * 8:(k + 1) * 8],
                w["Wr"][k * 128:(k + 1) * 128, :])
    wtok0 = [Tok(ctx.sems["gpsimd"], ctx.sems["gpsimd"].n)]

    sf1 = ctx.op("gpsimd", lambda e: e.memset(stile[:], 4096))
    sf2 = ctx.dma_st("sync", tid_d.rearrange("(a b) one -> a (b one)", a=128),
                     stile[:], waits=[sf1])
    zf1 = ctx.op("gpsimd", lambda e: e.memset(zrow16[:], 0.0))
    zf2 = ctx.dma_st("sync", xtok_pad[4096:4097, :], zrow16[:], waits=[zf1])
    cz = ctx.op("gpsimd", lambda e: e.memset(carry[:], 0.0))
    gi_cons = [None]

    xcons = [None]
    gl_cons = [None]
    pm_cons = [None]
    carry_mm = [cz]
    for g in range(16):
        xtoks = load_xnT(ctx, xnT, xnT4_ap, g, waits=[xcons[0]], tg=256)
        for ti in range(2):
            tile = g * 2 + ti
            xsl = lambda k, ti=ti: xnT[:, k * 256 + ti * 128: k * 256 + (ti + 1) * 128]
            pb, pi = ctx.bank()
            pairs = [(xsl(k), wr_sb[:, k * 8:(k + 1) * 8]) for k in range(8)]
            mt = ctx.mm_chain(pb[:, 0:8], pairs,
                              waits=xtoks + wtok0 + [ctx.ptok[pi]])
            e1 = ctx.op("vector", lambda e, pb=pb: e.tensor_copy(gl[:], pb[:, 0:8]),
                        waits=[mt, gl_cons[0]])
            ctx.ptok[pi] = e1
            hv = h_all[:, tile * 8:(tile + 1) * 8]
            hm = emit_first_match(ctx, f"m{tile}", gl[:], hv, sc2[:, 0:3],
                                  waits=[e1])
            s1 = ctx.op("vector", lambda e: e.reduce_max(sc2[:, 3:4], gl[:],
                                                         axis=AXX), waits=[e1])
            s2 = ctx.op("vector", lambda e: e.tensor_scalar(
                sc2[:, 4:5], sc2[:, 3:4], -1.0, None, OP.mult), waits=[s1])
            s4 = ctx.op("scalar", lambda e: e.activation(
                gl[:], gl[:], AF.Exp, bias=sc2[:, 4:5]), waits=[s2, hm])
            s5 = ctx.op("vector", lambda e: e.reduce_sum(sc2[:, 5:6], gl[:],
                                                         axis=AXX), waits=[s4])
            s6 = ctx.op("vector", lambda e: e.reciprocal(
                sc2[:, 6:7], sc2[:, 5:6]), waits=[s5, pm_cons[0]])
            pm_cons[0] = ctx.dma_st("sync", pm_tok[tile * 128:(tile + 1) * 128, :],
                                    sc2[:, 6:7], waits=[s6])
            gl_cons[0] = s4
            pbA, piA = ctx.bank()
            mA = ctx.op("tensor", lambda e, pbA=pbA, hv=hv: e.matmul(
                pbA[:, 0:8], triu0[:], hv, start=True, stop=True),
                waits=[hm, ctk1, ctx.ptok[piA]])
            pbB, piB = ctx.bank()
            mB = ctx.op("tensor", lambda e, pbB=pbB, hv=hv: e.matmul(
                pbB[0:1, 0:8], com.ones_col[:], hv, start=True, stop=True),
                waits=[hm, com.init_tok, ctx.ptok[piB]])
            pbC, piC = ctx.bank()
            mC = ctx.op("tensor", lambda e, pbC=pbC: e.matmul(
                pbC[:, 0:8], com.ones_row[:, 0:128], carry[:],
                start=True, stop=True),
                waits=[carry_mm[0], com.init_tok, ctx.ptok[piC]])
            c1 = ctx.op("vector", lambda e, pbA=pbA, hv=hv: e.tensor_tensor(
                sc2[:, 8:16], hv, pbA[:, 0:8], OP.mult), waits=[mA])
            ctx.ptok[piA] = c1
            c2 = ctx.op("vector", lambda e: e.reduce_sum(
                sc2[:, 3:4], sc2[:, 8:16], axis=AXX), waits=[c1])
            c3 = ctx.op("vector", lambda e, pbC=pbC, hv=hv: e.tensor_tensor(
                sc2[:, 8:16], hv, pbC[:, 0:8], OP.mult), waits=[mC, c2])
            ctx.ptok[piC] = c3
            c4 = ctx.op("vector", lambda e: e.reduce_sum(
                sc2[:, 4:5], sc2[:, 8:16], axis=AXX), waits=[c3, s4])
            c7 = ctx.op("vector", lambda e: e.tensor_add(
                sc2[:, 7:8], sc2[:, 3:4], sc2[:, 4:5]), waits=[c4, c2])
            c8 = ctx.op("vector", lambda e, tile=tile: e.tensor_scalar(
                m_pos[:, tile:tile + 1], sc2[:, 7:8], -1.0, None, OP.add),
                waits=[c7])
            u1 = ctx.op("vector", lambda e, pbB=pbB: e.tensor_add(
                carry[:], carry[:], pbB[0:1, 0:8]), waits=[mB, mC])
            ctx.ptok[piB] = u1
            carry_mm[0] = u1
        xcons[0] = Tok(ctx.sems["tensor"], ctx.sems["tensor"].n)

    pb, pi = ctx.bank()
    q1 = ctx.op("tensor", lambda e, pb=pb: e.transpose(
        pb[0:8, 0:1], carry[:], idf[0:1, 0:1]),
        waits=[carry_mm[0], idf_tok, ctx.ptok[pi]])
    q2 = ctx.op("vector", lambda e, pb=pb: e.tensor_copy(
        sidx[0:8, 0:1], pb[0:8, 0:1]), waits=[q1])
    ctx.ptok[pi] = q2
    pb, pi = ctx.bank()
    q3 = ctx.op("tensor", lambda e, pb=pb: e.matmul(
        pb[0:8, 0:8], com.ones_row[:, 0:8], j512[:], start=True, stop=True),
        waits=[ctk5, com.init_tok, ctx.ptok[pi]])
    q4 = ctx.op("vector", lambda e, pb=pb: e.tensor_scalar(
        sidx[0:8, 4:12], pb[0:8, 0:8], sidx[0:8, 0:1], None, OP.is_lt),
        waits=[q3, q2])
    ctx.ptok[pi] = q4
    q5 = ctx.op("vector", lambda e: e.reduce_sum(
        sidx[0:8, 1:2], sidx[0:8, 4:12], axis=AXX), waits=[q4])
    q6 = ctx.op("vector", lambda e: e.tensor_scalar(
        sidx[0:8, 2:3], sidx[0:8, 1:2], 512.0, None, OP.mult), waits=[q5])
    pb, pi = ctx.bank()
    q7 = ctx.op("tensor", lambda e, pb=pb: e.matmul(
        pb[0:8, 0:1], sutril[0:8, 0:8], sidx[0:8, 2:3],
        start=True, stop=True), waits=[q6, ctk2, ctx.ptok[pi]])
    q8 = ctx.op("vector", lambda e, pb=pb: e.tensor_copy(
        sidx[0:8, 3:4], pb[0:8, 0:1]), waits=[q7])
    ctx.ptok[pi] = q8
    pb, pi = ctx.bank()
    q9 = ctx.op("tensor", lambda e, pb=pb: e.transpose(
        pb[0:1, 0:8], sidx[0:8, 3:4], idf[0:8, 0:8]),
        waits=[q8, ctx.ptok[pi]])
    q10 = ctx.op("vector", lambda e, pb=pb: e.tensor_copy(
        grow[0:1, 16:24], pb[0:1, 0:8]), waits=[q9])
    ctx.ptok[pi] = q10
    pb, pi = ctx.bank()
    q11 = ctx.op("tensor", lambda e, pb=pb: e.matmul(
        pb[:, 0:8], com.ones_row[:, 0:128], grow[0:1, 16:24],
        start=True, stop=True), waits=[q10, com.init_tok, ctx.ptok[pi]])
    q12 = ctx.op("vector", lambda e, pb=pb: e.tensor_copy(
        cumrep[:], pb[:, 0:8]), waits=[q11])
    ctx.ptok[pi] = q12
    q13 = ctx.op("vector", lambda e: e.tensor_scalar(
        sidx[0:GW, 12:13], iota[0:GW, :], 512.0, None, OP.mult),
        waits=[ctk3, q4])
    q14 = ctx.op("vector", lambda e: e.tensor_scalar(
        sidx[0:GW, 4:12], cumrep[0:GW, 0:8], sidx[0:GW, 12:13], None,
        OP.is_le), waits=[q12, q13])
    q15 = ctx.op("vector", lambda e: e.reduce_sum(
        sidx[0:GW, 13:14], sidx[0:GW, 4:12], axis=AXX), waits=[q14])
    q16 = ctx.op("vector", lambda e: e.tensor_scalar(
        sidx[0:GW, 14:15], sidx[0:GW, 13:14], -1.0, None, OP.add),
        waits=[q15])
    pb, pi = ctx.bank()
    q17 = ctx.op("tensor", lambda e, pb=pb: e.transpose(
        pb[0:1, 0:GW], sidx[0:GW, 14:15], idf[0:GW, 0:GW]),
        waits=[q16, ctx.ptok[pi]])
    q18 = ctx.op("vector", lambda e, pb=pb: e.tensor_copy(
        grow[0:1, 0:GW], pb[0:1, 0:GW]), waits=[q17])
    ctx.ptok[pi] = q18

    for tile in range(NT):
        e5 = ctx.op("vector", lambda e, tile=tile: e.tensor_tensor(
            sc2[:, 8:16], h_all[:, tile * 8:(tile + 1) * 8], cumrep[:],
            OP.mult), waits=[q12])
        e6 = ctx.op("vector", lambda e: e.reduce_sum(
            sc2[:, 3:4], sc2[:, 8:16], axis=AXX), waits=[e5])
        e7 = ctx.op("vector", lambda e, tile=tile: e.tensor_add(
            sc2[:, 3:4], sc2[:, 3:4], m_pos[:, tile:tile + 1]), waits=[e6])
        e8 = ctx.op("vector", lambda e: e.tensor_single_scalar(
            sc2[:, 3:4], sc2[:, 3:4], 8191.0, OP.min), waits=[e7])
        e9 = ctx.op("vector", lambda e, tile=tile: e.tensor_scalar(
            sc2[:, 4:5], iota[:], float(tile * 128), None, OP.add),
            waits=[ctk3, e8])
        c14 = ctx.op("vector", lambda e: e.tensor_copy(
            idxc[:, 0:1], sc2[:, 3:4]), waits=[e8, gi_cons[0]])
        c15 = ctx.op("vector", lambda e: e.tensor_copy(
            idxc[:, 1:2], sc2[:, 4:5]), waits=[e9])
        g1 = ctx.op("gpsimd", lambda e: e.indirect_dma_start(
            tid_d[:], IndirectOffsetOnAxis(idxc[:, 0:1], 0),
            idxc[:, 1:2], None), waits=[c14, c15, sf2], amt=16,
            sem_name="acc")
        gi_cons[0] = g1

    pg_idx_cons = [None, None]
    pg_xg_cons = [None, None]
    scat_all = Tok(ctx.sems["acc"], ctx.sems["acc"].n)
    for sl in range(NSLOT // 128):
        p = sl % 2
        ic = idxc[:, 2 + p:3 + p]
        il = ctx.dma("sync", ic, tid_d[sl * 128:(sl + 1) * 128, :],
                     waits=[scat_all, pg_idx_cons[p]])
        gx = ctx.op("gpsimd", lambda e, p=p, ic=ic: e.indirect_dma_start(
            xg[:, p * 1024:(p + 1) * 1024], None, xtok_pad[:],
            IndirectOffsetOnAxis(ic, 0)),
            waits=[il, zf2, pg_xg_cons[p]], amt=16, sem_name="acc")
        pg_idx_cons[p] = gx
        st = ctx.dma_st("sync", xg_d[sl * 128:(sl + 1) * 128, :],
                        xg[:, p * 1024:(p + 1) * 1024], waits=[gx])
        pg_xg_cons[p] = st
    pg_done = Tok(ctx.sems["st"], ctx.sems["st"].n)

    w1_cons = [None]
    wb_cons = [None]
    w2_cons = [None]
    w1i_cons = [None]
    w2i_cons = [None]
    hb_cons = [Tok(ctx.sems["tensor"], ctx.sems["tensor"].n)]
    idx_cons = [None, None]
    xgld_cons = [None, None]
    yb_cons = [None, None]
    for gG in range(GW):
        pb, pi = ctx.bank()
        v1 = ctx.op("tensor", lambda e, pb=pb, gG=gG: e.matmul(
            pb[:, 0:1], com.ones_row[:, 0:128], grow[0:1, gG:gG + 1],
            start=True, stop=True),
            waits=[q18, com.init_tok, ctx.ptok[pi]])
        v2 = ctx.op("vector", lambda e, pb=pb: e.tensor_copy(
            ecol[:, 0:1], pb[:, 0:1]), waits=[v1])
        ctx.ptok[pi] = v2
        v3 = ctx.op("vector", lambda e: e.tensor_scalar(
            ecol[:, 1:2], ecol[:, 0:1], 1024.0, None, OP.mult), waits=[v2])
        v4 = ctx.op("vector", lambda e: e.tensor_scalar(
            ecol[:, 2:3], ecol[:, 0:1], 4096.0, None, OP.mult), waits=[v2])
        v5 = ctx.op("vector", lambda e: e.tensor_scalar(
            w1f[:], idx8v[:], ecol[:, 1:2], None, OP.add),
            waits=[ctk6, v3])
        v6 = ctx.op("vector", lambda e: e.tensor_copy(w1i[:], w1f[:]),
                    waits=[v5, w1i_cons[0]])
        v7 = ctx.op("vector", lambda e: e.tensor_scalar(
            w2f[:], idx32v[:], ecol[:, 2:3], None, OP.add),
            waits=[ctk7, v4])
        v8 = ctx.op("vector", lambda e: e.tensor_copy(w2i[:], w2f[:]),
                    waits=[v7, w2i_cons[0]])
        v9 = ctx.op("vector", lambda e: e.scalar_tensor_tensor(
            ecol[:, 3:4], ecol[:, 0:1], 128.0, iota[:], OP.mult, OP.add),
            waits=[v2, ctk3])
        v10 = ctx.op("vector", lambda e: e.tensor_copy(
            b1i[:], ecol[:, 3:4]), waits=[v9, w1i_cons[0]])
        v11 = ctx.op("vector", lambda e: e.tensor_copy(
            b2i[:], ecol[0:2, 0:1]), waits=[v2, w2i_cons[0]])
        for k in range(8):
            ctx.op("gpsimd", lambda e, k=k: e.indirect_dma_start(
                w1buf[:, k * 4096:(k + 1) * 4096], None, W1flat,
                IndirectOffsetOnAxis(w1i[:, k:k + 1], 0)),
                waits=[v6, w1_cons[0]], amt=16, sem_name="w1s")
        ctx.op("gpsimd", lambda e: e.indirect_dma_start(
            b1_sb[:], None, b1flat, IndirectOffsetOnAxis(b1i[:, 0:1], 0)),
            waits=[v10, wb_cons[0]], amt=16, sem_name="w1s")
        for c in range(32):
            ctx.op("gpsimd", lambda e, c=c: e.indirect_dma_start(
                w2buf[:, c * 1024:(c + 1) * 1024], None, W2flat,
                IndirectOffsetOnAxis(w2i[:, c:c + 1], 0)),
                waits=[v8, w2_cons[0]], amt=16, sem_name="w2s")
        ctx.op("gpsimd", lambda e: e.indirect_dma_start(
            b2_sb[:], None, w["b2"], IndirectOffsetOnAxis(b2i[:, 0:1], 0)),
            waits=[v11, w2_cons[0]], amt=16, sem_name="w2s")
        wtoks = [Tok(ctx.sems["w1s"], ctx.sems["w1s"].n), com.init_tok]
        w2toks = [Tok(ctx.sems["w2s"], ctx.sems["w2s"].n)]
        w1i_cons[0] = wtoks[0]
        w2i_cons[0] = w2toks[0]
        for half in range(2):
            stoks = []
            for s_ in range(TG // 128):
                sl = gG * 4 + half * 2 + s_
                p = sl % 2
                ic = idxc[:, 4 + s_:5 + s_]
                il = ctx.dma("sync", ic, tid_d[sl * 128:(sl + 1) * 128, :],
                             waits=[idx_cons[s_], scat_all])
                xl = ctx.dma("sync", xg[:, p * 1024:(p + 1) * 1024],
                             xg_d[sl * 128:(sl + 1) * 128, :],
                             waits=[pg_done, xgld_cons[p]])
                cps = []
                for k in range(8):
                    pb, pi = ctx.bank()
                    ptr = pb[:, 0:64].bitcast(BF16)
                    tr = ctx.op("tensor", lambda e, ptr=ptr, p=p, k=k:
                                e.transpose(ptr, xg[:, p * 1024 + k * 128:
                                                    p * 1024 + (k + 1) * 128],
                                            id16[:]),
                                waits=[xl, id16_tok, ctx.ptok[pi]])
                    cp = ctx.op("vector", lambda e, ptr=ptr, s_=s_, k=k:
                                e.tensor_copy(
                                    xgT[:, k * TG + s_ * 128: k * TG + (s_ + 1) * 128],
                                    ptr),
                                waits=[tr, hb_cons[0]])
                    ctx.ptok[pi] = cp
                    cps.append(cp)
                xgld_cons[p] = cps[-1]
                stoks.append(cps[-1])
            up_toks = []
            for fc in range(32):
                pb, pi = ctx.bank()
                pairs = [(w1buf[:, k * 4096 + fc * 128: k * 4096 + (fc + 1) * 128],
                          xgT[:, k * TG:(k + 1) * TG]) for k in range(8)]
                mt = ctx.mm_chain(pb[:, 0:TG], pairs,
                                  waits=stoks + wtoks + [ctx.ptok[pi],
                                                         hb_cons[0]])
                r1 = ctx.op("scalar", lambda e, pb=pb, fc=fc: e.activation(
                    hbuf[:, fc * TG:(fc + 1) * TG], pb[:, 0:TG], AF.Gelu,
                    bias=b1_sb[:, fc:fc + 1]), waits=[mt])
                ctx.ptok[pi] = r1
                up_toks.append(r1)
            if half == 1:
                w1_cons[0] = Tok(ctx.sems["tensor"], ctx.sems["tensor"].n)
                wb_cons[0] = Tok(ctx.sems["scalar"], ctx.sems["scalar"].n)
            for s_ in range(TG // 128):
                sl2 = s_ % 2
                last = None
                for ms in range(2):
                    pb, pi = ctx.bank()
                    pairs = [(hbuf[:, c * TG + s_ * 128: c * TG + (s_ + 1) * 128],
                              w2buf[:, c * 1024 + ms * 512:
                                    c * 1024 + (ms + 1) * 512])
                             for c in range(32)]
                    pairs.append((com.ones_row[:, 0:128],
                                  b2_sb[0:1, ms * 512:(ms + 1) * 512]))
                    mt = ctx.mm_chain(pb[:], pairs,
                                      waits=up_toks + wtoks + w2toks
                                      + [ctx.ptok[pi]])
                    y1 = ctx.op("vector", lambda e, pb=pb, sl2=sl2, ms=ms:
                                e.tensor_copy(
                                    yb[:, sl2 * 1024 + ms * 512:
                                       sl2 * 1024 + (ms + 1) * 512], pb[:]),
                                waits=[mt, yb_cons[sl2]])
                    ctx.ptok[pi] = y1
                    last = y1
                sc_ = ctx.op("gpsimd", lambda e, s_=s_, sl2=sl2: e.indirect_dma_start(
                    y_pad[:], IndirectOffsetOnAxis(idxc[:, 4 + s_:5 + s_], 0),
                    yb[:, sl2 * 1024:(sl2 + 1) * 1024], None),
                    waits=[last], amt=16, sem_name="acc")
                yb_cons[sl2] = sc_
                idx_cons[s_] = sc_
            hb_cons[0] = Tok(ctx.sems["tensor"], ctx.sems["tensor"].n)
        w2_cons[0] = hb_cons[0]
    scat_done = Tok(ctx.sems["acc"], ctx.sems["acc"].n)
    wend = Tok(ctx.sems["tensor"], ctx.sems["tensor"].n)
    fin_cons = [wend, wend]
    for tile in range(NT):
        t0 = tile * 128
        p = tile % 2
        xl = ctx.dma("sync", xrs[:, p * 1024:(p + 1) * 1024],
                     x_pad[t0:t0 + 128, :], waits=[fin_cons[p]])
        yl = ctx.dma("sync", ypd[:, p * 1024:(p + 1) * 1024],
                     y_pad[t0:t0 + 128, :], waits=[scat_done, fin_cons[p]])
        pml = ctx.dma("sync", pmcol[:, p:p + 1],
                      pm_tok[t0:t0 + 128, :], waits=[fin_cons[p]])
        f1 = ctx.op("vector", lambda e, p=p: e.scalar_tensor_tensor(
            xrs[:, p * 1024:(p + 1) * 1024],
            ypd[:, p * 1024:(p + 1) * 1024], pmcol[:, p:p + 1],
            xrs[:, p * 1024:(p + 1) * 1024], OP.mult, OP.add),
            waits=[xl, yl, pml])
        so = ctx.dma_st("sync", y_out[t0:t0 + 128, :],
                        xrs[:, p * 1024:(p + 1) * 1024], waits=[f1])
        fin_cons[p] = so
    return Tok(ctx.sems["st"], ctx.sems["st"].n)


def prep_weights(inp):
    g = {}
    f32c = lambda a: np.ascontiguousarray(np.asarray(a, np.float32))
    g["Wq"] = f32c(inp["L0_attn_Wq"])
    g["Wk"] = f32c(inp["L0_attn_Wk"])
    g["Wv"] = f32c(inp["L0_attn_Wv"])
    fo = np.arange(4096)
    qpar = fo // 2048
    i2 = (fo // 64) % 32
    j = fo % 64
    ref_row = (2 * i2 + qpar) * 64 + j
    g["WoP"] = f32c(np.asarray(inp["L0_attn_Wo"], np.float32)[ref_row, :])
    g["bo"] = f32c(inp["L0_attn_bo"]).reshape(1, 1024)
    g["gWT"] = f32c(np.asarray(inp["L0_ffn_gate_W"], np.float32).T)
    g["gate_b"] = f32c(inp["L0_ffn_gate_b"]).reshape(1, 8)
    g["wiT"] = f32c(np.asarray(inp["L0_ffn_wi_W"], np.float32).T)
    g["wi_b"] = f32c(np.asarray(inp["L0_ffn_wi_b"], np.float32)
                     .reshape(32, 128).T)
    g["A_lay"] = f32c(np.asarray(inp["L0_ffn_lora_A"], np.float32)
                      .reshape(32, 1024).T)
    g["B_flat"] = f32c(np.transpose(np.asarray(inp["L0_ffn_lora_B"], np.float32),
                                    (0, 2, 1)).reshape(32, 4096))
    g["woT"] = f32c(np.asarray(inp["L0_ffn_wo_W"], np.float32).T)
    g["wo_b"] = f32c(inp["L0_ffn_wo_b"]).reshape(1, 1024)
    g["WkT1"] = f32c(np.asarray(inp["L1_attn_Wk"], np.float32).T)
    g["WvT1"] = f32c(np.asarray(inp["L1_attn_Wv"], np.float32).T)
    g["WqT1"] = f32c(np.asarray(inp["L1_attn_Wq"], np.float32).T)
    g["fa"] = f32c(inp["L1_attn_fast_alpha"]).reshape(1, 1)
    g["sa"] = f32c(inp["L1_attn_slow_alpha"]).reshape(1, 1)
    g["nm1T"] = f32c(np.asarray(inp["L1_attn_nm_w1"], np.float32).T)
    g["nm_b1"] = f32c(inp["L1_attn_nm_b1"]).reshape(1, 128)
    g["nm2T"] = f32c(np.asarray(inp["L1_attn_nm_w2"], np.float32).T)
    g["nm_b2"] = f32c(inp["L1_attn_nm_b2"]).reshape(1, 64)
    g["ir1T"] = f32c(np.asarray(inp["L1_attn_ir_w1"], np.float32).T)
    g["ir_b1"] = f32c(inp["L1_attn_ir_b1"]).reshape(1, 64)
    g["ir_w2"] = f32c(inp["L1_attn_ir_w2"]).reshape(1, 64)
    g["ir_b2"] = f32c(inp["L1_attn_ir_b2"]).reshape(1, 1)
    g["pwT"] = f32c(np.asarray(inp["L1_attn_pw"], np.float32).T)
    g["WoT1"] = f32c(np.asarray(inp["L1_attn_Wo"], np.float32).T)
    g["norm_w"] = f32c(inp["L1_attn_norm_w"]).reshape(1, 64)
    g["Wr"] = f32c(inp["L1_ffn_Wr"])
    g["W1"] = f32c(inp["L1_ffn_W1"])
    g["b1"] = f32c(np.transpose(np.asarray(inp["L1_ffn_b1"], np.float32)
                                .reshape(8, 32, 128), (0, 2, 1)))
    g["W2"] = f32c(inp["L1_ffn_W2"])
    g["b2"] = f32c(inp["L1_ffn_b2"])
    return g


WSHAPES = {
    "Wq": [1024, 4096], "Wk": [1024, 4096], "Wv": [1024, 4096],
    "WoP": [4096, 1024], "bo": [1, 1024],
    "gWT": [1024, 8], "gate_b": [1, 8], "wiT": [1024, 4096],
    "wi_b": [128, 32], "A_lay": [1024, 32], "B_flat": [32, 4096],
    "woT": [4096, 1024], "wo_b": [1, 1024],
    "WkT1": [1024, 64], "WvT1": [1024, 64], "WqT1": [1024, 64],
    "fa": [1, 1], "sa": [1, 1], "nm1T": [64, 128], "nm_b1": [1, 128],
    "nm2T": [128, 64], "nm_b2": [1, 64], "ir1T": [1024, 64],
    "ir_b1": [1, 64], "ir_w2": [1, 64], "ir_b2": [1, 1],
    "pwT": [1024, 3], "WoT1": [64, 1024], "norm_w": [1, 64],
    "Wr": [1024, 8], "W1": [8, 1024, 4096], "b1": [8, 128, 32],
    "W2": [8, 4096, 1024], "b2": [8, 1024],
}

BLOCK_W = {
    "l0a": ["Wq", "Wk", "Wv", "WoP", "bo"],
    "l0f": ["gWT", "gate_b", "wiT", "wi_b", "A_lay", "B_flat", "woT", "wo_b"],
    "l1a": ["WkT1", "WvT1", "WqT1", "fa", "sa", "nm1T", "nm_b1", "nm2T",
            "nm_b2", "ir1T", "ir_b1", "ir_w2", "ir_b2", "pwT", "WoT1",
            "norm_w"],
    "l1f": ["Wr", "W1", "b1", "W2", "b2"],
}


def decl_weights(nc, names):
    return {n: nc.dram_tensor("w_" + n, WSHAPES[n], F32,
                              kind="ExternalInput").ap() for n in names}


def build_l0a_nc(dbg=False):
    nc = bass.Bass("TRN2", target_bir_lowering=False)
    x = nc.dram_tensor("x", [S, D], F32, kind="ExternalInput").ap()
    y = nc.dram_tensor("y", [S, D], F32, kind="ExternalOutput").ap()
    caps = decl_consts(nc)
    w = decl_weights(nc, BLOCK_W["l0a"])
    kd = "ExternalOutput" if dbg else "Internal"
    xn16 = nc.dram_tensor("xnT4", [8, 128, S], BF16, kind="Internal").ap()
    sc = {
        "q4": nc.dram_tensor("q4", [S, 4096], BF16, kind=kd).ap(),
        "v4d": nc.dram_tensor("v4d", [S, 8192], BF16, kind=kd).ap(),
        "of4": nc.dram_tensor("of4", [S, 4096], BF16, kind=kd).ap(),
    }
    es = contextlib.ExitStack()
    with es:
        ctx = Ctx(nc, es)
        com = Common(ctx, caps)
        emit_ln(ctx, com, x, xn16, "ln0")
        emit_l0_attn(ctx, com, x, xn16, w, sc, y)
        nc_dbg = dbg
        fin = Tok(ctx.sems["sync"], ctx.sems["sync"].n)
        ctx.op("sync", lambda e: e.nop(), waits=[fin])
        block = es.enter_context(nc.Block())
        ctx.finalize(block)
    return nc


def build_l0f_nc(dbg=False):
    nc = bass.Bass("TRN2", target_bir_lowering=False)
    x = nc.dram_tensor("x", [S, D], F32, kind="ExternalInput").ap()
    y = nc.dram_tensor("y", [S, D], F32, kind="ExternalOutput").ap()
    caps = decl_consts(nc)
    w = decl_weights(nc, BLOCK_W["l0f"])
    xn16 = nc.dram_tensor("xnT4", [8, 128, S], BF16, kind="Internal").ap()
    es = contextlib.ExitStack()
    with es:
        ctx = Ctx(nc, es)
        com = Common(ctx, caps)
        emit_ln(ctx, com, x, xn16, "ln1")
        emit_l0_ffn(ctx, com, x, xn16, w, y)
        ctx.op("sync", lambda e: e.nop(),
               waits=[Tok(ctx.sems["st"], ctx.sems["st"].n)])
        block = es.enter_context(nc.Block())
        ctx.finalize(block)
    return nc


def build_l1a_nc(dbg=False):
    nc = bass.Bass("TRN2", target_bir_lowering=False)
    x = nc.dram_tensor("x", [S, D], F32, kind="ExternalInput").ap()
    y = nc.dram_tensor("y", [S, D], F32, kind="ExternalOutput").ap()
    caps = decl_consts(nc)
    w = decl_weights(nc, BLOCK_W["l1a"])
    xn16 = nc.dram_tensor("xnT4", [8, 128, S], BF16, kind="Internal").ap()
    es = contextlib.ExitStack()
    with es:
        ctx = Ctx(nc, es)
        com = Common(ctx, caps)
        emit_ln(ctx, com, x, xn16, "ln2")
        emit_l1_attn(ctx, com, x, xn16, w, y)
        ctx.op("sync", lambda e: e.nop(),
               waits=[Tok(ctx.sems["st"], ctx.sems["st"].n)])
        block = es.enter_context(nc.Block())
        ctx.finalize(block)
    return nc


def build_l1f_nc(dbg=False):
    nc = bass.Bass("TRN2", target_bir_lowering=False)
    x = nc.dram_tensor("x", [S, D], F32, kind="ExternalInput").ap()
    y = nc.dram_tensor("y", [S, D], F32, kind="ExternalOutput").ap()
    caps = decl_consts(nc)
    w = decl_weights(nc, BLOCK_W["l1f"])
    xn16 = nc.dram_tensor("xnT4", [8, 128, S], BF16, kind="Internal").ap()
    acc = nc.dram_tensor("acc", [S, D], F32, kind="Internal").ap()
    es = contextlib.ExitStack()
    with es:
        ctx = Ctx(nc, es)
        com = Common(ctx, caps)
        emit_ln(ctx, com, x, xn16, "ln3")
        emit_l1_ffn(ctx, com, x, xn16, w, acc, y)
        ctx.op("sync", lambda e: e.nop(),
               waits=[Tok(ctx.sems["st"], ctx.sems["st"].n),
                      Tok(ctx.sems["acc"], ctx.sems["acc"].n)])
        block = es.enter_context(nc.Block())
        ctx.finalize(block)
    return nc


class Carve:

    def __init__(self, region):
        self.r = region
        self.off = 0
        self.cap = region.shape[1]

    def __call__(self, name, shape, dtype):
        n = 1
        for s_ in shape[1:]:
            n *= s_
        cols = n * (2 if dtype == F32 else 1)
        off = (self.off + 15) // 16 * 16
        self.off = off + cols
        assert self.off <= self.cap, (name, self.off, self.cap)
        v = self.r[0:shape[0], off:off + cols]
        if dtype == F32:
            v = v.bitcast(F32)
        return v


def make_mem(ctx, table, carves):
    def mem(name, shape, dtype):
        if name in table:
            return table[name]
        for pred, cv in carves:
            if pred(name):
                return cv(name, shape, dtype)
        return ctx.sbuf(name, shape, dtype)
    return mem


def barrier(ctx):
    allk = list(ENGINES) + ["st", "acc", "trq", "w2s", "w1s", "tr2"]
    snaps = {e: Tok(ctx.sems[e], ctx.sems[e].n) for e in allk}
    for e in ENGINES:
        ctx.op(e, lambda en: en.nop(),
               waits=[snaps[o] for o in allk if o != e])


def build_fused_nc(dbg=False):
    nc = bass.Bass("TRN2", target_bir_lowering=False)
    x = nc.dram_tensor("x", [S, D], F32, kind="ExternalInput").ap()
    y = nc.dram_tensor("y", [S, D], F32, kind="ExternalOutput").ap()
    caps = decl_consts(nc)
    w = decl_weights(nc, BLOCK_W["l0a"] + BLOCK_W["l0f"] + BLOCK_W["l1a"]
                     + BLOCK_W["l1f"])
    kd = "ExternalOutput" if dbg else "Internal"
    xn16 = nc.dram_tensor("xnT4", [16, 128, 2048], BF16, kind="Internal").ap()
    xA_pad = nc.dram_tensor("xA", [S + 1, D], F32, kind=kd).ap()
    xA = xA_pad[0:S, :]
    xB = nc.dram_tensor("xB", [S, D], F32, kind=kd).ap()
    xtok_pad = nc.dram_tensor("xtokp", [S + 1, D], BF16, kind="Internal").ap()
    xg_d = nc.dram_tensor("xg_d", [7680, D], BF16, kind="Internal").ap()
    tid_d = nc.dram_tensor("tid_d", [8 * 1024 + 128, 1], I32,
                           kind=kd).ap()
    pm_tok = nc.dram_tensor("pm_tok", [S, 1], F32, kind=kd).ap()
    y_pad = nc.dram_tensor("y_pad", [S + 1, D], F32, kind="Internal").ap()
    acc = nc.dram_tensor("acc", [S, D], F32, kind="Internal").ap()
    sc = {
        "q4": nc.dram_tensor("q4", [S, 4096], BF16, kind=kd).ap(),
        "v4d": nc.dram_tensor("v4d", [S, 8192], BF16, kind="Internal").ap(),
        "of4": nc.dram_tensor("of4", [S, 4096], BF16, kind="Internal").ap(),
    }
    es = contextlib.ExitStack()
    with es:
        ctx = Ctx(nc, es)
        com = Common(ctx, caps)
        wbig = ctx.sbuf("sh_wbig", [128, 32768], BF16)
        wbig2 = ctx.sbuf("sh_wbig2", [128, 32768], BF16)
        hbuf = ctx.sbuf("sh_hbuf", [128, 16384], BF16)
        xnTs = ctx.sbuf("sh_xnT", [128, 4096], BF16)

        ln_mem = make_mem(ctx, {}, [(lambda n: n.startswith("ln_"),
                                     Carve(wbig[:, 0:16384]))])
        l0a_tab = {
            "a_wbig": wbig[:],
            "a_qh": wbig2[:, 0:8192], "a_vd": wbig2[:, 8192:24576],
            "a_ofT": wbig2[:],
            "a_ksum": hbuf[:, 0:4096].bitcast(F32),
            "a_xnT": hbuf[:, 6144:10240],
            "a_ev": hbuf[:, 10240:14336].bitcast(F32),
            "a_ob": hbuf[:, 14336:16384],
            "a_xres": hbuf[:, 0:4096].bitcast(F32),
        }
        l0a_mem = make_mem(ctx, l0a_tab, [])
        l0f_tab = {"f_wi": wbig[:], "f_wo": wbig2[:],
                   "f_hbuf": hbuf[:, 0:8192], "f_xnT": xnTs[:]}
        l0f_mem = make_mem(ctx, l0f_tab,
                           [(lambda n: n.startswith("f_") and n != "f_wob",
                             Carve(hbuf[:, 8192:16384]))])
        l1a_tab = {"t_xnT": hbuf[:, 0:8192]}
        carveA = Carve(wbig[:])
        carveB = Carve(wbig2[:])
        big4 = {"t_Ocf", "t_Ocs", "t_nu", "t_v"}
        l1a_mem = make_mem(ctx, l1a_tab,
                           [(lambda n: n in big4, carveA),
                            (lambda n: n.startswith("t_"), carveB)])
        l1f_tab = {"m_w1": wbig[:], "m_w2": wbig2[:],
                   "m_hbuf": hbuf[:, 0:8192],
                   "m_xnT": xnTs[:, 0:2048],
                   "m_xg": xnTs[:, 2048:4096],
                   "m_xrs": wbig[:, 0:4096].bitcast(F32),
                   "m_ypd": wbig[:, 4096:8192].bitcast(F32)}
        l1f_mem = make_mem(ctx, l1f_tab,
                           [(lambda n: n.startswith("m_") and n not in
                             ("m_b2", "m_stile", "m_idxc", "m_w1i",
                              "m_w2i", "m_b1i", "m_b2i"),
                             Carve(hbuf[:, 8192:16384]))])

        xn16b = nc.dram_tensor("xnT4b", [16, 128, 2048], BF16,
                               kind="Internal").ap()
        lnfA = make_mem(ctx, {}, [(lambda n: n.startswith("lnf_A"),
                                   Carve(hbuf[:, 4096:13000]))])
        lnfT = make_mem(ctx, {}, [(lambda n: n.startswith("lnf_T"),
                                   Carve(hbuf[:, 0:8192]))])
        emit_ln(ctx, com, x, xn16, "ln0", mem=ln_mem)
        barrier(ctx)
        emit_l0_attn(ctx, com, x, xn16, w, sc, xA, mem=l0a_mem)
        barrier(ctx)
        emit_ln(ctx, com, xA, xn16b, "ln1")
        barrier(ctx)
        emit_l0_ffn(ctx, com, xA, xn16b, w, xB, mem=l0f_mem)
        barrier(ctx)
        emit_ln(ctx, com, xB, xn16, "ln2")
        barrier(ctx)
        emit_l1_attn(ctx, com, xB, xn16, w, xA, mem=l1a_mem)
        barrier(ctx)
        emit_ln(ctx, com, xA, xn16b, "ln3b", tok_ap=xtok_pad)
        barrier(ctx)
        emit_l1_ffn_g(ctx, com, xA_pad, xn16b, xtok_pad, w, tid_d, pm_tok,
                      y_pad, xg_d, y, mem=l1f_mem)
        ctx.op("sync", lambda e: e.nop(),
               waits=[Tok(ctx.sems["st"], ctx.sems["st"].n),
                      Tok(ctx.sems["acc"], ctx.sems["acc"].n)])
        block = es.enter_context(nc.Block())
        ctx.finalize(block)
    return nc


_BUILDERS = {"l0a": build_l0a_nc, "l0f": build_l0f_nc,
             "l1a": build_l1a_nc, "l1f": build_l1f_nc}


def kernel(**inputs):
    g = prep_weights(inputs)
    consts = {"c_" + k: v for k, v in host_consts().items()}
    x = np.ascontiguousarray(np.asarray(inputs["L0_attn_x"], np.float32))
    B = x.shape[0]
    nc = build_fused_nc()
    allw = BLOCK_W["l0a"] + BLOCK_W["l0f"] + BLOCK_W["l1a"] + BLOCK_W["l1f"]
    in_maps = []
    for b in range(B):
        im = {"x": np.ascontiguousarray(x[b])}
        im.update(consts)
        for n in allw:
            im["w_" + n] = g[n]
        in_maps.append(im)
    res = run_bass_kernel_spmd(nc, in_maps, core_ids=list(range(B)))
    return np.stack([np.asarray(r["y"]) for r in res.results], axis=0)


def kernel_unfused(**inputs):
    g = prep_weights(inputs)
    consts = {"c_" + k: v for k, v in host_consts().items()}
    x = np.ascontiguousarray(np.asarray(inputs["L0_attn_x"], np.float32))
    B = x.shape[0]
    cur = [np.ascontiguousarray(x[b]) for b in range(B)]
    for blk in ("l0a", "l0f", "l1a", "l1f"):
        nc = _BUILDERS[blk]()
        in_maps = []
        for b in range(B):
            im = {"x": cur[b]}
            im.update(consts)
            for n in BLOCK_W[blk]:
                im["w_" + n] = g[n]
            in_maps.append(im)
        res = run_bass_kernel_spmd(nc, in_maps, core_ids=list(range(B)))
        cur = [np.asarray(r["y"]) for r in res.results]
    return np.stack(cur, axis=0)
